# version 01
# baseline (speedup 1.0000x reference)
import numpy as np
import ml_dtypes
from contextlib import ExitStack

import concourse.bass as bass
import concourse.mybir as mybir
from concourse.bass_utils import run_bass_kernel_spmd

F32 = mybir.dt.float32
F32R = mybir.dt.float32r
BF16 = mybir.dt.bfloat16
U8 = mybir.dt.uint8
I32 = mybir.dt.int32
I8 = mybir.dt.int8
AF = mybir.ActivationFunctionType
ALU = mybir.AluOpType

D = 2048
NCH = D // 128
PI = np.r_[0:D:2, 1:D:2]
PI_INV = np.argsort(PI)
MLP1 = 16384
PIM = np.r_[0:MLP1:2, 1:MLP1:2]
BF = ml_dtypes.bfloat16

LN_EPS = 1e-5



def img_kn(WT):
    Kk, Nn = WT.shape
    assert Kk % 128 == 0 and Nn % 128 == 0
    return np.ascontiguousarray(
        WT.reshape(Kk // 128, 128, Nn // 128, 128).transpose(2, 1, 0, 3)
        .reshape(Nn // 128, 128, (Kk // 128) * 128))


def img_stream(XT):
    Kk, T = XT.shape
    assert Kk % 128 == 0
    return np.ascontiguousarray(
        XT.reshape(Kk // 128, 128, T).transpose(1, 0, 2).reshape(128, (Kk // 128) * T))


def unimg_stream(img, Kk):
    P, CT = img.shape
    T = CT // (Kk // 128)
    return img.reshape(128, Kk // 128, T).transpose(1, 0, 2).reshape(Kk, T)


def col_tile(v):
    n = v.shape[0]
    assert n % 128 == 0
    return np.ascontiguousarray(v.reshape(n // 128, 128).T)



class Sem:
    def __init__(self, handle):
        self.h = handle
        self.n = 0

    def inc(self, inst, by=1):
        inst.then_inc(self.h, by)
        self.n += by
        return self.n


class K:
    def __init__(self, nc, ctx):
        self.nc = nc
        self.ctx = ctx
        self._id = 0

    def sem(self):
        self._id += 1
        return Sem(self.ctx.enter_context(self.nc.semaphore(f"s{self._id}")))

    def sbuf(self, name, shape, dtype):
        return self.ctx.enter_context(self.nc.sbuf_tensor(name, shape, dtype))

    def psum(self, name):
        return self.ctx.enter_context(self.nc.psum_tensor(name, [128, 512], F32))



HID0 = 5632
NH0 = HID0 // 128


def build_l0ffn(T=1024):
    nc = bass.Bass("TRN2", target_bir_lowering=False)
    NT = T // 512

    xin = nc.dram_tensor("xin", [128, NCH * T], F32, kind="ExternalInput").ap()
    wg = nc.dram_tensor("wg", [NH0, 128, NCH * 128], U8, kind="ExternalInput").ap()
    wu = nc.dram_tensor("wu", [NH0, 128, NCH * 128], U8, kind="ExternalInput").ap()
    wd = nc.dram_tensor("wd", [NH0, 128, NCH * 128], U8, kind="ExternalInput").ap()
    sg = nc.dram_tensor("sg", [128, NH0], F32, kind="ExternalInput").ap()
    su = nc.dram_tensor("su", [128, NH0], F32, kind="ExternalInput").ap()
    sd = nc.dram_tensor("sd", [128, NCH], F32, kind="ExternalInput").ap()
    xout = nc.dram_tensor("xout", [128, NCH * T], F32, kind="ExternalOutput").ap()

    with ExitStack() as ctx:
        k = K(nc, ctx)
        xs = k.sbuf("xs", [128, NCH * T], F32)
        xh = k.sbuf("xh", [128, NCH * 512], BF16)
        inter = k.sbuf("inter", [128, NH0 * 512], BF16)
        sq = inter
        gsc = [k.sbuf(f"gsc{i}", [128, 512], BF16) for i in range(2)]
        usc = [k.sbuf(f"usc{i}", [128, 512], BF16) for i in range(2)]
        t1g = [k.sbuf(f"t1g{i}", [128, 512], F32) for i in range(2)]
        t1u = [k.sbuf(f"t1u{i}", [128, 512], F32) for i in range(2)]
        dscbig = k.sbuf("dscbig", [128, 8 * 512], F32)
        rows = k.sbuf("rows", [1, 512 * 4], F32)
        sxb = k.sbuf("sxb", [128, 512], F32)
        sib = k.sbuf("sib", [128, 512], F32)
        sgt = k.sbuf("sgt", [128, NH0], F32)
        sut = k.sbuf("sut", [128, NH0], F32)
        sdt = k.sbuf("sdt", [128, NCH], F32)
        ones_col_f = k.sbuf("ones_col_f", [128, 1], F32)
        ones_col_b = k.sbuf("ones_col_b", [128, 1], BF16)
        ones_row_f = k.sbuf("ones_row_f", [1, 128], F32)
        cpool = k.sbuf("cpool", [128, 2], F32)
        nc.const_aps.register(F32, LN_EPS, cpool[:, 0:1])
        wgt = [k.sbuf(f"wgt{i}", [128, NCH * 128], BF16) for i in range(2)]
        wut = [k.sbuf(f"wut{i}", [128, NCH * 128], BF16) for i in range(2)]
        wdt = [k.sbuf(f"wdt{i}", [128, NCH * 128], BF16) for i in range(2)]

        pb = [k.psum(f"pb{i}") for i in range(8)]

        dse = k.sem()
        wse = [k.sem(), k.sem()]
        sse = k.sem()
        mse = k.sem()
        ese = k.sem()
        ase = k.sem()
        tse = k.sem()
        xse = k.sem()

        def row(i):
            return rows[0:1, i * 512:(i + 1) * 512]

        with nc.Block() as b:
            @b.sync
            def _(sync):
                for c in range(NCH):
                    sync.dma_start(out=xs[:, c * T:(c + 1) * T],
                                   in_=xin[:, c * T:(c + 1) * T]).then_inc(dse.h, 16)
                sync.dma_start(out=sgt[:], in_=sg).then_inc(dse.h, 16)
                sync.dma_start(out=sut[:], in_=su).then_inc(dse.h, 16)
                sync.dma_start(out=sdt[:], in_=sd).then_inc(dse.h, 16)
                dse.n += 16 * (NCH + 3)
                sync.wait_ge(dse.h, dse.n)

            @b.vector
            def _(v):
                v.memset(cpool[:, 0:1], LN_EPS)
                v.memset(ones_col_f[:], 1.0)
                v.memset(ones_col_b[:], 1.0)
                v.memset(ones_row_f[:], 1.0)

        for tt in range(NT):
            t0 = tt * 512
            ROW_MU, ROW_RR, ROW_MS, ROW_M2 = 0, 1, 2, 3
            ROW_VAR, ROW_LNV, ROW_SX, ROW_SI = 2, 3, 2, 2

            with nc.Block() as b:
                s_base = sse.n

                @b.scalar
                def _(s):
                    for c in range(NCH):
                        inst = s.activation(sq[:, c * 512:(c + 1) * 512],
                                            xs[:, c * T + t0: c * T + t0 + 512],
                                            AF.Square)
                        sse.inc(inst)

                @b.tensor
                def _(t):
                    for c in range(NCH):
                        inst = t.matmul(pb[0][0:1, :], ones_col_f[:],
                                        xs[:, c * T + t0: c * T + t0 + 512],
                                        start=(c == 0), stop=(c == NCH - 1))
                    for c in range(NCH):
                        t.wait_ge(sse.h, s_base + c + 1)
                        inst = t.matmul(pb[1][0:1, :], ones_col_b[:],
                                        sq[:, c * 512:(c + 1) * 512],
                                        start=(c == 0), stop=(c == NCH - 1))

            with nc.Block() as b:
                @b.vector
                def _(v):
                    inv_n = 1.0 / D
                    v.tensor_scalar(row(ROW_MU), pb[0][0:1, :], inv_n, None, ALU.mult)
                    v.tensor_scalar(row(ROW_MS), pb[1][0:1, :], inv_n, None, ALU.mult)
                    v.drain()
                    v.tensor_tensor(row(ROW_M2), row(ROW_MU), row(ROW_MU), ALU.mult)
                    v.drain()
                    inst = v.tensor_tensor(row(ROW_VAR), row(ROW_MS), row(ROW_M2), ALU.subtract)
                    xse.inc(inst)

                @b.scalar
                def _(s):
                    s.wait_ge(xse.h, xse.n)
                    s.activation(row(ROW_LNV), row(ROW_VAR), AF.Ln, bias=LN_EPS)
                    s.drain()
                    s.activation(row(ROW_RR), row(ROW_LNV), AF.Exp, scale=-0.5)

            with nc.Block() as b:
                @b.tensor
                def _(t):
                    t.matmul(pb[2][:], ones_row_f[:], row(ROW_MU), start=True, stop=True)
                    t.matmul(pb[3][:], ones_row_f[:], row(ROW_RR), start=True, stop=True)

            with nc.Block() as b:
                @b.vector
                def _(v):
                    for c in range(NCH):
                        v.tensor_tensor(xh[:, c * 512:(c + 1) * 512],
                                        xs[:, c * T + t0: c * T + t0 + 512],
                                        pb[2][:], ALU.subtract)
                    v.drain()
                    for c in range(NCH):
                        v.tensor_tensor(xh[:, c * 512:(c + 1) * 512],
                                        xh[:, c * 512:(c + 1) * 512],
                                        pb[3][:], ALU.mult)

            with nc.Block() as b:
                @b.tensor
                def _(t):
                    for c in range(NCH):
                        t.matmul(pb[4][0:1, :], ones_col_b[:],
                                 xh[:, c * 512:(c + 1) * 512],
                                 start=(c == 0), stop=(c == NCH - 1))
            with nc.Block() as b:
                @b.vector
                def _(v):
                    v.tensor_copy(row(ROW_SX), pb[4][0:1, :])
            with nc.Block() as b:
                @b.tensor
                def _(t):
                    t.matmul(pb[5][:], ones_row_f[:], row(ROW_SX), start=True, stop=True)
            with nc.Block() as b:
                @b.vector
                def _(v):
                    v.tensor_copy(sxb[:], pb[5][:])

            with nc.Block() as b:
                w_base = [wse[0].n, wse[1].n]
                m_base, e_base, a_base, t_base = mse.n, ese.n, ase.n, tse.n

                @b.gpsimd
                def _(g):
                    for c in range(NH0):
                        bi = c % 2
                        if c >= 2:
                            g.wait_ge(mse.h, m_base + (c - 2) + 1)
                            g.wait_ge(wse[bi].h, w_base[bi] + 32 * (c // 2))
                        g.dma_start(out=wgt[bi][:], in_=wg[c]).then_inc(wse[bi].h, 16)
                        g.dma_start(out=wut[bi][:], in_=wu[c]).then_inc(wse[bi].h, 16)
                        wse[bi].n += 32

                @b.tensor
                def _(t):
                    for c in range(NH0):
                        bi = c % 2
                        t.wait_ge(wse[bi].h, w_base[bi] + 32 * (c // 2 + 1))
                        if c >= 2:
                            t.wait_ge(ese.h, e_base + (c - 2) + 1)
                        gb, ub = pb[bi], pb[2 + bi]
                        for kc in range(NCH):
                            t.matmul(gb[:], wgt[bi][:, kc * 128:(kc + 1) * 128],
                                     xh[:, kc * 512:(kc + 1) * 512],
                                     start=(kc == 0), stop=(kc == NCH - 1))
                        for kc in range(NCH):
                            inst = t.matmul(ub[:], wut[bi][:, kc * 128:(kc + 1) * 128],
                                            xh[:, kc * 512:(kc + 1) * 512],
                                            start=(kc == 0), stop=(kc == NCH - 1))
                        mse.inc(inst)

                @b.vector
                def _(v):
                    def mult_inter(c):
                        bi = c % 2
                        v.wait_ge(ase.h, a_base + c + 1)
                        inst = v.tensor_tensor(inter[:, c * 512:(c + 1) * 512],
                                               gsc[bi][:], usc[bi][:], ALU.mult)
                        tse.inc(inst)

                    for c in range(NH0):
                        bi = c % 2
                        gb, ub = pb[bi], pb[2 + bi]
                        v.wait_ge(mse.h, m_base + c + 1)
                        if c >= 2:
                            v.wait_ge(ase.h, a_base + (c - 2) + 1)
                        v.scalar_tensor_tensor(t1g[bi][:], sxb[:], -128.0, gb[:],
                                               ALU.mult, ALU.add)
                        inst = v.scalar_tensor_tensor(t1u[bi][:], sxb[:], -128.0, ub[:],
                                                      ALU.mult, ALU.add)
                        ese.inc(inst)
                        if c >= 1:
                            mult_inter(c - 1)
                    mult_inter(NH0 - 1)

                @b.scalar
                def _(s):
                    for c in range(NH0):
                        bi = c % 2
                        s.wait_ge(ese.h, e_base + c + 1)
                        if c >= 2:
                            s.wait_ge(tse.h, t_base + (c - 2) + 1)
                        s.activation(gsc[bi][:], t1g[bi][:], AF.Silu, scale=sgt[:, c:c + 1])
                        inst = s.activation(usc[bi][:], t1u[bi][:], AF.Copy,
                                            scale=sut[:, c:c + 1])
                        ase.inc(inst)

            with nc.Block() as b:
                @b.tensor
                def _(t):
                    for c in range(NH0):
                        t.matmul(pb[6][0:1, :], ones_col_b[:],
                                 inter[:, c * 512:(c + 1) * 512],
                                 start=(c == 0), stop=(c == NH0 - 1))
            with nc.Block() as b:
                @b.vector
                def _(v):
                    v.tensor_copy(row(ROW_SI), pb[6][0:1, :])
            with nc.Block() as b:
                @b.tensor
                def _(t):
                    t.matmul(pb[7][:], ones_row_f[:], row(ROW_SI), start=True, stop=True)
            with nc.Block() as b:
                @b.vector
                def _(v):
                    v.tensor_copy(sib[:], pb[7][:])

            for dh in range(2):
                with nc.Block() as b:
                    w_base = [wse[0].n, wse[1].n]
                    m_base = mse.n

                    @b.gpsimd
                    def _(g):
                        for mc in range(NH0):
                            bi = mc % 2
                            if mc >= 2:
                                g.wait_ge(mse.h, m_base + (mc - 2) + 1)
                                g.wait_ge(wse[bi].h, w_base[bi] + 16 * (mc // 2))
                            g.dma_start(out=wdt[bi][:], in_=wd[mc]).then_inc(wse[bi].h, 16)
                            wse[bi].n += 16

                    @b.tensor
                    def _(t):
                        for mc in range(NH0):
                            bi = mc % 2
                            t.wait_ge(wse[bi].h, w_base[bi] + 16 * (mc // 2 + 1))
                            for dc in range(8):
                                inst = t.matmul(
                                    pb[dc][:],
                                    wdt[bi][:, (dh * 8 + dc) * 128:(dh * 8 + dc + 1) * 128],
                                    inter[:, mc * 512:(mc + 1) * 512],
                                    start=(mc == 0), stop=(mc == NH0 - 1))
                            mse.inc(inst)

                with nc.Block() as b:
                    @b.vector
                    def _(v):
                        for dc in range(8):
                            v.scalar_tensor_tensor(dscbig[:, dc * 512:(dc + 1) * 512],
                                                   sib[:], -128.0, pb[dc][:],
                                                   ALU.mult, ALU.add)
                        v.drain()
                        for dc in range(8):
                            ci = dh * 8 + dc
                            v.scalar_tensor_tensor(
                                xs[:, ci * T + t0: ci * T + t0 + 512],
                                dscbig[:, dc * 512:(dc + 1) * 512], sdt[:, ci:ci + 1],
                                xs[:, ci * T + t0: ci * T + t0 + 512],
                                ALU.mult, ALU.add)

        with nc.Block() as b:
            @b.sync
            def _(sync):
                for c in range(NCH):
                    sync.dma_start(out=xout[:, c * T:(c + 1) * T],
                                   in_=xs[:, c * T:(c + 1) * T]).then_inc(dse.h, 16)
                dse.n += 16 * NCH
                sync.wait_ge(dse.h, dse.n)
    return nc



def bview(packed_i32):
    a = packed_i32.view(np.uint8).reshape(*packed_i32.shape, 4)[..., 0]
    return np.ascontiguousarray(a)


def prep_l0ffn_weights(inp):
    wgq = bview(np.asarray(inp["L0_ffn_wq_gate"]))
    wuq = bview(np.asarray(inp["L0_ffn_wq_up"]))
    wdq = bview(np.asarray(inp["L0_ffn_wq_down"]))
    wgT = np.ascontiguousarray(wgq.T[PI])
    wuT = np.ascontiguousarray(wuq.T[PI])
    wdT = np.ascontiguousarray(wdq.T)
    wdT = np.ascontiguousarray(wdT[:, PI])
    return {
        "wg": img_kn(wgT).astype(np.uint8),
        "wu": img_kn(wuT).astype(np.uint8),
        "wd": np.ascontiguousarray(wdT.reshape(NH0, 128, NCH * 128)),
        "sg": col_tile(np.asarray(inp["L0_ffn_s_gate"], np.float32)),
        "su": col_tile(np.asarray(inp["L0_ffn_s_up"], np.float32)),
        "sd": col_tile(np.asarray(inp["L0_ffn_s_down"], np.float32)[PI]),
    }


def run_l0ffn(x_pi_T_shards, wmaps, T=1024):
    nc = build_l0ffn(T)
    in_maps = []
    for s in x_pi_T_shards:
        m = {"xin": img_stream(s)}
        m.update(wmaps)
        in_maps.append(m)
    res = run_bass_kernel_spmd(nc, in_maps, core_ids=list(range(len(in_maps))))
    return [unimg_stream(np.asarray(r["xout"]), D) for r in res.results]



NDEC = 16


def build_dec1():
    nc = bass.Bass("TRN2", target_bir_lowering=False)
    pg = nc.dram_tensor("pg", [NDEC, 128, 1024], U8, kind="ExternalInput").ap()
    pu = nc.dram_tensor("pu", [NDEC, 128, 1024], U8, kind="ExternalInput").ap()
    pd = nc.dram_tensor("pd", [NDEC // 2, 128, 2048], U8, kind="ExternalInput").ap()
    scg = nc.dram_tensor("scg", [NDEC, 8, 1024], F32, kind="ExternalInput").ap()
    scu = nc.dram_tensor("scu", [NDEC, 8, 1024], F32, kind="ExternalInput").ap()
    scd = nc.dram_tensor("scd", [NDEC // 2, 8, 2048], F32, kind="ExternalInput").ap()
    sel4 = nc.dram_tensor("sel4", [4, 128], F32, kind="ExternalInput").ap()
    sel8 = nc.dram_tensor("sel8", [8, 128], F32, kind="ExternalInput").ap()
    dg = nc.dram_tensor("dg", [NDEC, 128, 2048], BF16, kind="ExternalOutput").ap()
    du = nc.dram_tensor("du", [NDEC, 128, 2048], BF16, kind="ExternalOutput").ap()
    dd = nc.dram_tensor("dd", [NDEC, 128, 2048], BF16, kind="ExternalOutput").ap()

    with ExitStack() as ctx:
        k = K(nc, ctx)
        w32 = [k.sbuf(f"w32_{i}", [128, 2048], I32) for i in range(2)]
        jl = k.sbuf("jl", [128, 2048], I32)
        jh = k.sbuf("jh", [128, 2048], I32)
        sl = k.sbuf("sl", [128, 2048], I32)
        sh = k.sbuf("sh", [128, 2048], I32)
        p1 = [k.sbuf(f"p1_{i}", [128, 2048], F32) for i in range(2)]
        m2 = [k.sbuf(f"m2_{i}", [128, 2048], F32) for i in range(2)]
        m4 = [k.sbuf(f"m4_{i}", [128, 2048], F32) for i in range(2)]
        sgn = [k.sbuf(f"sgn_{i}", [128, 2048], F32) for i in range(2)]
        qq = [k.sbuf(f"qq_{i}", [128, 2048], F32) for i in range(2)]
        scr = k.sbuf("scr", [8, 2048], F32)
        outb = [k.sbuf(f"outb{i}", [128, 2048], BF16) for i in range(2)]
        sel4t = k.sbuf("sel4t", [4, 128], F32)
        sel8t = k.sbuf("sel8t", [8, 128], F32)
        cpool = k.sbuf("cpool", [128, 2], F32)
        nc.const_aps.register(F32, -4.0, cpool[:, 0:1])
        nc.const_aps.register(F32, -2.0, cpool[:, 1:2])
        pb = [k.psum(f"pb{i}") for i in range(4)]

        dse = k.sem()
        gse = k.sem()
        sse = k.sem()
        ose = k.sem()
        mse = k.sem()
        ise = k.sem()
        ase_ = k.sem()
        vse = k.sem()

        with nc.Block() as b:
            @b.sync
            def _(sync):
                sync.dma_start(out=sel4t[:], in_=sel4).then_inc(dse.h, 16)
                sync.dma_start(out=sel8t[:], in_=sel8).then_inc(dse.h, 16)
                dse.n += 32
                sync.wait_ge(dse.h, dse.n)

            @b.vector
            def _(v):
                v.memset(cpool[:, 0:1], -4.0)
                v.memset(cpool[:, 1:2], -2.0)

        def decode_block(src_ap, W, nsel, selt, sc_ap, dsts):
            with nc.Block() as b:
                NW = W // 512

                @b.gpsimd
                def _(g):
                    g.dma_start(out=w32[0][:, 0:W], in_=src_ap).then_inc(gse.h, 16)
                    gse.n += 16
                    g.wait_ge(gse.h, gse.n)

                @b.sync
                def _(sync):
                    sync.dma_start(out=scr[0:nsel, 0:W], in_=sc_ap).then_inc(sse.h, 16)
                    sse.n += 16
                    sync.wait_ge(sse.h, sse.n)

            with nc.Block() as b:
                @b.tensor
                def _(t):
                    for q in range(NW):
                        inst = t.matmul(pb[q][:, 0:512], selt,
                                        scr[0:nsel, q * 512:(q + 1) * 512],
                                        start=True, stop=True)
                    mse.inc(inst)

                @b.vector
                def _(v):
                    w = w32[0][:, 0:W]
                    v.tensor_scalar(jl[:, 0:W], w, 7, None, ALU.bitwise_and)
                    v.tensor_scalar(jh[:, 0:W], w, 4, 7, ALU.logical_shift_right,
                                    ALU.bitwise_and)
                    v.tensor_scalar(sl[:, 0:W], w, 3, 1, ALU.logical_shift_right,
                                    ALU.bitwise_and)
                    v.tensor_scalar(sh[:, 0:W], w, 7, 1, ALU.logical_shift_right,
                                    ALU.bitwise_and)
                    v.drain()
                    v.tensor_scalar(p1[0][:, 0:W], jl[:, 0:W], 0.5, None, ALU.mult)
                    v.tensor_scalar(m2[0][:, 0:W], jl[:, 0:W], 1.0, -2.0, ALU.mult, ALU.add)
                    v.tensor_scalar(m4[0][:, 0:W], jl[:, 0:W], 2.0, -8.0, ALU.mult, ALU.add)
                    v.tensor_scalar(p1[1][:, 0:W], jh[:, 0:W], 0.5, None, ALU.mult)
                    v.tensor_scalar(m2[1][:, 0:W], jh[:, 0:W], 1.0, -2.0, ALU.mult, ALU.add)
                    v.tensor_scalar(m4[1][:, 0:W], jh[:, 0:W], 2.0, -8.0, ALU.mult, ALU.add)
                    v.tensor_scalar(sgn[0][:, 0:W], sl[:, 0:W], -2.0, 1.0, ALU.mult, ALU.add)
                    v.tensor_scalar(sgn[1][:, 0:W], sh[:, 0:W], -2.0, 1.0, ALU.mult, ALU.add)
                    v.drain()
                    v.tensor_tensor(p1[0][:, 0:W], p1[0][:, 0:W], m2[0][:, 0:W], ALU.max)
                    v.tensor_tensor(p1[1][:, 0:W], p1[1][:, 0:W], m2[1][:, 0:W], ALU.max)
                    v.drain()
                    v.tensor_tensor(p1[0][:, 0:W], p1[0][:, 0:W], m4[0][:, 0:W], ALU.max)
                    v.tensor_tensor(p1[1][:, 0:W], p1[1][:, 0:W], m4[1][:, 0:W], ALU.max)
                    v.wait_ge(mse.h, mse.n)
                    for lh in range(2):
                        for q in range(NW):
                            v.tensor_tensor(qq[lh][:, q * 512:(q + 1) * 512],
                                            sgn[lh][:, q * 512:(q + 1) * 512],
                                            pb[q][:, 0:512], ALU.mult)
                    v.drain()
                    if len(dsts) == 1:
                        v.tensor_tensor(outb[0][:, 0:W], p1[0][:, 0:W],
                                        qq[0][:, 0:W], ALU.mult)
                        inst = v.tensor_tensor(outb[0][:, W:2 * W], p1[1][:, 0:W],
                                               qq[1][:, 0:W], ALU.mult)
                    else:
                        v.tensor_tensor(outb[0][:, 0:W], p1[0][:, 0:W],
                                        qq[0][:, 0:W], ALU.mult)
                        inst = v.tensor_tensor(outb[1][:, 0:W], p1[1][:, 0:W],
                                               qq[1][:, 0:W], ALU.mult)
                    vse.inc(inst)

            with nc.Block() as b:
                @b.sync
                def _(sync):
                    if len(dsts) == 1:
                        sync.dma_start(out=dsts[0], in_=outb[0][:]).then_inc(ose.h, 16)
                        ose.n += 16
                    else:
                        sync.dma_start(out=dsts[0], in_=outb[0][:, 0:W]).then_inc(ose.h, 16)
                        sync.dma_start(out=dsts[1], in_=outb[1][:, 0:W]).then_inc(ose.h, 16)
                        ose.n += 32
                    sync.wait_ge(ose.h, ose.n)

        for i in range(NDEC):
            decode_block(pg[i], 1024, 8, sel8t[:], scg[i], [dg[i]])
        for i in range(NDEC):
            decode_block(pu[i], 1024, 8, sel8t[:], scu[i], [du[i]])
        for i in range(NDEC // 2):
            decode_block(pd[i], 2048, 8, sel8t[:], scd[i], [dd[i], dd[8 + i]])
    return nc


def prep_dec1_host(inp):
    gP = bview(np.asarray(inp["L1_ffn_gate_W_packed"]))
    uP = bview(np.asarray(inp["L1_ffn_up_W_packed"]))
    dP = bview(np.asarray(inp["L1_ffn_down_W_packed"]))
    gS = np.asarray(inp["L1_ffn_gate_scales"], np.float32)
    uS = np.asarray(inp["L1_ffn_up_scales"], np.float32)
    dS = np.asarray(inp["L1_ffn_down_scales"], np.float32)

    gPT = np.ascontiguousarray(gP.T[:, PIM])
    uPT = np.ascontiguousarray(uP.T[:, PIM])
    dPT = np.ascontiguousarray(dP.T[:, PI])
    gST = np.ascontiguousarray(gS.T[:, PIM])
    uST = np.ascontiguousarray(uS.T[:, PIM])
    dST = np.ascontiguousarray(dS.T[:, PI])

    gPB = img_kn(gPT)
    uPB = img_kn(uPT)
    dPB = np.ascontiguousarray(dPT.reshape(64, 128, D))

    def g_st(ST):
        out = np.empty((128, 8, 8, 128), np.float32)
        for c in range(128):
            for a in range(8):
                out[c, :, a, :] = ST[8 * a:8 * a + 8, 128 * c:128 * c + 128]
        return out.reshape(128, 8, 1024)
    gSS = g_st(gST)
    uSS = g_st(uST)
    dSS = np.ascontiguousarray(dST.reshape(64, 8, D))

    sel4 = np.zeros((4, 128), np.float32)
    for p in range(128):
        sel4[p // 32, p] = 1.0
    sel8 = np.zeros((8, 128), np.float32)
    for p in range(128):
        sel8[p // 16, p] = 1.0
    return dict(gPB=gPB, uPB=uPB, dPB=dPB, gSS=gSS, uSS=uSS, dSS=dSS,
                sel4=sel4, sel8=sel8)


def dec1_in_maps(hp):
    maps = []
    for j in range(8):
        maps.append({
            "pg": hp["gPB"][16 * j:16 * j + 16],
            "pu": hp["uPB"][16 * j:16 * j + 16],
            "pd": hp["dPB"][8 * j:8 * j + 8],
            "scg": hp["gSS"][16 * j:16 * j + 16],
            "scu": hp["uSS"][16 * j:16 * j + 16],
            "scd": hp["dSS"][8 * j:8 * j + 8],
            "sel4": hp["sel4"], "sel8": hp["sel8"],
        })
    return maps


def assemble_dec1(results):
    dg = np.concatenate([np.asarray(r["dg"]) for r in results], axis=0)
    du = np.concatenate([np.asarray(r["du"]) for r in results], axis=0)
    dd_lo = np.concatenate([np.asarray(r["dd"])[0:8] for r in results], axis=0)
    dd_hi = np.concatenate([np.asarray(r["dd"])[8:16] for r in results], axis=0)
    dd = np.concatenate([dd_lo, dd_hi], axis=0)
    return dg, du, dd



NH1 = 128


def build_l1ffn(T=1024):
    nc = bass.Bass("TRN2", target_bir_lowering=False)
    NT = T // 512

    xin = nc.dram_tensor("xin", [128, NCH * T], F32, kind="ExternalInput").ap()
    dg = nc.dram_tensor("dg", [NH1, 128, NCH * 128], BF16, kind="ExternalInput").ap()
    du = nc.dram_tensor("du", [NH1, 128, NCH * 128], BF16, kind="ExternalInput").ap()
    dd = nc.dram_tensor("dd", [NH1, 128, NCH * 128], BF16, kind="ExternalInput").ap()
    xout = nc.dram_tensor("xout", [128, NCH * T], F32, kind="ExternalOutput").ap()

    with ExitStack() as ctx:
        k = K(nc, ctx)
        xh = k.sbuf("xh", [128, NCH * 512], BF16)
        inter = k.sbuf("inter", [128, NH1 * 512], BF16)
        sq = inter
        xtile = [k.sbuf(f"xtile{i}", [128, 512], F32) for i in range(4)]
        g2 = [k.sbuf(f"g2_{i}", [128, 512], F32) for i in range(2)]
        uu = [k.sbuf(f"uu_{i}", [128, 512], BF16) for i in range(2)]
        gl = [k.sbuf(f"gl_{i}", [128, 512], BF16) for i in range(2)]
        xres = [k.sbuf(f"xres{i}", [128, 512], F32) for i in range(4)]
        oref = [k.sbuf(f"oref{i}", [128, 512], F32) for i in range(4)]
        rows = k.sbuf("rows", [1, 512 * 4], F32)
        ones_col_f = k.sbuf("ones_col_f", [128, 1], F32)
        ones_col_b = k.sbuf("ones_col_b", [128, 1], BF16)
        ones_row_f = k.sbuf("ones_row_f", [1, 128], F32)
        cpool = k.sbuf("cpool", [128, 2], F32)
        nc.const_aps.register(F32, LN_EPS, cpool[:, 0:1])
        wgt = [k.sbuf(f"wgt{i}", [128, NCH * 128], BF16) for i in range(2)]
        wut = [k.sbuf(f"wut{i}", [128, NCH * 128], BF16) for i in range(2)]
        wdt = [k.sbuf(f"wdt{i}", [128, 8 * 128], BF16) for i in range(2)]

        pb = [k.psum(f"pb{i}") for i in range(8)]

        dse = k.sem()
        dse4 = [k.sem() for _ in range(4)]
        ose4 = [k.sem() for _ in range(4)]
        wse = [k.sem(), k.sem()]
        wse2 = [k.sem(), k.sem()]
        sse = k.sem()
        pse = k.sem()
        mse = k.sem()
        a1se = k.sem()
        v1se = k.sem()
        a2se = k.sem()
        v2se = k.sem()
        vse = k.sem()
        ose = k.sem()
        xse = k.sem()

        def row(i):
            return rows[0:1, i * 512:(i + 1) * 512]

        with nc.Block() as b:
            @b.vector
            def _(v):
                v.memset(cpool[:, 0:1], LN_EPS)
                v.memset(ones_col_f[:], 1.0)
                v.memset(ones_col_b[:], 1.0)
                v.memset(ones_row_f[:], 1.0)

        for tt in range(NT):
            t0 = tt * 512
            ROW_MU, ROW_RR, ROW_MS, ROW_M2 = 0, 1, 2, 3
            ROW_VAR, ROW_LNV = 2, 3

            with nc.Block() as b:
                d_base = [x.n for x in dse4]
                s_base, p_base = sse.n, pse.n

                @b.sync
                def _(sync):
                    for c in range(NCH):
                        bi = c % 4
                        if c >= 4:
                            sync.wait_ge(sse.h, s_base + (c - 4) + 1)
                            sync.wait_ge(pse.h, p_base + (c - 4) + 1)
                            sync.wait_ge(dse4[bi].h, d_base[bi] + 16 * (c // 4))
                        sync.dma_start(out=xtile[bi][:],
                                       in_=xin[:, c * T + t0: c * T + t0 + 512]
                                       ).then_inc(dse4[bi].h, 16)
                        dse4[bi].n += 16

                @b.scalar
                def _(s):
                    for c in range(NCH):
                        bi = c % 4
                        s.wait_ge(dse4[bi].h, d_base[bi] + 16 * (c // 4 + 1))
                        inst = s.activation(sq[:, c * 512:(c + 1) * 512],
                                            xtile[bi][:], AF.Square)
                        sse.inc(inst)

                @b.tensor
                def _(t):
                    s_base2 = s_base
                    for c in range(NCH):
                        bi = c % 4
                        t.wait_ge(dse4[bi].h, d_base[bi] + 16 * (c // 4 + 1))
                        inst = t.matmul(pb[0][0:1, :], ones_col_f[:], xtile[bi][:],
                                        start=(c == 0), stop=(c == NCH - 1))
                        pse.inc(inst)
                    for c in range(NCH):
                        t.wait_ge(sse.h, s_base2 + c + 1)
                        t.matmul(pb[1][0:1, :], ones_col_b[:],
                                 sq[:, c * 512:(c + 1) * 512],
                                 start=(c == 0), stop=(c == NCH - 1))

            with nc.Block() as b:
                @b.vector
                def _(v):
                    inv_n = 1.0 / D
                    v.tensor_scalar(row(ROW_MU), pb[0][0:1, :], inv_n, None, ALU.mult)
                    v.tensor_scalar(row(ROW_MS), pb[1][0:1, :], inv_n, None, ALU.mult)
                    v.drain()
                    v.tensor_tensor(row(ROW_M2), row(ROW_MU), row(ROW_MU), ALU.mult)
                    v.drain()
                    inst = v.tensor_tensor(row(ROW_VAR), row(ROW_MS), row(ROW_M2),
                                           ALU.subtract)
                    xse.inc(inst)

                @b.scalar
                def _(s):
                    s.wait_ge(xse.h, xse.n)
                    s.activation(row(ROW_LNV), row(ROW_VAR), AF.Ln, bias=LN_EPS)
                    s.drain()
                    s.activation(row(ROW_RR), row(ROW_LNV), AF.Exp, scale=-0.5)

            with nc.Block() as b:
                @b.tensor
                def _(t):
                    t.matmul(pb[2][:], ones_row_f[:], row(ROW_MU), start=True, stop=True)
                    t.matmul(pb[3][:], ones_row_f[:], row(ROW_RR), start=True, stop=True)

            with nc.Block() as b:
                d_base = [x.n for x in dse4]
                v_base = vse.n

                @b.sync
                def _(sync):
                    for c in range(NCH):
                        bi = c % 4
                        if c >= 4:
                            sync.wait_ge(vse.h, v_base + (c - 4) + 1)
                            sync.wait_ge(dse4[bi].h, d_base[bi] + 16 * (c // 4))
                        sync.dma_start(out=xtile[bi][:],
                                       in_=xin[:, c * T + t0: c * T + t0 + 512]
                                       ).then_inc(dse4[bi].h, 16)
                        dse4[bi].n += 16

                @b.vector
                def _(v):
                    for c in range(NCH):
                        bi = c % 4
                        v.wait_ge(dse4[bi].h, d_base[bi] + 16 * (c // 4 + 1))
                        inst = v.tensor_tensor(xh[:, c * 512:(c + 1) * 512],
                                               xtile[bi][:], pb[2][:], ALU.subtract)
                        vse.inc(inst)
                    v.drain()
                    for c in range(NCH):
                        v.tensor_tensor(xh[:, c * 512:(c + 1) * 512],
                                        xh[:, c * 512:(c + 1) * 512],
                                        pb[3][:], ALU.mult)

            with nc.Block() as b:
                w_base = [wse[0].n, wse[1].n]
                m_base, a1_base, v1_base, a2_base, v2_base = (
                    mse.n, a1se.n, v1se.n, a2se.n, v2se.n)

                @b.gpsimd
                def _(g):
                    for c in range(NH1):
                        bi = c % 2
                        if c >= 2:
                            g.wait_ge(mse.h, m_base + (c - 2) + 1)
                            g.wait_ge(wse[bi].h, w_base[bi] + 32 * (c // 2))
                        g.dma_start(out=wgt[bi][:], in_=dg[c]).then_inc(wse[bi].h, 16)
                        g.dma_start(out=wut[bi][:], in_=du[c]).then_inc(wse[bi].h, 16)
                        wse[bi].n += 32

                @b.tensor
                def _(t):
                    for c in range(NH1):
                        bi = c % 2
                        t.wait_ge(wse[bi].h, w_base[bi] + 32 * (c // 2 + 1))
                        if c >= 2:
                            t.wait_ge(v2se.h, v2_base + (c - 2) + 1)
                        gb, ub = pb[bi], pb[2 + bi]
                        for kc in range(NCH):
                            t.matmul(gb[:], wgt[bi][:, kc * 128:(kc + 1) * 128],
                                     xh[:, kc * 512:(kc + 1) * 512],
                                     start=(kc == 0), stop=(kc == NCH - 1))
                        for kc in range(NCH):
                            inst = t.matmul(ub[:], wut[bi][:, kc * 128:(kc + 1) * 128],
                                            xh[:, kc * 512:(kc + 1) * 512],
                                            start=(kc == 0), stop=(kc == NCH - 1))
                        mse.inc(inst)

                @b.scalar
                def _(s):
                    for c in range(NH1):
                        bi = c % 2
                        gb, ub = pb[bi], pb[2 + bi]
                        s.wait_ge(mse.h, m_base + c + 1)
                        if c >= 2:
                            s.wait_ge(v2se.h, v2_base + (c - 2) + 1)
                        inst = s.activation(g2[bi][:], gb[:], AF.Square)
                        a1se.inc(inst)
                        s.wait_ge(v1se.h, v1_base + c + 1)
                        s.drain()
                        s.activation(g2[bi][:], g2[bi][:], AF.Sigmoid)
                        inst = s.activation(uu[bi][:], ub[:], AF.Copy)
                        a2se.inc(inst)

                @b.vector
                def _(v):
                    for c in range(NH1):
                        bi = c % 2
                        gb = pb[bi]
                        v.wait_ge(a1se.h, a1_base + c + 1)
                        v.tensor_scalar(g2[bi][:], g2[bi][:], 0.07135775452626324,
                                        1.5957691216057308, ALU.mult, ALU.add)
                        v.drain()
                        inst = v.tensor_tensor(g2[bi][:], g2[bi][:], gb[:], ALU.mult)
                        v1se.inc(inst)
                        v.wait_ge(a2se.h, a2_base + c + 1)
                        v.tensor_tensor(gl[bi][:], g2[bi][:], gb[:], ALU.mult)
                        v.drain()
                        inst = v.tensor_tensor(inter[:, c * 512:(c + 1) * 512],
                                               gl[bi][:], uu[bi][:], ALU.mult)
                        v2se.inc(inst)

            for dh in range(2):
                with nc.Block() as b:
                    w2_base = [wse2[0].n, wse2[1].n]
                    m_base = mse.n

                    @b.gpsimd
                    def _(g):
                        for mc in range(NH1):
                            bi = mc % 2
                            if mc >= 2:
                                g.wait_ge(mse.h, m_base + (mc - 2) + 1)
                                g.wait_ge(wse2[bi].h, w2_base[bi] + 16 * (mc // 2))
                            g.dma_start(out=wdt[bi][:, 0:1024],
                                        in_=dd[mc][:, dh * 1024:(dh + 1) * 1024]
                                        ).then_inc(wse2[bi].h, 16)
                            wse2[bi].n += 16

                    @b.tensor
                    def _(t):
                        for mc in range(NH1):
                            bi = mc % 2
                            t.wait_ge(wse2[bi].h, w2_base[bi] + 16 * (mc // 2 + 1))
                            for dc in range(8):
                                inst = t.matmul(
                                    pb[dc][:],
                                    wdt[bi][:, dc * 128:(dc + 1) * 128],
                                    inter[:, mc * 512:(mc + 1) * 512],
                                    start=(mc == 0), stop=(mc == NH1 - 1))
                            mse.inc(inst)

                with nc.Block() as b:
                    d_base = [x.n for x in dse4]
                    o_base = [x.n for x in ose4]
                    v_base = vse.n

                    def store(j, sync):
                        bj = j % 4
                        cj = dh * 8 + j
                        sync.wait_ge(vse.h, v_base + j + 1)
                        if j >= 4:
                            sync.wait_ge(ose4[bj].h, o_base[bj] + 16 * (j // 4))
                        sync.dma_start(
                            out=xout[:, cj * T + t0: cj * T + t0 + 512],
                            in_=oref[bj][:]).then_inc(ose4[bj].h, 16)
                        ose4[bj].n += 16

                    @b.sync
                    def _(sync):
                        for j in range(8):
                            bi = j % 4
                            ci = dh * 8 + j
                            if j >= 4:
                                sync.wait_ge(vse.h, v_base + (j - 4) + 1)
                                sync.wait_ge(dse4[bi].h, d_base[bi] + 16 * (j // 4))
                            sync.dma_start(
                                out=xres[bi][:],
                                in_=xin[:, ci * T + t0: ci * T + t0 + 512]
                                ).then_inc(dse4[bi].h, 16)
                            dse4[bi].n += 16
                            if j >= 1:
                                store(j - 1, sync)
                        store(7, sync)
                        for x in ose4:
                            sync.wait_ge(x.h, x.n)

                    @b.vector
                    def _(v):
                        for j in range(8):
                            bi = j % 4
                            v.wait_ge(dse4[bi].h, d_base[bi] + 16 * (j // 4 + 1))
                            if j >= 4:
                                v.wait_ge(ose4[bi].h, o_base[bi] + 16 * (j // 4))
                            inst = v.tensor_tensor(
                                oref[bi][:], xres[bi][:], pb[j][:], ALU.add)
                            vse.inc(inst)
    return nc



NIN = 66
NXBC = 33
CHUNK = 128


def build_l0attn(T=2048, TILE=512):
    nc = bass.Bass("TRN2", target_bir_lowering=False)
    NT = T // TILE
    NCK = TILE // CHUNK

    xin = nc.dram_tensor("xin", [128, NCH * T], F32, kind="ExternalInput").ap()
    win = nc.dram_tensor("win", [NIN, 128, NCH * 128], F32, kind="ExternalInput").ap()
    wout = nc.dram_tensor("wout", [NCH, 128, 32 * 128], F32, kind="ExternalInput").ap()
    cwt = nc.dram_tensor("cwt", [128, NXBC * 4], F32, kind="ExternalInput").ap()
    cbt = nc.dram_tensor("cbt", [128, NXBC], F32, kind="ExternalInput").ap()
    dtb = nc.dram_tensor("dtb", [64, 1], F32, kind="ExternalInput").ap()
    alog = nc.dram_tensor("alog", [64, 1], F32, kind="ExternalInput").ap()
    drep = nc.dram_tensor("drep", [128, 32], F32, kind="ExternalInput").ap()
    nwt = nc.dram_tensor("nwt", [128, 32], F32, kind="ExternalInput").ap()
    sel32 = nc.dram_tensor("sel32", [64, 32 * 128], F32, kind="ExternalInput").ap()
    penl = nc.dram_tensor("penl", [128, 128], BF16, kind="ExternalInput").ap()
    penr = nc.dram_tensor("penr", [128, 4 * 128], BF16, kind="ExternalInput").ap()
    identb = nc.dram_tensor("identb", [128, 128], BF16, kind="ExternalInput").ap()
    identf = nc.dram_tensor("identf", [128, 128], F32, kind="ExternalInput").ap()
    zspill = nc.dram_tensor("zspill", [128, 32 * TILE], BF16).ap()
    lbounce = nc.dram_tensor("lbounce", [9, 64], F32).ap()
    xout = nc.dram_tensor("xout", [128, NCH * T], F32, kind="ExternalOutput").ap()

    with ExitStack() as ctx:
        k = K(nc, ctx)
        xh = k.sbuf("xh", [128, NCH * 512], BF16)
        mesall = xh
        sq = k.sbuf("sq", [128, NCH * 512], BF16)
        wstall = sq
        xbcp = k.sbuf("xbcp", [128, NXBC * 516], BF16)
        catall = xbcp
        xtmv = xbcp
        bwall = xbcp
        XTM0 = 8192
        BW0 = 12288
        xbc = k.sbuf("xbc", [128, NXBC * TILE], BF16)
        ybuf = k.sbuf("ybuf", [128, 32 * TILE], BF16)
        hbuf = k.sbuf("hbuf", [128, 32 * 64], F32)
        hbuf_bf = k.sbuf("hbuf_bf", [128, 32 * 64], BF16)
        xtile = [k.sbuf(f"xtile{i}", [128, 512], F32) for i in range(2)]
        xres = [k.sbuf(f"xres{i}", [128, 512], F32) for i in range(2)]
        oref = [k.sbuf(f"oref{i}", [128, 512], F32) for i in range(2)]
        dsc = [k.sbuf(f"dsc{i}", [128, 512], F32) for i in range(2)]
        tails = k.sbuf("tails", [128, NXBC * 3], BF16)
        gct = k.sbuf("gct", [128, 128], BF16)
        btm = k.sbuf("btm", [128, 128], BF16)
        cdup = k.sbuf("cdup", [128, 128], BF16)
        dtraw = k.sbuf("dtraw", [64, TILE], F32)
        dtv = k.sbuf("dtv", [64, TILE], F32)
        lam = k.sbuf("lam", [64, TILE], F32)
        lamc = k.sbuf("lamc", [64, TILE], F32)
        lcarry = k.sbuf("lcarry", [64, 1], F32)
        ldstack = k.sbuf("ldstack", [128, TILE], F32)
        ldtm = k.sbuf("ldtm", [128, NCK * 128], F32)
        lsm = k.sbuf("lsm", [64, 2 * NCK + 1], F32)
        lnext_bc = k.sbuf("lnext_bc", [128, NCK * 64], F32)
        lstart_bc = k.sbuf("lstart_bc", [128, NCK * 64], F32)
        kap_bc = k.sbuf("kap_bc", [128, 64], F32)
        decfull = k.sbuf("decfull", [128, NCK * 64], F32)
        decpair = k.sbuf("decpair", [128, NCK * 32], F32)
        kappair = k.sbuf("kappair", [128, 32], F32)
        wall = k.sbuf("wall", [128, NCK * 64], F32)
        rows = k.sbuf("rows", [1, 512], F32)
        selt = k.sbuf("selt", [64, 32 * 128], F32)
        penlt = k.sbuf("penlt", [128, 128], BF16)
        penrt = k.sbuf("penrt", [128, 4 * 128], BF16)
        identtb = k.sbuf("identtb", [128, 128], BF16)
        identtf = k.sbuf("identtf", [128, 128], F32)
        cwtt = k.sbuf("cwtt", [128, NXBC * 4], F32)
        cbtt = k.sbuf("cbtt", [128, NXBC], F32)
        dtbt = k.sbuf("dtbt", [64, 1], F32)
        acol = k.sbuf("acol", [64, 1], F32)
        drept = k.sbuf("drept", [128, 32], F32)
        nwtt = k.sbuf("nwtt", [128, 32], F32)
        ones_col_f = k.sbuf("ones_col_f", [128, 1], F32)
        ones_col_b = k.sbuf("ones_col_b", [128, 1], BF16)
        ones_row_f = k.sbuf("ones_row_f", [1, 128], F32)
        ones_row_b = k.sbuf("ones_row_b", [1, 128], BF16)
        cpool = k.sbuf("cpool", [128, 2], F32)
        nc.const_aps.register(F32, LN_EPS, cpool[:, 0:1])
        wblk = [k.sbuf(f"wblk{i}", [128, 8 * 128], BF16) for i in range(2)]
        woblk = [k.sbuf(f"woblk{i}", [128, 8 * 128], BF16) for i in range(2)]

        pb = [k.psum(f"pb{i}") for i in range(7)]
        pbt = ctx.enter_context(nc.psum_tensor("pbt", [128, 1024], BF16))

        dse4 = [k.sem() for _ in range(4)]
        ose4 = [k.sem() for _ in range(4)]
        wse = [k.sem(), k.sem()]
        zse = [k.sem(), k.sem()]
        z2se = [k.sem(), k.sem()]
        lse = k.sem()
        sse = k.sem()
        pse = k.sem()
        mse = k.sem()
        m2se = k.sem()
        m3se = k.sem()
        vse = k.sem()
        v2se = k.sem()
        v3se = k.sem()
        ase_ = k.sem()
        a2se = k.sem()
        xse = k.sem()
        use = k.sem()

        def row(i):
            return rows[0:1, i * 512:(i + 1) * 512]

        ROW_MU, ROW_RR = 0, 1

        def emit_ln(t0):
            with nc.Block() as b:
                d_base = [x.n for x in dse4]
                s_base, p_base = sse.n, pse.n

                @b.sync
                def _(sync):
                    for c in range(NCH):
                        bi = c % 2
                        if c >= 2:
                            sync.wait_ge(sse.h, s_base + (c - 2) + 1)
                            sync.wait_ge(pse.h, p_base + (c - 2) + 1)
                            sync.wait_ge(dse4[bi].h, d_base[bi] + 16 * (c // 2))
                        sync.dma_start(out=xtile[bi][:],
                                       in_=xin[:, c * T + t0: c * T + t0 + 512]
                                       ).then_inc(dse4[bi].h, 16)
                        dse4[bi].n += 16

                @b.scalar
                def _(s):
                    for c in range(NCH):
                        bi = c % 2
                        s.wait_ge(dse4[bi].h, d_base[bi] + 16 * (c // 2 + 1))
                        inst = s.activation(sq[:, c * 512:(c + 1) * 512],
                                            xtile[bi][:], AF.Square)
                        sse.inc(inst)

                @b.tensor
                def _(t):
                    for c in range(NCH):
                        bi = c % 2
                        t.wait_ge(dse4[bi].h, d_base[bi] + 16 * (c // 2 + 1))
                        inst = t.matmul(pb[0][0:1, :], ones_col_f[:], xtile[bi][:],
                                        start=(c == 0), stop=(c == NCH - 1))
                        pse.inc(inst)
                    for c in range(NCH):
                        t.wait_ge(sse.h, s_base + c + 1)
                        t.matmul(pb[1][0:1, :], ones_col_b[:],
                                 sq[:, c * 512:(c + 1) * 512],
                                 start=(c == 0), stop=(c == NCH - 1))

            with nc.Block() as b:
                @b.vector
                def _(v):
                    inv_n = 1.0 / D
                    v.tensor_scalar(row(ROW_MU), pb[0][0:1, :], inv_n, None, ALU.mult)
                    v.tensor_scalar(dsc[0][0:1, 0:512], pb[1][0:1, :], inv_n, None,
                                    ALU.mult)
                    v.drain()
                    v.tensor_tensor(dsc[1][0:1, 0:512], row(ROW_MU), row(ROW_MU),
                                    ALU.mult)
                    v.drain()
                    inst = v.tensor_tensor(dsc[0][0:1, 0:512], dsc[0][0:1, 0:512],
                                           dsc[1][0:1, 0:512], ALU.subtract)
                    xse.inc(inst)

                @b.scalar
                def _(s):
                    s.wait_ge(xse.h, xse.n)
                    s.activation(dsc[0][0:1, 0:512], dsc[0][0:1, 0:512], AF.Ln,
                                 bias=LN_EPS)
                    s.drain()
                    s.activation(dsc[1][0:1, 0:512], dsc[0][0:1, 0:512], AF.Exp, scale=-0.5)

            with nc.Block() as b:
                @b.tensor
                def _(t):
                    t.matmul(pb[2][:], ones_row_f[:], row(ROW_MU), start=True, stop=True)
                    t.matmul(pb[3][:], ones_row_f[:], dsc[1][0:1, 0:512], start=True, stop=True)

            with nc.Block() as b:
                d_base = [x.n for x in dse4]
                v_base = vse.n

                @b.sync
                def _(sync):
                    for c in range(NCH):
                        bi = c % 2
                        if c >= 2:
                            sync.wait_ge(vse.h, v_base + (c - 2) + 1)
                            sync.wait_ge(dse4[bi].h, d_base[bi] + 16 * (c // 2))
                        sync.dma_start(out=xtile[bi][:],
                                       in_=xin[:, c * T + t0: c * T + t0 + 512]
                                       ).then_inc(dse4[bi].h, 16)
                        dse4[bi].n += 16

                @b.vector
                def _(v):
                    for c in range(NCH):
                        bi = c % 2
                        v.wait_ge(dse4[bi].h, d_base[bi] + 16 * (c // 2 + 1))
                        inst = v.tensor_tensor(xh[:, c * 512:(c + 1) * 512],
                                               xtile[bi][:], pb[2][:], ALU.subtract)
                        vse.inc(inst)
                    v.drain()
                    for c in range(NCH):
                        v.tensor_tensor(xh[:, c * 512:(c + 1) * 512],
                                        xh[:, c * 512:(c + 1) * 512],
                                        pb[3][:], ALU.mult)

        with nc.Block() as b:
            @b.sync
            def _(sync):
                for ap_d, ap_s in [(selt[:], sel32), (penlt[:], penl), (penrt[:], penr),
                                   (identtb[:], identb), (identtf[:], identf),
                                   (cwtt[:], cwt), (cbtt[:], cbt),
                                   (dtbt[:], dtb), (acol[:], alog), (drept[:], drep),
                                   (nwtt[:], nwt)]:
                    sync.dma_start(out=ap_d, in_=ap_s).then_inc(use.h, 16)
                    use.n += 16
                sync.wait_ge(use.h, use.n)

            @b.vector
            def _(v):
                v.memset(cpool[:, 0:1], LN_EPS)
                v.memset(ones_col_f[:], 1.0)
                v.memset(ones_col_b[:], 1.0)
                v.memset(ones_row_f[:], 1.0)
                v.memset(ones_row_b[:], 1.0)
                v.memset(hbuf[:], 0.0)
                v.memset(hbuf_bf[:], 0.0)
                v.memset(lcarry[:], 0.0)
                v.memset(tails[:], 0.0)

        with nc.Block() as b:
            @b.scalar
            def _(s):
                s.activation(acol[:], acol[:], AF.Exp)
                s.drain()
                s.activation(acol[:], acol[:], AF.Copy, scale=-1.0)

        for tt in range(NT):
            t0 = tt * TILE

            emit_ln(t0)

            with nc.Block() as b:
                w_base = [wse[0].n, wse[1].n]
                m_base, v_base = mse.n, vse.n
                z_base = [zse[0].n, zse[1].n]

                @b.gpsimd
                def _(g):
                    for c in range(NIN):
                        for hf in range(2):
                            i = 2 * c + hf
                            bi = i % 2
                            if i >= 2:
                                g.wait_ge(mse.h, m_base + (i - 2) + 1)
                                g.wait_ge(wse[bi].h, w_base[bi] + 16 * (i // 2))
                            g.dma_start(out=wblk[bi][:],
                                        in_=win[c][:, hf * 1024:(hf + 1) * 1024]
                                        ).then_inc(wse[bi].h, 16)
                            wse[bi].n += 16

                @b.tensor
                def _(t):
                    for c in range(NIN):
                        bank = pb[4 + c % 2]
                        if c >= 2:
                            t.wait_ge(vse.h, v_base + (c - 2) + 1)
                        for hf in range(2):
                            i = 2 * c + hf
                            bi = i % 2
                            t.wait_ge(wse[bi].h, w_base[bi] + 16 * (i // 2 + 1))
                            for kc in range(8):
                                inst = t.matmul(bank[:],
                                                wblk[bi][:, kc * 128:(kc + 1) * 128],
                                                xh[:, (hf * 8 + kc) * 512:
                                                   (hf * 8 + kc) * 512 + 512],
                                                start=(hf == 0 and kc == 0),
                                                stop=(hf == 1 and kc == 7))
                            mse.inc(inst)

                @b.vector
                def _(v):
                    for c in range(NIN):
                        bi = c % 2
                        bank = pb[4 + bi]
                        v.wait_ge(mse.h, m_base + 2 * c + 2)
                        if c < 32:
                            if c >= 2:
                                v.wait_ge(zse[c % 2].h, z_base[c % 2] + 16 * (c // 2))
                            inst = v.tensor_copy(
                                ybuf[:, (c % 2) * 512:(c % 2) * 512 + 512], bank[:])
                        elif c < 65:
                            xc = c - 32
                            inst = v.tensor_copy(
                                xbcp[:, xc * 516 + 3: xc * 516 + 3 + 512], bank[:])
                        else:
                            inst = v.tensor_copy(dtraw[0:64, :], bank[0:64, :])
                        vse.inc(inst)

                @b.sync
                def _(sync):
                    for c in range(32):
                        sync.wait_ge(vse.h, v_base + c + 1)
                        if c >= 2:
                            sync.wait_ge(zse[c % 2].h, z_base[c % 2] + 16 * (c // 2))
                        sync.dma_start(out=zspill[:, c * TILE:(c + 1) * TILE],
                                       in_=ybuf[:, (c % 2) * 512:(c % 2) * 512 + 512]
                                       ).then_inc(zse[c % 2].h, 16)
                        zse[c % 2].n += 16
                    sync.wait_ge(zse[0].h, zse[0].n)
                    sync.wait_ge(zse[1].h, zse[1].n)

            with nc.Block() as b:
                @b.vector
                def _(v):
                    v.tensor_copy(xbcp[:].rearrange("p (c q) -> p c q", q=516)[:, :, 0:3],
                                  tails[:].rearrange("p (c q) -> p c q", q=3))

            with nc.Block() as b:
                v_base2, a_base2 = vse.n, ase_.n

                @b.vector
                def _(v):
                    for xc in range(NXBC):
                        bi = xc % 2
                        base = xc * 516
                        if xc >= 2:
                            v.wait_ge(ase_.h, a_base2 + (xc - 2) + 1)
                        v.tensor_scalar(dsc[bi][:], xbcp[:, base + 0: base + 512],
                                        cwtt[:, xc * 4: xc * 4 + 1], None, ALU.mult)
                        inst = None
                        for j in range(1, 4):
                            v.drain()
                            inst = v.scalar_tensor_tensor(
                                dsc[bi][:], xbcp[:, base + j: base + j + 512],
                                cwtt[:, xc * 4 + j: xc * 4 + j + 1], dsc[bi][:],
                                ALU.mult, ALU.add)
                        vse.inc(inst)
                    v.tensor_copy(tails[:].rearrange("p (c q) -> p c q", q=3),
                                  xbcp[:].rearrange("p (c q) -> p c q", q=516)
                                  [:, :, 512:515])

                @b.scalar
                def _(s):
                    for xc in range(NXBC):
                        bi = xc % 2
                        s.wait_ge(vse.h, v_base2 + xc + 1)
                        inst = s.activation(xbc[:, xc * TILE: xc * TILE + 512],
                                            dsc[bi][:], AF.Silu, bias=cbtt[:, xc:xc + 1])
                        ase_.inc(inst)

            with nc.Block() as b:
                @b.scalar
                def _(s):
                    s.activation(dtv[:], dtraw[:], AF.Abs, bias=dtbt[:])
                    s.drain()
                    s.activation(dtv[:], dtv[:], AF.Exp, scale=-1.0)
                    s.drain()
                    s.activation(dtv[:], dtv[:], AF.Ln, bias=1.0)
                    s.drain()
                    inst = s.activation(lam[:], dtraw[:], AF.Relu, bias=dtbt[:])
                    ase_.inc(inst)

                @b.vector
                def _(v):
                    v.wait_ge(ase_.h, ase_.n)
                    v.tensor_tensor(dtv[:], dtv[:], lam[:], ALU.add)
                    v.drain()
                    v.tensor_scalar(lam[:], dtv[:], acol[:], None, ALU.mult)
                    v.drain()
                    v.tensor_tensor_scan(lamc[:], lam[:], lam[:], lcarry[:],
                                         ALU.add, ALU.bypass)
                    v.drain()
                    v.tensor_copy(lcarry[:], lamc[:, TILE - 1:TILE])
                    v.tensor_copy(ldstack[0:64, :], lamc[:])
                    v.tensor_copy(ldstack[64:128, :], dtv[:])
                    lview = lamc[:].rearrange("p (c q) -> p c q", q=128)
                    v.tensor_copy(lsm[:, NCK:2 * NCK], lview[:, :, 0:1]
                                  .rearrange("p c o -> p (c o)"))
                    v.tensor_copy(lsm[:, 0:NCK - 1], lview[:, 1:, 0:1]
                                  .rearrange("p c o -> p (c o)"))
                    v.tensor_copy(lsm[:, NCK - 1:NCK], lamc[:, TILE - 1:TILE])
                    inst = v.tensor_copy(lsm[:, 2 * NCK:2 * NCK + 1], lam[:, 0:1])
                    xse.inc(inst)

            with nc.Block() as b:
                @b.tensor
                def _(t):
                    t.transpose(pb[1][0:9, 0:64], lsm[:], identtf[0:64, 0:64])
                    for tc in range(NCK):
                        t.transpose(pb[2 + tc % 2][:, (tc // 2) * 128:(tc // 2) * 128 + 128],
                                    ldstack[:, tc * 128:(tc + 1) * 128], identtf[:])

            with nc.Block() as b:
                @b.vector
                def _(v):
                    v.tensor_copy(dsc[0][0:9, 0:64], pb[1][0:9, 0:64])
                    for tc in range(NCK):
                        v.tensor_copy(ldtm[:, tc * 128:(tc + 1) * 128],
                                      pb[2 + tc % 2][:, (tc // 2) * 128:(tc // 2) * 128 + 128])

            with nc.Block() as b:
                @b.sync
                def _(sync):
                    sync.dma_start(out=lbounce, in_=dsc[0][0:9, 0:64]).then_inc(lse.h, 16)
                    lse.n += 16
                    sync.wait_ge(lse.h, lse.n)
                    sync.dma_start(out=lnext_bc[:],
                                   in_=lbounce[0:NCK, :].unsqueeze(0).broadcast_to(
                                       [128, NCK, 64]).rearrange("p c h -> p (c h)")
                                   ).then_inc(lse.h, 16)
                    sync.dma_start(out=lstart_bc[:],
                                   in_=lbounce[NCK:2 * NCK, :].unsqueeze(0).broadcast_to(
                                       [128, NCK, 64]).rearrange("p c h -> p (c h)")
                                   ).then_inc(lse.h, 16)
                    sync.dma_start(out=kap_bc[:],
                                   in_=lbounce[2 * NCK:2 * NCK + 1, :].unsqueeze(0)
                                   .broadcast_to([128, 1, 64]).rearrange("p c h -> p (c h)")
                                   ).then_inc(lse.h, 16)
                    lse.n += 48
                    sync.wait_ge(lse.h, lse.n)

            with nc.Block() as b:
                @b.vector
                def _(v):
                    v.tensor_tensor(decfull[:], lnext_bc[:], lstart_bc[:], ALU.subtract)
                    inst = None
                    for tc in range(NCK):
                        inst = v.tensor_tensor(wall[:, tc * 64:(tc + 1) * 64],
                                               lnext_bc[:, tc * 64:(tc + 1) * 64],
                                               ldtm[:, tc * 128: tc * 128 + 64],
                                               ALU.subtract)
                    xse.inc(inst)

                @b.scalar
                def _(s):
                    s.wait_ge(xse.h, xse.n)
                    s.activation(decfull[:], decfull[:], AF.Exp)
                    inst = s.activation(wall[:], wall[:], AF.Exp)
                    a2se.inc(inst)

            with nc.Block() as b:
                @b.vector
                def _(v):
                    for tc in range(NCK):
                        v.tensor_tensor(wall[:, tc * 64:(tc + 1) * 64],
                                        wall[:, tc * 64:(tc + 1) * 64],
                                        ldtm[:, tc * 128 + 64: tc * 128 + 128], ALU.mult)
                    dv = decfull[:].rearrange("p (c h) -> p c h", h=64)
                    v.tensor_copy(decpair[0:64, :].rearrange("p (c j) -> p c j", j=32),
                                  dv[0:64, :, 0:64:2])
                    v.tensor_copy(decpair[64:128, :].rearrange("p (c j) -> p c j", j=32),
                                  dv[64:128, :, 1:64:2])
                    v.tensor_copy(kappair[0:64, :], kap_bc[0:64, 0:64:2])
                    v.tensor_copy(kappair[64:128, :], kap_bc[64:128, 1:64:2])
                    v.drain()
                    v.tensor_tensor(hbuf[:].rearrange("p (j q) -> p j q", q=64),
                                    hbuf[:].rearrange("p (j q) -> p j q", q=64),
                                    kappair[:].rearrange("p (j o) -> p j o", o=1)
                                    .broadcast_to([128, 32, 64]), ALU.mult)
                    v.drain()
                    v.tensor_copy(hbuf_bf[:], hbuf[:])

            for tc in range(NCK):
                ck0 = tc * CHUNK
                with nc.Block() as b:
                    m_base, v_base = m2se.n, v2se.n
                    l_base = lse.n

                    @b.tensor
                    def _(t):
                        t.wait_ge(lse.h, l_base + 32)
                        inst = t.matmul(pb[0][:, 0:128],
                                        xbc[0:64, 32 * TILE + ck0: 32 * TILE + ck0 + 128],
                                        cdup[0:64, :],
                                        start=True, stop=True)
                        inst = t.transpose(pbt[:, 0:128],
                                           xbc[:, 32 * TILE + ck0: 32 * TILE + ck0 + 128],
                                           identtb[:])
                        m2se.inc(inst)
                        for r in range(4):
                            t.wait_ge(v2se.h, v_base + r + 1)
                            for q in range(8):
                                xc = r * 8 + q
                                inst = t.transpose(
                                    pbt[:, q * 128:(q + 1) * 128],
                                    xbc[:, xc * TILE + ck0: xc * TILE + ck0 + 128],
                                    identtb[:])
                            m2se.inc(inst)

                    @b.vector
                    def _(v):
                        v.wait_ge(m2se.h, m_base + 1)
                        v.tensor_copy(gct[:], pb[0][:, 0:128])
                        inst = v.tensor_copy(btm[:], pbt[:, 0:128])
                        v2se.inc(inst)
                        for r in range(4):
                            v.wait_ge(m2se.h, m_base + 2 + r)
                            inst = v.tensor_copy(
                                xtmv[:, XTM0 + r * 1024: XTM0 + (r + 1) * 1024], pbt[:])
                            v2se.inc(inst)
                        v.drain()
                        v.tensor_tensor(
                            xbcp[:, BW0:BW0 + 4096].rearrange("p (h n) -> p h n", n=64),
                            btm[:, 0:64].rearrange("p (o n) -> p o n", o=1)
                            .broadcast_to([128, 64, 64]),
                            wall[:, tc * 64:(tc + 1) * 64]
                            .rearrange("p (h o) -> p h o", o=1)
                            .broadcast_to([128, 64, 64]), ALU.mult)

                    @b.sync
                    def _(sync):
                        sync.dma_start(out=cdup[0:64, :],
                                       in_=xbc[64:128, 32 * TILE + ck0:
                                               32 * TILE + ck0 + 128]
                                       ).then_inc(lse.h, 16)
                        sync.dma_start(out=cdup[64:128, :],
                                       in_=xbc[64:128, 32 * TILE + ck0:
                                               32 * TILE + ck0 + 128]
                                       ).then_inc(lse.h, 16)
                        lse.n += 32
                        sync.wait_ge(lse.h, lse.n)

                with nc.Block() as b:
                    m_base, v_base, a_base = mse.n, vse.n, ase_.n
                    m3_base = m3se.n

                    @b.tensor
                    def _(t):
                        for g in range(16):
                            bk = pb[2 + g % 2]
                            if g >= 2:
                                t.wait_ge(vse.h, v_base + 2 * (g - 2) + 1)
                            for hh in range(4):
                                h = 4 * g + hh
                                rb = (h // 32) * 32
                                h32 = h % 32
                                t.matmul(bk[:, hh * 128:(hh + 1) * 128],
                                         selt[rb:rb + 32, h32 * 128:(h32 + 1) * 128],
                                         lamc[rb:rb + 32, ck0:ck0 + 128],
                                         start=(hh == 0), stop=False)
                            inst = t.matmul(bk[:], penlt[:], penrt[:],
                                            start=False, stop=True)
                            mse.inc(inst)
                            if g >= 1:
                                t.wait_ge(ase_.h, a_base + g)
                                inst = t.matmul(pb[4 + (g - 1) % 2][:],
                                                ones_row_b[:],
                                                mesall[0:1, (g - 1) * 512:g * 512],
                                                start=True, stop=True)
                                m3se.inc(inst)
                        t.wait_ge(ase_.h, a_base + 16)
                        inst = t.matmul(pb[4 + 15 % 2][:], ones_row_b[:],
                                        mesall[0:1, 15 * 512:16 * 512],
                                        start=True, stop=True)
                        m3se.inc(inst)

                    @b.vector
                    def _(v):
                        for g in range(16):
                            v.wait_ge(mse.h, m_base + g + 1)
                            bk = pb[2 + g % 2]
                            lcols = ldtm[:, tc * 128 + 4 * g: tc * 128 + 4 * g + 4]
                            dcols = ldtm[:, tc * 128 + 64 + 4 * g:
                                         tc * 128 + 64 + 4 * g + 4]
                            inst = v.tensor_tensor(dsc[g % 2][:].rearrange(
                                "p (h q) -> p h q", q=128),
                                bk[:].rearrange("p (h q) -> p h q", q=128),
                                lcols.rearrange("p (h o) -> p h o", o=1)
                                .broadcast_to([128, 4, 128]), ALU.subtract)
                            vse.inc(inst)
                            v.wait_ge(ase_.h, a_base + g + 1)
                            v.tensor_tensor(wstall[:, g * 512:(g + 1) * 512]
                                            .rearrange("p (h q) -> p h q", q=128),
                                            mesall[:, g * 512:(g + 1) * 512]
                                            .rearrange("p (h q) -> p h q", q=128),
                                            dcols.rearrange("p (h o) -> p h o", o=1)
                                            .broadcast_to([128, 4, 128]), ALU.mult)
                            v.drain()
                            inst = v.tensor_tensor(
                                wstall[:, g * 512:(g + 1) * 512]
                                .rearrange("p (h q) -> p h q", q=128),
                                wstall[:, g * 512:(g + 1) * 512]
                                .rearrange("p (h q) -> p h q", q=128),
                                gct[:].rearrange("p (o q) -> p o q", o=1)
                                .broadcast_to([128, 4, 128]), ALU.mult)
                            vse.inc(inst)
                            v.wait_ge(m3se.h, m3_base + g + 1)
                            inst = v.tensor_tensor(
                                catall[:, g * 512:(g + 1) * 512]
                                .rearrange("p (h q) -> p h q", q=128),
                                cdup[:].rearrange("p (o q) -> p o q", o=1)
                                .broadcast_to([128, 4, 128]),
                                pb[4 + g % 2][:].rearrange("p (h q) -> p h q", q=128),
                                ALU.mult)
                            v3se.inc(inst)

                    @b.scalar
                    def _(s):
                        for g in range(16):
                            s.wait_ge(vse.h, v_base + 2 * g + 1)
                            inst = s.activation(mesall[:, g * 512:(g + 1) * 512],
                                                dsc[g % 2][:], AF.Exp)
                            ase_.inc(inst)

                with nc.Block() as b:
                    m_base, v_base = mse.n, vse.n

                    @b.tensor
                    def _(t):
                        for g in range(16):
                            yb = pb[2 + g % 2]
                            ib = pb[4 + g % 2]
                            if g >= 2:
                                t.wait_ge(vse.h, v_base + 2 * (g - 1))
                            for hh in range(2):
                                for jj in range(2):
                                    j = 2 * g + jj
                                    h = 2 * j + hh
                                    t.matmul(yb[hh * 64:hh * 64 + 64,
                                                jj * 128:jj * 128 + 128],
                                             xtmv[:, XTM0 + j * 128 + hh * 64:
                                                  XTM0 + j * 128 + hh * 64 + 64],
                                             wstall[:, h * 128:(h + 1) * 128],
                                             start=(jj == 0), stop=False)
                                    inst = t.matmul(
                                        yb[hh * 64:hh * 64 + 64,
                                           jj * 128:jj * 128 + 128],
                                        hbuf_bf[hh * 64:hh * 64 + 64,
                                                j * 64:(j + 1) * 64],
                                        catall[hh * 64:hh * 64 + 64,
                                               h * 128:(h + 1) * 128],
                                        start=False,
                                        stop=(jj == 1))
                            mse.inc(inst)
                            for hh in range(2):
                                for jj in range(2):
                                    j = 2 * g + jj
                                    h = 2 * j + hh
                                    inst = t.matmul(
                                        ib[hh * 64:hh * 64 + 64,
                                           jj * 128 + 0: jj * 128 + 64],
                                        xbcp[:, BW0 + h * 64: BW0 + h * 64 + 64],
                                        xtmv[:, XTM0 + j * 128 + hh * 64:
                                             XTM0 + j * 128 + hh * 64 + 64],
                                        start=(jj == 0), stop=(jj == 1))
                            mse.inc(inst)

                    @b.vector
                    def _(v):
                        for g in range(16):
                            yb = pb[2 + g % 2]
                            ib = pb[4 + g % 2]
                            v.wait_ge(mse.h, m_base + 2 * g + 2)
                            for jj in range(2):
                                j = 2 * g + jj
                                v.scalar_tensor_tensor(
                                    ybuf[:, j * TILE + ck0: j * TILE + ck0 + 128],
                                    xbc[:, j * TILE + ck0: j * TILE + ck0 + 128],
                                    drept[:, j:j + 1],
                                    yb[:, jj * 128:(jj + 1) * 128],
                                    ALU.mult, ALU.add)
                                inst = v.scalar_tensor_tensor(
                                    hbuf[:, j * 64:(j + 1) * 64],
                                    hbuf[:, j * 64:(j + 1) * 64],
                                    decpair[:, tc * 32 + j: tc * 32 + j + 1],
                                    ib[:, jj * 128: jj * 128 + 64],
                                    ALU.mult, ALU.add)
                                vse.inc(inst)
                        v.drain()
                        v.tensor_copy(hbuf_bf[:], hbuf[:])

            with nc.Block() as b:
                z_base = [z2se[0].n, z2se[1].n]
                v_base, a_base = vse.n, ase_.n

                @b.gpsimd
                def _(g):
                    for c in range(32):
                        bi = c % 2
                        if c >= 2:
                            g.wait_ge(ase_.h, a_base + (c - 2) + 1)
                            g.wait_ge(z2se[bi].h, z_base[bi] + 16 * (c // 2))
                        g.dma_start(out=dsc[bi][:].bitcast(BF16)[:, 0:512],
                                    in_=zspill[:, c * TILE:(c + 1) * TILE]
                                    ).then_inc(z2se[bi].h, 16)
                        z2se[bi].n += 16

                @b.scalar
                def _(s):
                    for c in range(32):
                        bi = c % 2
                        s.wait_ge(z2se[bi].h, z_base[bi] + 16 * (c // 2 + 1))
                        if c >= 2:
                            s.wait_ge(vse.h, v_base + (c - 2) + 1)
                        inst = s.activation(dsc[bi][:].bitcast(BF16)[:, 512:1024],
                                            dsc[bi][:].bitcast(BF16)[:, 0:512], AF.Silu)
                        ase_.inc(inst)

                @b.vector
                def _(v):
                    for c in range(32):
                        bi = c % 2
                        v.wait_ge(ase_.h, a_base + c + 1)
                        inst = v.tensor_tensor(ybuf[:, c * TILE:(c + 1) * TILE],
                                               ybuf[:, c * TILE:(c + 1) * TILE],
                                               dsc[bi][:].bitcast(BF16)[:, 512:1024],
                                               ALU.mult)
                        vse.inc(inst)

            with nc.Block() as b:
                s_base = sse.n

                @b.scalar
                def _(s):
                    p_base0 = pse.n
                    for c in range(32):
                        if c >= 16:
                            s.wait_ge(pse.h, p_base0 + (c - 16) + 1)
                        inst = s.activation(sq[:, (c % 16) * 512:(c % 16) * 512 + 512],
                                            ybuf[:, c * TILE:(c + 1) * TILE], AF.Square)
                        sse.inc(inst)

                @b.tensor
                def _(t):
                    for c in range(32):
                        t.wait_ge(sse.h, s_base + c + 1)
                        inst = t.matmul(pb[1][0:1, :], ones_col_b[:],
                                        sq[:, (c % 16) * 512:(c % 16) * 512 + 512],
                                        start=(c == 0), stop=(c == 31))
                        pse.inc(inst)


            with nc.Block() as b:
                @b.vector
                def _(v):
                    inst = v.tensor_scalar(dsc[0][0:1, 0:512], pb[1][0:1, :],
                                           1.0 / 4096.0, None, ALU.mult)
                    xse.inc(inst)

                @b.scalar
                def _(s):
                    s.wait_ge(xse.h, xse.n)
                    s.activation(dsc[0][0:1, 0:512], dsc[0][0:1, 0:512], AF.Ln,
                                 bias=LN_EPS)
                    s.drain()
                    s.activation(dsc[1][0:1, 0:512], dsc[0][0:1, 0:512], AF.Exp,
                                 scale=-0.5)

            with nc.Block() as b:
                @b.tensor
                def _(t):
                    t.matmul(pb[2][:], ones_row_f[:], dsc[1][0:1, 0:512],
                             start=True, stop=True)

            with nc.Block() as b:
                @b.vector
                def _(v):
                    for c in range(32):
                        v.tensor_tensor(ybuf[:, c * TILE:(c + 1) * TILE],
                                        ybuf[:, c * TILE:(c + 1) * TILE],
                                        pb[2][:], ALU.mult)
                    v.drain()
                    for c in range(32):
                        v.tensor_scalar(ybuf[:, c * TILE:(c + 1) * TILE],
                                        ybuf[:, c * TILE:(c + 1) * TILE],
                                        nwtt[:, c:c + 1], None, ALU.mult)

            with nc.Block() as b:
                w_base = [wse[0].n, wse[1].n]
                m_base, v_base = mse.n, vse.n
                d_base = [x.n for x in dse4]
                o_base = [x.n for x in ose4]

                @b.gpsimd
                def _(g):
                    for dc in range(NCH):
                        for qt in range(4):
                            i = 4 * dc + qt
                            bi = i % 2
                            if i >= 2:
                                g.wait_ge(mse.h, m_base + (i - 2) + 1)
                                g.wait_ge(wse[bi].h, w_base[bi] + 16 * (i // 2))
                            g.dma_start(out=woblk[bi][:],
                                        in_=wout[dc][:, qt * 1024:(qt + 1) * 1024]
                                        ).then_inc(wse[bi].h, 16)
                            wse[bi].n += 16

                @b.tensor
                def _(t):
                    for dc in range(NCH):
                        bank = pb[4 + dc % 2]
                        if dc >= 2:
                            t.wait_ge(vse.h, v_base + (dc - 2) + 1)
                        for qt in range(4):
                            i = 4 * dc + qt
                            bi = i % 2
                            t.wait_ge(wse[bi].h, w_base[bi] + 16 * (i // 2 + 1))
                            for kc in range(8):
                                inst = t.matmul(
                                    bank[:],
                                    woblk[bi][:, kc * 128:(kc + 1) * 128],
                                    ybuf[:, (qt * 8 + kc) * TILE:
                                         (qt * 8 + kc) * TILE + 512],
                                    start=(qt == 0 and kc == 0),
                                    stop=(qt == 3 and kc == 7))
                            mse.inc(inst)

                @b.sync
                def _(sync):
                    for dc in range(NCH):
                        bi = dc % 2
                        if dc >= 2:
                            sync.wait_ge(vse.h, v_base + (dc - 2) + 1)
                            sync.wait_ge(dse4[bi].h, d_base[bi] + 16 * (dc // 2))
                        sync.dma_start(out=xres[bi][:],
                                       in_=xin[:, dc * T + t0: dc * T + t0 + 512]
                                       ).then_inc(dse4[bi].h, 16)
                        dse4[bi].n += 16
                        if dc >= 1:
                            bj = (dc - 1) % 2
                            sync.wait_ge(vse.h, v_base + dc)
                            if dc >= 3:
                                sync.wait_ge(ose4[bj].h, o_base[bj] + 16 * ((dc - 1) // 2))
                            sync.dma_start(
                                out=xout[:, (dc - 1) * T + t0: (dc - 1) * T + t0 + 512],
                                in_=oref[bj][:]).then_inc(ose4[bj].h, 16)
                            ose4[bj].n += 16
                    bj = 15 % 2
                    sync.wait_ge(vse.h, v_base + 16)
                    sync.dma_start(out=xout[:, 15 * T + t0: 15 * T + t0 + 512],
                                   in_=oref[bj][:]).then_inc(ose4[bj].h, 16)
                    ose4[bj].n += 16
                    for x in ose4:
                        sync.wait_ge(x.h, x.n)

                @b.vector
                def _(v):
                    for dc in range(NCH):
                        bi = dc % 2
                        v.wait_ge(mse.h, m_base + 4 * dc + 4)
                        v.wait_ge(dse4[bi].h, d_base[bi] + 16 * (dc // 2 + 1))
                        if dc >= 2:
                            v.wait_ge(ose4[bi].h, o_base[bi] + 16 * (dc // 2))
                        inst = v.tensor_tensor(oref[bi][:], xres[bi][:],
                                               pb[4 + dc % 2][:], ALU.add)
                        vse.inc(inst)
    return nc

MNH = 8


def build_l1attn(T=2048, NITER=30, NCK_RUN=None, STAGE=6, MS=4):
    nc = bass.Bass("TRN2", target_bir_lowering=False)
    NCKT = T // 128
    NT = T // 512

    xin = nc.dram_tensor("xin", [128, NCH * T], F32, kind="ExternalInput").ap()
    wq = nc.dram_tensor("wq", [MNH, 128, NCH * 128], F32, kind="ExternalInput").ap()
    wk = nc.dram_tensor("wk", [MNH, 128, NCH * 128], F32, kind="ExternalInput").ap()
    wv = nc.dram_tensor("wv", [MNH, 128, NCH * 128], F32, kind="ExternalInput").ap()
    waba = nc.dram_tensor("waba", [128, NCH * MNH], F32, kind="ExternalInput").ap()
    wabb = nc.dram_tensor("wabb", [128, NCH * MNH], F32, kind="ExternalInput").ap()
    abain = nc.dram_tensor("abain", [MNH, 1], F32, kind="ExternalInput").ap()
    abbin = nc.dram_tensor("abbin", [MNH, 1], F32, kind="ExternalInput").ap()
    lamin = nc.dram_tensor("lamin", [128, MNH], F32, kind="ExternalInput").ap()
    onw = nc.dram_tensor("onw", [1, MNH * 128], F32, kind="ExternalInput").ap()
    sel8 = nc.dram_tensor("sel8", [MNH, MNH * 128], F32, kind="ExternalInput").ap()
    penl = nc.dram_tensor("penl", [128, 128], BF16, kind="ExternalInput").ap()
    penr = nc.dram_tensor("penr", [128, 128], BF16, kind="ExternalInput").ap()
    identb = nc.dram_tensor("identb", [128, 128], BF16, kind="ExternalInput").ap()
    identf = nc.dram_tensor("identf", [128, 128], F32, kind="ExternalInput").ap()
    kspill = nc.dram_tensor("kspill", [NT, 128, MNH, 512], BF16).ap()
    vspill = nc.dram_tensor("vspill", [NT, 128, MNH, 512], BF16).ap()
    qspill = nc.dram_tensor("qspill", [NT, 128, MNH, 512], BF16).ap()
    gbounce = nc.dram_tensor("gbounce", [NCKT + 1, MNH], F32).ap()
    abounce = nc.dram_tensor("abounce", [NCKT * MNH, 128], F32).ap()
    lrbounce = nc.dram_tensor("lrbounce", [MNH, 128], F32).ap()
    oimg = nc.dram_tensor("oimg", [128, MNH * T], BF16, kind="ExternalOutput").ap()

    with ExitStack() as ctx:
        k = K(nc, ctx)
        W = MNH * 128

        xh = k.sbuf("xh", [128, NCH * T], BF16)
        xst = k.sbuf("xst", [128, W], F32)
        rst = k.sbuf("rst", [128, W], F32)
        pst = k.sbuf("pst", [128, W], F32)
        qst = k.sbuf("qst", [128, W], F32)
        bqt = k.sbuf("bqt", [128, W], F32)
        rnm = k.sbuf("rnm", [128, W], F32)
        junk = k.sbuf("junk", [128, W], F32)
        akk = k.sbuf("akk", [128, W], F32)
        akv = k.sbuf("akv", [128, W], F32)
        akkb = k.sbuf("akkb", [128, W], BF16)
        akvb = k.sbuf("akvb", [128, W], BF16)
        kfm = k.sbuf("kfm", [128, W], BF16)
        vfm = k.sbuf("vfm", [128, W], BF16)
        ktm = k.sbuf("ktm", [128, W], BF16)
        vtm = k.sbuf("vtm", [128, W], BF16)
        k2tm = k.sbuf("k2tm", [128, W], BF16)
        maskb = k.sbuf("maskb", [128, W], BF16)
        kwall = k.sbuf("kwall", [128, W], BF16)
        ebuf = k.sbuf("ebuf", [128, W], BF16)
        paft = k.sbuf("paft", [128, W], BF16)
        pfm = k.sbuf("pfm", [128, W], BF16)
        pafm = k.sbuf("pafm", [128, W], BF16)
        stall = k.sbuf("stall", [128, W], BF16)
        obuf = k.sbuf("obuf", [128, W], BF16)
        obuf2 = k.sbuf("obuf2", [128, W], BF16)
        lamd = k.sbuf("lamd", [128, W], BF16)
        wnb = k.sbuf("wnb", [128, W], F32)
        dstate = k.sbuf("dstate", [1, W], F32)
        lr8 = k.sbuf("lr8", [1, W], F32)
        arow1 = k.sbuf("arow1", [1, W], F32)
        gmg = k.sbuf("gmg", [MNH, T], F32)
        gmb = k.sbuf("gmb", [MNH, T], F32)
        gsm = k.sbuf("gsm", [MNH, NCKT + 1], F32)
        gzero = k.sbuf("gzero", [MNH, 1], F32)
        ghatb = k.sbuf("ghatb", [MNH, 128], F32)
        gbtm = k.sbuf("gbtm", [128, NCKT * 2 * MNH], F32)
        gstart_bc = k.sbuf("gstart_bc", [128, NCKT * MNH], F32)
        gnext_bc = k.sbuf("gnext_bc", [128, NCKT * MNH], F32)
        ghall = k.sbuf("ghall", [128, NCKT * MNH], F32)
        acall = k.sbuf("acall", [128, NCKT * MNH], F32)
        wcall = k.sbuf("wcall", [128, NCKT * MNH], F32)
        decbig = k.sbuf("decbig", [128, NCKT * MNH], F32)
        atrp = k.sbuf("atrp", [128, 128], F32)
        lamct = k.sbuf("lamct", [128, MNH], F32)
        lamct2 = k.sbuf("lamct2", [128, MNH], F32)
        lsc = k.sbuf("lsc", [128, MNH], F32)
        lr8t = k.sbuf("lr8t", [MNH, 128], F32)
        gs1 = k.sbuf("gs1", [MNH, 512], F32)
        gs2 = k.sbuf("gs2", [MNH, 512], F32)
        abta = k.sbuf("abta", [MNH, 1], F32)
        abtb = k.sbuf("abtb", [MNH, 1], F32)
        abtan = k.sbuf("abtan", [MNH, 1], F32)
        wabat = k.sbuf("wabat", [128, NCH * MNH], BF16)
        wabbt = k.sbuf("wabbt", [128, NCH * MNH], BF16)
        rowsg = k.sbuf("rowsg", [128, 16 * MNH], F32)
        rows = k.sbuf("rows", [1, 2 * 512], F32)
        xtile = [k.sbuf(f"xtile{i}", [128, 512], F32) for i in range(2)]
        tsc = [k.sbuf(f"tsc{i}", [128, 512], F32) for i in range(2)]
        sqb = [k.sbuf(f"sqb{i}", [128, 512], BF16) for i in range(2)]
        qtile = [k.sbuf(f"qtile{i}", [128, 512], BF16) for i in range(2)]
        wblk = [k.sbuf(f"wblk{i}", [128, NCH * 128], BF16) for i in range(2)]
        selt = k.sbuf("selt", [MNH, MNH * 128], F32)
        penlt = k.sbuf("penlt", [128, 128], BF16)
        penrt = k.sbuf("penrt", [128, 128], BF16)
        identtb = k.sbuf("identtb", [128, 128], BF16)
        identtf = k.sbuf("identtf", [128, 128], F32)
        ones_col_f = k.sbuf("ones_col_f", [128, 1], F32)
        ones_col_b = k.sbuf("ones_col_b", [128, 1], BF16)
        ones_row_f = k.sbuf("ones_row_f", [1, 128], F32)
        cpool = k.sbuf("cpool", [128, 6], F32)
        nc.const_aps.register(F32, LN_EPS, cpool[:, 0:1])
        nc.const_aps.register(F32, 1e-30, cpool[:, 1:2])
        nc.const_aps.register(F32, 1.0, cpool[:, 2:3])
        nc.const_aps.register(F32, 1e-6, cpool[:, 3:4])

        ps = [ctx.enter_context(nc.psum_tensor(f"ps{i}", [128, 512], F32))
              for i in range(6)]
        pbt = ctx.enter_context(nc.psum_tensor("pbt", [128, 1024], BF16))
        ps7 = ctx.enter_context(nc.psum_tensor("ps7", [128, 512], F32))

        mse = k.sem()
        vse = k.sem()
        ase_ = k.sem()
        wse = [k.sem(), k.sem()]
        spse = [k.sem(), k.sem()]
        lnse = [k.sem(), k.sem()]
        kse = k.sem()
        vlse = k.sem()
        qlse = k.sem()
        ose = k.sem()
        gse = k.sem()
        g2se = k.sem()
        aese = k.sem()
        ar2se = k.sem()
        lrse = k.sem()
        use = k.sem()
        uge = k.sem()

        R_PQ, R_DN, R_D, R_ND, R_AL, R_NA, R_BE, R_S1, R_S2, R_S3 = range(10)

        def row(i):
            return rowsg[:, i * MNH:(i + 1) * MNH]

        def bcv(buf, c0):
            return (buf[:, c0 * MNH:(c0 + 1) * MNH].unsqueeze(2)
                    .broadcast_to([128, MNH, 128]))

        def v3(ap2d):
            return ap2d.rearrange("p (h j) -> p h j", j=128)

        seq = {"t": [], "v": [], "s": [], "y": [], "g": []}
        esem = {"t": mse, "v": vse, "s": ase_}

        def op(eng, fn, waits=(), sem=None, inc=0, noinc=False):
            wl = [x for x in waits if x is not None]
            if noinc:
                seq[eng].append((fn, wl, None, 0))
                s_ = esem.get(eng)
                return (s_, s_.n if s_ is not None else 0)
            if sem is None and eng in esem:
                sem, inc = esem[eng], 1
            if sem is not None and inc:
                sem.n += inc
            store_sem = sem if eng in esem else None
            seq[eng].append((fn, wl, store_sem, inc))
            return (sem, sem.n if sem is not None else 0)

        def drain(eng):
            op(eng, lambda e: e.drain(), noinc=True)

        G = {}

        def w(tok):
            return tok if (tok is not None and tok[0] is not None and
                           tok[1] > 0) else None

        with nc.Block() as b:
            @b.sync
            def _(sync):
                for ap_d, ap_s in [(selt[:], sel8), (penlt[:], penl),
                                   (penrt[:], penr), (identtb[:], identb),
                                   (identtf[:], identf), (abta[:], abain),
                                   (abtb[:], abbin), (lamct[:], lamin)]:
                    sync.dma_start(out=ap_d, in_=ap_s).then_inc(use.h, 16)
                    use.n += 16
                sync.dma_start(out=wnb[:],
                               in_=onw.unsqueeze(0).broadcast_to([128, 1, W])
                               .rearrange("p o q -> p (o q)")).then_inc(use.h, 16)
                use.n += 16
                sync.wait_ge(use.h, use.n)

            @b.gpsimd
            def _(g):
                g.dma_start(out=wabat[:], in_=waba).then_inc(uge.h, 16)
                g.dma_start(out=wabbt[:], in_=wabb).then_inc(uge.h, 16)
                uge.n += 32
                g.wait_ge(uge.h, uge.n)

            @b.vector
            def _(v):
                v.memset(cpool[:, 0:1], LN_EPS)
                v.memset(cpool[:, 1:2], 1e-30)
                v.memset(cpool[:, 2:3], 1.0)
                v.memset(cpool[:, 3:4], 1e-6)
                v.memset(akk[:], 0.0)
                v.memset(akv[:], 0.0)
                v.memset(akkb[:], 0.0)
                v.memset(akvb[:], 0.0)
                v.memset(dstate[:], 0.0)
                v.memset(gzero[:], 0.0)
                v.memset(ones_col_f[:], 1.0)
                v.memset(ones_col_b[:], 1.0)
                v.memset(ones_row_f[:], 1.0)
                v.wait_ge(use.h, use.n)
                v.drain()
                v.tensor_scalar(abtan[:], abta[:], -1.0, None, ALU.mult)


        op("s", lambda s: s.activation(lsc[:], lamct[:], AF.Abs))
        drain("s")
        op("s", lambda s: s.activation(lsc[:], lsc[:], AF.Exp, scale=-1.0))
        drain("s")
        op("s", lambda s: s.activation(lsc[:], lsc[:], AF.Ln, bias=1.0))
        drain("s")
        t2 = op("s", lambda s: s.activation(lamct2[:], lamct[:], AF.Relu))
        op("v", lambda v: v.tensor_tensor(lamct2[:], lamct2[:], lsc[:],
                                          ALU.add), waits=[w(t2)])
        drain("v")
        t4 = op("v", lambda v: v.tensor_scalar(lamct2[:], lamct2[:], 0.25,
                                               None, ALU.add))
        drain("v")
        for h in range(MNH):
            def f(v, h=h):
                return v.tensor_scalar(lamd[:, h * 128:(h + 1) * 128],
                                       identtb[:], lamct2[:, h:h + 1], None,
                                       ALU.mult)
            op("v", f)
        t5 = op("t", lambda t: t.transpose(ps7[0:MNH, 0:128], lamct2[:],
                                           identtf[:]), waits=[w(t4)])
        t6 = op("v", lambda v: v.tensor_scalar(lr8t[:], ps7[0:MNH, 0:128],
                                               1e-8, None, ALU.add),
                waits=[w(t5)])
        op("y", lambda y: y.dma_start(out=lrbounce, in_=lr8t[:]).then_inc(
            lrse.h, 16), waits=[w(t6)], sem=lrse, inc=16)
        lrtok = (lrse, lrse.n)
        op("y", lambda y: y.dma_start(
            out=lr8[:],
            in_=lrbounce.rearrange("h t -> (h t)").unsqueeze(0)).then_inc(
            lrse.h, 16), waits=[w(lrtok)], sem=lrse, inc=16)
        lr8_tok = (lrse, lrse.n)

        lnuse = [0, 0]
        for tt in range(NT):
            t0 = tt * 512
            sq_toks, mt1_toks, mt2_toks = [], [], []
            for c in range(NCH):
                bi = c % 2
                waits = []
                if c >= 2:
                    waits += [w(sq_toks[c - 2]), w(mt1_toks[c - 2])]
                elif tt > 0:
                    waits += [w(G["lnlast"])]
                lnuse[bi] += 1

                def f(y, bi=bi, c=c, t0=t0):
                    return y.dma_start(
                        out=xtile[bi][:],
                        in_=xin[:, c * T + t0: c * T + t0 + 512]).then_inc(
                        lnse[bi].h, 16)
                op("y", f, waits=waits, sem=lnse[bi], inc=16)
                dtok = (lnse[bi], 16 * lnuse[bi])

                def fs(s, bi=bi, c=c, t0=t0):
                    return s.activation(xh[:, c * T + t0: c * T + t0 + 512],
                                        xtile[bi][:], AF.Square)
                sq_toks.append(op("s", fs, waits=[w(dtok)]))

                def fm(t, bi=bi, c=c):
                    return t.matmul(ps[2][0:1, :], ones_col_f[:], xtile[bi][:],
                                    start=(c == 0), stop=(c == NCH - 1))
                mt1_toks.append(op("t", fm, waits=[w(dtok)] + (
                    [w(G.get("lnrow"))] if c == 0 else [])))

                def fm2(t, c=c, t0=t0):
                    return t.matmul(ps[3][0:1, :], ones_col_b[:],
                                    xh[:, c * T + t0: c * T + t0 + 512],
                                    start=(c == 0), stop=(c == NCH - 1))
                mt2_toks.append(op("t", fm2, waits=[w(sq_toks[-1])]))

            inv_n = 1.0 / D
            op("v", lambda v: v.tensor_scalar(rows[0:1, 0:512], ps[2][0:1, :],
                                              inv_n, None, ALU.mult),
               waits=[w(mt1_toks[-1])])
            rtok = op("v", lambda v: v.tensor_scalar(
                rows[0:1, 512:1024], ps[3][0:1, :], inv_n, None, ALU.mult),
                waits=[w(mt2_toks[-1])])
            G["lnrow"] = rtok
            drain("v")
            op("v", lambda v: v.tensor_tensor(junk[0:1, 0:512],
                                              rows[0:1, 0:512],
                                              rows[0:1, 0:512], ALU.mult))
            drain("v")
            vtok = op("v", lambda v: v.tensor_tensor(
                rows[0:1, 512:1024], rows[0:1, 512:1024], junk[0:1, 0:512],
                ALU.subtract))
            op("s", lambda s: s.activation(rows[0:1, 512:1024],
                                           rows[0:1, 512:1024], AF.Ln,
                                           bias=LN_EPS), waits=[w(vtok)])
            drain("s")
            etok = op("s", lambda s: s.activation(rows[0:1, 512:1024],
                                                  rows[0:1, 512:1024], AF.Exp,
                                                  scale=-0.5))
            m1 = op("t", lambda t: t.matmul(ps[4][:], ones_row_f[:],
                                            rows[0:1, 0:512], start=True,
                                            stop=True),
                    waits=[w(vtok), w(G.get("lnbc"))])
            m2 = op("t", lambda t: t.matmul(ps[5][:], ones_row_f[:],
                                            rows[0:1, 512:1024], start=True,
                                            stop=True), waits=[w(etok)])
            ntoks = []
            for c in range(NCH):
                bi = c % 2
                waits = [w(m2)] if c == 0 else []
                if c >= 2:
                    waits.append(w(ntoks[c - 2]))
                lnuse[bi] += 1

                def f(y, bi=bi, c=c, t0=t0):
                    return y.dma_start(
                        out=xtile[bi][:],
                        in_=xin[:, c * T + t0: c * T + t0 + 512]).then_inc(
                        lnse[bi].h, 16)
                op("y", f, waits=waits, sem=lnse[bi], inc=16)
                dtok = (lnse[bi], 16 * lnuse[bi])

                def fv(v, bi=bi, c=c):
                    return v.tensor_tensor(
                        junk[:, (c % 2) * 512:(c % 2) * 512 + 512],
                        xtile[bi][:], ps[4][:], ALU.subtract)
                op("v", fv, waits=[w(dtok), w(m1), w(m2)])
                drain("v")

                def fv2(v, c=c, t0=t0):
                    return v.tensor_tensor(
                        xh[:, c * T + t0: c * T + t0 + 512],
                        junk[:, (c % 2) * 512:(c % 2) * 512 + 512],
                        ps[5][:], ALU.mult)
                ntoks.append(op("v", fv2, waits=[w(mt2_toks[-1])]
                              if c == 0 else []))
            G["lnbc"] = ntoks[-1]
            G["lnlast"] = ntoks[-1]

        spuse = [0, 0]
        nlin = 0
        for ti, (wsrc, spill, do_l2) in enumerate(
                [(wq, qspill, True), (wk, kspill, True), (wv, vspill, False)]):
            for h in range(MNH):
                bi = ti * MNH + h
                slot = bi % 2
                waits = []
                if bi >= 2:
                    waits.append(w(G[f"blkMM_{bi-2}_{NT-1}"]))

                def f(g, slot=slot, wsrc=wsrc, h=h):
                    return g.dma_start(out=wblk[slot][:],
                                       in_=wsrc[h]).then_inc(wse[slot].h, 16)
                op("g", f, waits=waits, sem=wse[slot], inc=16)
                wtok = (wse[slot], wse[slot].n)

                for sub in range(NT):
                    n = bi * NT + sub
                    bnk = ps[n % 2]
                    waits = [w(wtok)]
                    if n == 0:
                        waits.append(w(G["lnlast"]))
                    if n >= 2:
                        np_, ns_ = (n - 2) // NT, (n - 2) % NT
                        waits.append(w(G[f"silu_{np_}_{ns_}"]))
                    for kc in range(NCH):
                        def fm(t, slot=slot, kc=kc, sub=sub, bnk=bnk):
                            return t.matmul(
                                bnk[:],
                                wblk[slot][:, kc * 128:(kc + 1) * 128],
                                xh[:, kc * T + sub * 512: kc * T + sub * 512 + 512],
                                start=(kc == 0), stop=(kc == NCH - 1))
                        mtok = op("t", fm, waits=waits if kc == 0 else [])
                    G[f"blkMM_{bi}_{sub}"] = mtok

                    ss = n % 2
                    qslot = nlin % 2
                    nlin += 1
                    spuse[qslot] += 1
                    sptok = ((spse[qslot], 16 * (spuse[qslot] - 1))
                             if spuse[qslot] > 1 else None)
                    if do_l2:
                        def fs(s, ss=ss, bnk=bnk):
                            return s.activation(tsc[ss][:], bnk[:], AF.Silu)
                        sil = op("s", fs, waits=[w(mtok),
                                                 w(G.get(f"tscr_{ss}"))])
                        G[f"silu_{bi}_{sub}"] = sil
                        drain("s")

                        def fs2(s, ss=ss):
                            return s.activation(sqb[ss][:], tsc[ss][:],
                                                AF.Square)
                        sq2 = op("s", fs2, waits=[w(G.get(f"sqbrd_{ss}"))])

                        def fd(t, ss=ss, n=n):
                            return t.matmul(ps[2 + n % 2][0:1, :],
                                            ones_col_b[:], sqb[ss][:],
                                            start=True, stop=True)
                        dtok = op("t", fd, waits=[w(sq2),
                                                  w(G.get(f"lnr_{n%2}"))])
                        G[f"sqbrd_{ss}"] = dtok

                        def fl(s, ss=ss, n=n):
                            return s.activation(
                                rows[0:1, ss * 512:ss * 512 + 512],
                                ps[2 + n % 2][0:1, :], AF.Ln, bias=1e-30)
                        op("s", fl, waits=[w(dtok),
                                           w(G.get(f"rowsrd_{ss}"))])
                        drain("s")

                        def fe(s, ss=ss):
                            return s.activation(
                                rows[0:1, ss * 512:ss * 512 + 512],
                                rows[0:1, ss * 512:ss * 512 + 512],
                                AF.Exp, scale=-0.5)
                        etok = op("s", fe)
                        G[f"lnr_{n%2}"] = etok

                        def fb(t, ss=ss, n=n):
                            return t.matmul(ps[4 + n % 2][:], ones_row_f[:],
                                            rows[0:1, ss * 512:ss * 512 + 512],
                                            start=True, stop=True)
                        btok = op("t", fb, waits=[w(etok),
                                                  w(G.get(f"qh_{n%2}"))])
                        G[f"rowsrd_{ss}"] = btok

                        def fq(v, ss=ss, n=n, qslot=qslot):
                            return v.tensor_tensor(qtile[qslot][:], tsc[ss][:],
                                                   ps[4 + n % 2][:], ALU.mult)
                        qh = op("v", fq, waits=[w(btok), w(sptok)])
                        G[f"qh_{n%2}"] = qh
                        G[f"tscr_{ss}"] = qh
                        wr = w(qh)
                    else:
                        def fs(s, qslot=qslot, bnk=bnk):
                            return s.activation(qtile[qslot][:], bnk[:],
                                                AF.Silu)
                        sil = op("s", fs, waits=[w(mtok), w(sptok)])
                        G[f"silu_{bi}_{sub}"] = sil
                        wr = w(sil)

                    def fy(y, spill=spill, sub=sub, h=h, qslot=qslot):
                        return y.dma_start(out=spill[sub, :, h, :],
                                           in_=qtile[qslot][:]).then_inc(
                            spse[qslot].h, 16)
                    op("y", fy, waits=[wr], sem=spse[qslot], inc=16)
                    G[f"sp_{qslot}"] = (spse[qslot], spse[qslot].n)
        last_silu = G[f"silu_{3*MNH-1}_{NT-1}"]

        for seg in range(NT):
            pa = ps[seg % 2]
            pbb = ps[2 + seg % 2]
            wa = [w(G.get(f"gev_{seg-2}")), w(G.get(f"gstt_{seg-2}"))]
            if seg == 0:
                wa.append(w(last_silu))
                wa.append(w(G.get("qh_0")))
                wa.append(w(G.get("qh_1")))
            for kc in range(NCH):
                def fa(t, kc=kc, seg=seg, pa=pa):
                    return t.matmul(
                        pa[0:MNH, :], wabat[:, kc * MNH:(kc + 1) * MNH],
                        xh[:, kc * T + seg * 512: kc * T + seg * 512 + 512],
                        start=(kc == 0), stop=(kc == NCH - 1))
                atok = op("t", fa, waits=wa if kc == 0 else [])
            for kc in range(NCH):
                def fbm(t, kc=kc, seg=seg, pbb=pbb):
                    return t.matmul(
                        pbb[0:MNH, :], wabbt[:, kc * MNH:(kc + 1) * MNH],
                        xh[:, kc * T + seg * 512: kc * T + seg * 512 + 512],
                        start=(kc == 0), stop=(kc == NCH - 1))
                btok = op("t", fbm, waits=[w(G.get(f"bev_{seg-2}"))]
                          if kc == 0 else [])

            def f1(s, pa=pa):
                return s.activation(gs1[:], pa[0:MNH, :], AF.Relu,
                                    bias=abtan[:], scale=-1.0)
            op("s", f1, waits=[w(atok), w(G.get(f"gstt_{seg-1}"))])

            def f2(s, pa=pa):
                return s.activation(gs2[:], pa[0:MNH, :], AF.Abs,
                                    bias=abta[:])
            G[f"gev_{seg}"] = op("s", f2)
            drain("s")
            op("s", lambda s: s.activation(gs2[:], gs2[:], AF.Exp, scale=-1.0))
            drain("s")
            etok = op("s", lambda s: s.activation(gs2[:], gs2[:], AF.Ln,
                                                  bias=1.0))

            def fg(v, seg=seg):
                return v.scalar_tensor_tensor(
                    gmg[0:MNH, seg * 512:(seg + 1) * 512], gs1[:], -1.0,
                    gs2[:], ALU.mult, ALU.subtract)
            G[f"gstt_{seg}"] = op("v", fg, waits=[w(etok)])

            def fbb(s, pbb=pbb, seg=seg):
                return s.activation(gmb[0:MNH, seg * 512:(seg + 1) * 512],
                                    pbb[0:MNH, :], AF.Sigmoid, bias=abtb[:])
            G[f"bev_{seg}"] = op("s", fbb, waits=[w(btok)])

        sc = op("v", lambda v: v.tensor_tensor_scan(
            gmg[0:MNH, :], gmg[0:MNH, :], gmg[0:MNH, :], gzero[:],
            ALU.add, ALU.bypass),
            waits=[w(G[f"bev_{NT-1}"]), w(G[f"gstt_{NT-1}"])])
        drain("v")
        op("v", lambda v: v.tensor_copy(gsm[0:MNH, 0:NCKT],
                                        gmg[0:MNH, 0:T:128]))
        sc2 = op("v", lambda v: v.tensor_copy(gsm[0:MNH, NCKT:NCKT + 1],
                                              gmg[0:MNH, T - 1:T]))
        op("y", lambda y: y.dma_start(out=gbounce.rearrange("c h -> h c"),
                                      in_=gsm[:],
                                      allow_slow_non_contiguous=True
                                      ).then_inc(gse.h, 16),
           waits=[w(sc2)], sem=gse, inc=16)
        gtok = (gse, gse.n)
        op("y", lambda y: y.dma_start(
            out=gstart_bc[:].rearrange("p (c h) -> p c h", h=MNH),
            in_=gbounce[0:NCKT, :].unsqueeze(0)
            .broadcast_to([128, NCKT, MNH])
        ).then_inc(g2se.h, 16), waits=[w(gtok)], sem=g2se, inc=16)
        op("y", lambda y: y.dma_start(
            out=gnext_bc[:].rearrange("p (c h) -> p c h", h=MNH),
            in_=gbounce[1:NCKT + 1, :].unsqueeze(0)
            .broadcast_to([128, NCKT, MNH])
        ).then_inc(g2se.h, 16), sem=g2se, inc=16)
        gbc_tok = (g2se, g2se.n)

        tr_last = None
        for c2 in range(NCKT):
            def ftr(t, c2=c2):
                return t.transpose(ps[4][:, c2 * 16:c2 * 16 + 8],
                                   gmg[0:MNH, c2 * 128:(c2 + 1) * 128],
                                   identtf[0:MNH, 0:MNH])
            op("t", ftr, waits=[w(sc2), w(G.get("qh_0")), w(G.get("qh_1"))]
               if c2 == 0 else [])

            def ftr2(t, c2=c2):
                return t.transpose(ps[4][:, c2 * 16 + 8:c2 * 16 + 16],
                                   gmb[0:MNH, c2 * 128:(c2 + 1) * 128],
                                   identtf[0:MNH, 0:MNH])
            tr_last = op("t", ftr2)
        gbtm_tok = op("v", lambda v: v.tensor_copy(gbtm[:, 0:NCKT * 16],
                                                   ps[4][:, 0:NCKT * 16]),
                      waits=[w(tr_last)])

        gbtmG = gbtm[:, 0:NCKT * 16].rearrange("p (c k) -> p c k", k=16)[:, :, 0:8]
        gbtmB = gbtm[:, 0:NCKT * 16].rearrange("p (c k) -> p c k", k=16)[:, :, 8:16]
        gsv = gstart_bc[:].rearrange("p (c h) -> p c h", h=MNH)
        gnv = gnext_bc[:].rearrange("p (c h) -> p c h", h=MNH)
        ghv = ghall[:].rearrange("p (c h) -> p c h", h=MNH)
        wcv = wcall[:].rearrange("p (c h) -> p c h", h=MNH)

        gh_tok = op("v", lambda v: v.tensor_tensor(ghv, gbtmG, gsv,
                                                   ALU.subtract),
                    waits=[w(gbtm_tok), w(gbc_tok)])
        drain("v")
        ac_tok = op("s", lambda s: s.activation(acall[:], ghall[:], AF.Exp),
                    waits=[w(gh_tok)])
        w1_tok = op("v", lambda v: v.tensor_tensor(wcv, gnv, gbtmG,
                                                   ALU.subtract))
        w2_tok = op("s", lambda s: s.activation(wcall[:], wcall[:], AF.Exp),
                    waits=[w(w1_tok)])
        w3_tok = op("v", lambda v: v.tensor_tensor(wcv, wcv, gbtmB, ALU.mult),
                    waits=[w(w2_tok)])
        d1_tok = op("v", lambda v: v.tensor_tensor(decbig[:], gnext_bc[:],
                                                   gstart_bc[:],
                                                   ALU.subtract))
        d2_tok = op("s", lambda s: s.activation(decbig[:], decbig[:], AF.Exp),
                    waits=[w(d1_tok)])
        at_tok = op("t", lambda t: t.transpose(ps[5][0:NCKT * MNH, 0:128],
                                               acall[:], identtf[:]),
                    waits=[w(ac_tok)])
        av_tok = op("v", lambda v: v.tensor_copy(atrp[0:NCKT * MNH, 0:128],
                                                 ps[5][0:NCKT * MNH, 0:128]),
                    waits=[w(at_tok)])
        op("y", lambda y: y.dma_start(out=abounce,
                                      in_=atrp[0:NCKT * MNH, 0:128]).then_inc(
            aese.h, 16), waits=[w(av_tok)], sem=aese, inc=16)
        arow_tok = (aese, aese.n)

        def emit_mv(c, src, m1lhs, state_bf, with_lambda, state_tok):
            d1 = op("v", lambda v: v.tensor_tensor(
                v3(paft[:]), v3(src[:]), bcv(acall, c), ALU.mult),
                waits=[w(G.get("paft_rd")), w(ac_tok)])
            tp_last = None
            for h in range(MNH):
                def ftp(t, h=h):
                    return t.transpose(
                        ps[4 + h // 4][:, (h % 4) * 128:(h % 4) * 128 + 128],
                        src[:, h * 128:(h + 1) * 128], identtf[:])
                tp_last = op("t", ftp, waits=[w(d1), w(G.get("pfm_ev"))]
                             if h == 0 else [])
            G["tp_last"] = tp_last
            ta = None
            for h in range(MNH):
                def fta(t, h=h):
                    return t.transpose(pbt[:, h * 128:(h + 1) * 128],
                                       paft[:, h * 128:(h + 1) * 128],
                                       identtb[:])
                ta = op("t", fta, waits=[w(G.get("pafm_ev")),
                                         w(G.get("pbt_ev"))]
                        if h == 0 else [])
            G["paft_rd"] = ta
            ev1 = op("s", lambda s: s.activation(pfm[:, 0:512], ps[4][:],
                                                 AF.Copy),
                     waits=[w(tp_last)])
            ev2 = op("s", lambda s: s.activation(pfm[:, 512:1024], ps[5][:],
                                                 AF.Copy))
            G["pfm_ev"] = ev2
            ev3 = op("s", lambda s: s.activation(pafm[:], pbt[:], AF.Copy),
                     waits=[w(ta)])
            G["pafm_ev"] = ev3
            G["pbt_ev"] = ev3
            m1g = [None, None]
            for h in range(MNH):
                def fm1(t, h=h):
                    return t.matmul(
                        ps[h // 4][:, (h % 4) * 128:(h % 4) * 128 + 128],
                        m1lhs[:, h * 128:(h + 1) * 128],
                        pfm[:, h * 128:(h + 1) * 128],
                        start=(h % 4 == 0), stop=(h % 4 == 3))
                w_ = []
                if h == 0:
                    w_ = [w(ev2), w(G.get("E1_0"))]
                if h == 4:
                    w_ = [w(G.get("E1_1"))]
                m1 = op("t", fm1, waits=w_)
                m1g[h // 4] = m1
            e10 = op("v", lambda v: v.tensor_tensor(stall[:, 0:512], ps[0][:],
                                                    maskb[:, 0:512],
                                                    ALU.mult),
                     waits=[w(m1g[0])])
            G["E1_0"] = e10
            e11 = op("v", lambda v: v.tensor_tensor(stall[:, 512:1024],
                                                    ps[1][:],
                                                    maskb[:, 512:1024],
                                                    ALU.mult),
                     waits=[w(m1g[1])])
            G["E1_1"] = e11
            mqg = [None, None]
            for h in range(MNH):
                g_ = h // 4
                bq = ps[2 + g_]

                def fm2(t, h=h, bq=bq):
                    return t.matmul(bq[:, (h % 4) * 128:(h % 4) * 128 + 128],
                                    stall[:, h * 128:(h + 1) * 128],
                                    ktm[:, h * 128:(h + 1) * 128],
                                    start=(h % 4 == 0), stop=False)
                w_ = [w(e10 if g_ == 0 else e11)]
                if h % 4 == 0:
                    w_.append(w(G.get(f"qev_{g_}")))
                op("t", fm2, waits=w_)

                def fm3(t, h=h, bq=bq):
                    return t.matmul(bq[:, (h % 4) * 128:(h % 4) * 128 + 128],
                                    pafm[:, h * 128:(h + 1) * 128],
                                    state_bf[:, h * 128:(h + 1) * 128],
                                    start=False,
                                    stop=(not with_lambda and h % 4 == 3))
                mm = op("t", fm3, waits=[w(ev3)] + (
                    [w(state_tok)] if h == 0 else []))
                if with_lambda:
                    def fm4(t, h=h, bq=bq):
                        return t.matmul(
                            bq[:, (h % 4) * 128:(h % 4) * 128 + 128],
                            pfm[:, h * 128:(h + 1) * 128],
                            lamd[:, h * 128:(h + 1) * 128],
                            start=False, stop=(h % 4 == 3))
                    mm = op("t", fm4)
                mqg[g_] = mm
            q0 = op("s", lambda s: s.activation(qst[:, 0:512], ps[2][:],
                                                AF.Copy),
                    waits=[w(mqg[0]), w(G.get("qst_rd"))])
            G["qev_0"] = q0
            q1 = op("s", lambda s: s.activation(qst[:, 512:1024], ps[3][:],
                                                AF.Copy), waits=[w(mqg[1])])
            G["qev_1"] = q1

        for c in range(NCKT if NCK_RUN is None else NCK_RUN):
            prev_pe = (mse, mse.n)
            prev_ve = (vse, vse.n)
            ldw = [w(prev_pe), w(prev_ve)]
            if c == 0:
                ldw += [w(G.get("sp_0")), w(G.get("sp_1"))]
            op("y", lambda y, c=c: y.dma_start(
                out=kfm[:],
                in_=kspill[c // 4, :, :, (c % 4) * 128:(c % 4) * 128 + 128]
            ).then_inc(kse.h, 16), waits=ldw, sem=kse, inc=16)
            ktok = (kse, kse.n)
            op("y", lambda y, c=c: y.dma_start(
                out=vfm[:],
                in_=vspill[c // 4, :, :, (c % 4) * 128:(c % 4) * 128 + 128]
            ).then_inc(vlse.h, 16), sem=vlse, inc=16)
            vtok = (vlse, vlse.n)
            op("y", lambda y, c=c: y.dma_start(
                out=paft[:],
                in_=qspill[c // 4, :, :, (c % 4) * 128:(c % 4) * 128 + 128]
            ).then_inc(qlse.h, 16), sem=qlse, inc=16)
            qtok = (qlse, qlse.n)
            op("y", lambda y, c=c: y.dma_start(
                out=arow1[:],
                in_=abounce[c * MNH:(c + 1) * MNH, :]
                .rearrange("r t -> (r t)").unsqueeze(0)
            ).then_inc(ar2se.h, 16), waits=[w(arow_tok)] if c == 0 else [],
                sem=ar2se, inc=16)
            artok = (ar2se, ar2se.n)

            for src_tok, srcb, dst in [(ktok, kfm, ktm), (vtok, vfm, vtm),
                                       (qtok, paft, bqt)]:
                ttok = None
                for h in range(MNH):
                    def ftr(t, srcb=srcb, h=h):
                        return t.transpose(pbt[:, h * 128:(h + 1) * 128],
                                           srcb[:, h * 128:(h + 1) * 128],
                                           identtb[:])
                    w_ = []
                    if h == 0:
                        w_ = [w(src_tok), w(G.get("pbt_ev")),
                              w(G.get("pafm_ev"))]
                    ttok = op("t", ftr, waits=w_)

                def fev(s, dst=dst):
                    return s.activation(dst[:], pbt[:], AF.Copy)
                G["pbt_ev"] = op("s", fev, waits=[w(ttok)])
                if srcb is paft:
                    G["paft_rd"] = ttok
            drain("s")
            k2_tok = op("s", lambda s: s.activation(k2tm[:], ktm[:],
                                                    AF.Square))

            if STAGE < 2:
                continue
            ghtok = op("v", lambda v, c=c: v.tensor_scalar(
                ghatb[:], gmg[0:MNH, c * 128:(c + 1) * 128],
                gsm[0:MNH, c:c + 1], None, ALU.subtract),
                waits=[w(prev_pe)])
            msk_evs = []
            for g_ in range(2):
                bnk = ps[g_]
                ptok = None
                for hh4 in range(4):
                    h = 4 * g_ + hh4

                    def fsel(t, h=h, bnk=bnk, hh4=hh4):
                        return t.matmul(
                            bnk[:, hh4 * 128:hh4 * 128 + 128],
                            selt[0:MNH, h * 128:(h + 1) * 128],
                            ghatb[:], start=(hh4 == 0), stop=False)
                    w_ = [w(ghtok)]
                    if g_ == 0 and hh4 == 0:
                        w_ += [w(G.get("akk_ev")), w(G.get("E1_1"))]
                    op("t", fsel, waits=w_)

                    def fpen(t, hh4=hh4, bnk=bnk):
                        return t.matmul(bnk[:, hh4 * 128:hh4 * 128 + 128],
                                        penlt[:], penrt[:], start=False,
                                        stop=(hh4 == 3))
                    ptok = op("t", fpen)
                for hh4 in range(4):
                    h = 4 * g_ + hh4
                    if MS < 2:
                        def fdc(v, h=h, bnk=bnk, hh4=hh4):
                            return v.tensor_copy(
                                junk[:, h * 128:(h + 1) * 128],
                                bnk[:, hh4 * 128:hh4 * 128 + 128])
                        msk_evs.append(op("v", fdc, waits=[w(ptok)]))
                        continue

                    def fd(v, h=h, bnk=bnk, hh4=hh4, c=c):
                        return v.tensor_scalar(
                            junk[:, h * 128:(h + 1) * 128],
                            bnk[:, hh4 * 128:hh4 * 128 + 128],
                            ghall[:, c * MNH + h:c * MNH + h + 1], None,
                            ALU.subtract)
                    dtok = op("v", fd, waits=[w(ptok), w(gbc_tok)])
                    if MS < 3:
                        msk_evs.append(dtok)
                        continue

                    def fex(s_, h=h):
                        return s_.activation(stall[:, h * 128:(h + 1) * 128],
                                             junk[:, h * 128:(h + 1) * 128],
                                             AF.Exp)
                    extok = op("s", fex, waits=[w(dtok)])
                    if MS < 4:
                        msk_evs.append(extok)
                        continue

                    def fmk(v, h=h, c=c):
                        return v.tensor_scalar(
                            maskb[:, h * 128:(h + 1) * 128],
                            stall[:, h * 128:(h + 1) * 128],
                            gbtm[:, c * 16 + 8 + h:c * 16 + 9 + h], None,
                            ALU.mult)
                    msk_evs.append(op("v", fmk, waits=[w(extok)]))
            G["E1_0"] = msk_evs[-1]
            G["E1_1"] = msk_evs[-1]

            if STAGE < 3:
                continue
            dgg = [None, None]
            for h in range(MNH):
                bq = ps[2 + h // 4]

                def fdg(t, h=h, bq=bq):
                    return t.matmul(bq[:, (h % 4) * 128:(h % 4) * 128 + 128],
                                    maskb[:, h * 128:(h + 1) * 128],
                                    k2tm[:, h * 128:(h + 1) * 128],
                                    start=(h % 4 == 0), stop=False)
                w_ = [w(msk_evs[h]), w(k2_tok)]
                if h == 0:
                    w_ += [w(G.get("qev_1")), w(lr8_tok), w(artok),
                           w(G.get("dst_ev"))]
                op("t", fdg, waits=w_)

                def flm(t, h=h, bq=bq):
                    return t.matmul(bq[:, (h % 4) * 128:(h % 4) * 128 + 128],
                                    ones_row_f[:],
                                    lr8[0:1, h * 128:(h + 1) * 128],
                                    start=False, stop=False)
                op("t", flm)

                def fam(t, h=h, bq=bq):
                    return t.matmul(
                        bq[:, (h % 4) * 128:(h % 4) * 128 + 128],
                        arow1[0:1, h * 128:(h + 1) * 128],
                        dstate[0:1, h * 128:(h + 1) * 128],
                        start=False, stop=(h % 4 == 3))
                dgg[h // 4] = op("t", fam)
            op("s", lambda s: s.activation(rnm[:, 0:512], ps[2][:], AF.Ln),
               waits=[w(dgg[0]), w(G.get("rnm_rd"))])
            op("s", lambda s: s.activation(rnm[:, 512:1024], ps[3][:], AF.Ln),
               waits=[w(dgg[1])])
            drain("s")
            op("s", lambda s: s.activation(rnm[:, 0:512], rnm[:, 0:512],
                                           AF.Exp, scale=-1.0))
            rc1 = op("s", lambda s: s.activation(rnm[:, 512:1024],
                                                 rnm[:, 512:1024],
                                                 AF.Exp, scale=-1.0))
            rc0 = rc1
            x0 = op("v", lambda v: v.tensor_tensor(xst[:], bqt[:], rnm[:],
                                                   ALU.mult),
                    waits=[w(rc1), w(G.get("pbt_ev"))])
            drain("v")
            G["qev_0"] = rc0
            G["qev_1"] = rc1
            if STAGE < 4:
                continue
            state_tok = G.get("akkb_ev")
            emit_mv(c, xst, kfm, akkb, True, state_tok)
            r0 = op("v", lambda v: v.tensor_tensor(rst[:], bqt[:], qst[:],
                                                   ALU.subtract),
                    waits=[w(G["qev_0"]), w(G["qev_1"])])
            drain("v")
            op("v", lambda v: v.tensor_copy(pst[:], rst[:]))
            for h in range(MNH):
                def fd0(v, h=h):
                    return v.scalar_tensor_tensor(
                        junk[:, h * 128:(h + 1) * 128],
                        rst[:, h * 128:(h + 1) * 128], 1.0,
                        rst[:, h * 128:(h + 1) * 128], ALU.mult, ALU.mult,
                        accum_out=row(R_D)[:, h:h + 1])
                op("v", fd0)
            nd0 = op("v", lambda v: v.memset(row(R_ND), 1.0))
            G["qst_rd"] = nd0
            drain("v")

            for it in range(NITER):
                emit_mv(c, pst, kfm, akkb, True, None)
                d6 = None
                for h in range(MNH):
                    def fd6(v, h=h):
                        return v.scalar_tensor_tensor(
                            junk[:, h * 128:(h + 1) * 128],
                            pst[:, h * 128:(h + 1) * 128], 1.0,
                            qst[:, h * 128:(h + 1) * 128], ALU.mult, ALU.mult,
                            accum_out=row(R_PQ)[:, h:h + 1])
                    d6 = op("v", fd6, waits=[w(G[f"qev_{h // 4}"])]
                            if h % 4 == 0 else [])
                drain("v")
                op("v", lambda v: v.tensor_scalar(row(R_S1), row(R_D), 1e-10,
                                                  None, ALU.is_ge))
                drain("v")
                op("v", lambda v: v.tensor_tensor(row(R_ND), row(R_ND),
                                                  row(R_S1), ALU.mult))
                op("v", lambda v: v.tensor_tensor(row(R_S2), row(R_PQ),
                                                  row(R_PQ), ALU.mult))
                drain("v")
                op("v", lambda v: v.tensor_scalar(row(R_S2), row(R_S2), 1e-20,
                                                  None, ALU.is_ge))
                drain("v")
                op("v", lambda v: v.tensor_tensor(row(R_ND), row(R_ND),
                                                  row(R_S2), ALU.mult))
                drain("v")
                op("v", lambda v: v.tensor_scalar(row(R_S1), row(R_ND), -1.0,
                                                  1.0, ALU.mult, ALU.add))
                op("v", lambda v: v.tensor_tensor(row(R_S3), row(R_PQ),
                                                  row(R_ND), ALU.mult))
                drain("v")
                op("v", lambda v: v.tensor_tensor(row(R_S3), row(R_S3),
                                                  row(R_S1), ALU.add))
                drain("v")
                s3tok = op("v", lambda v: v.tensor_copy(row(R_S1),
                                                        row(R_S3)))
                ltok = op("s", lambda s: s.activation(row(R_S3), row(R_S1),
                                                      AF.Ln),
                          waits=[w(s3tok)])
                drain("s")
                rtok = op("s", lambda s: s.activation(row(R_S3), row(R_S3),
                                                      AF.Exp, scale=-1.0))
                op("v", lambda v: v.tensor_tensor(row(R_AL), row(R_D),
                                                  row(R_S3), ALU.mult),
                   waits=[w(rtok)])
                drain("v")
                op("v", lambda v: v.tensor_tensor(row(R_AL), row(R_AL),
                                                  row(R_ND), ALU.mult))
                drain("v")
                op("v", lambda v: v.tensor_scalar(row(R_NA), row(R_AL), -1.0,
                                                  None, ALU.mult))
                drain("v")
                for h in range(MNH):
                    def fd7(v, h=h):
                        return v.scalar_tensor_tensor(
                            xst[:, h * 128:(h + 1) * 128],
                            pst[:, h * 128:(h + 1) * 128],
                            row(R_AL)[:, h:h + 1],
                            xst[:, h * 128:(h + 1) * 128], ALU.mult, ALU.add)
                    op("v", fd7)
                d8 = None
                for h in range(MNH):
                    def fd8(v, h=h):
                        return v.scalar_tensor_tensor(
                            rst[:, h * 128:(h + 1) * 128],
                            qst[:, h * 128:(h + 1) * 128],
                            row(R_NA)[:, h:h + 1],
                            rst[:, h * 128:(h + 1) * 128], ALU.mult, ALU.add)
                    d8 = op("v", fd8)
                G["qst_rd"] = d8
                drain("v")
                for h in range(MNH):
                    def fd9(v, h=h):
                        return v.scalar_tensor_tensor(
                            junk[:, h * 128:(h + 1) * 128],
                            rst[:, h * 128:(h + 1) * 128], 1.0,
                            rst[:, h * 128:(h + 1) * 128], ALU.mult, ALU.mult,
                            accum_out=row(R_DN)[:, h:h + 1])
                    op("v", fd9)
                drain("v")
                op("v", lambda v: v.tensor_scalar(row(R_S2), row(R_ND), -1.0,
                                                  1.0, ALU.mult, ALU.add))
                op("v", lambda v: v.tensor_tensor(row(R_S3), row(R_D),
                                                  row(R_ND), ALU.mult))
                drain("v")
                op("v", lambda v: v.tensor_tensor(row(R_S3), row(R_S3),
                                                  row(R_S2), ALU.add))
                drain("v")
                s3tok = op("v", lambda v: v.tensor_copy(row(R_S1),
                                                        row(R_S3)))
                ltok = op("s", lambda s: s.activation(row(R_S3), row(R_S1),
                                                      AF.Ln),
                          waits=[w(s3tok)])
                drain("s")
                rtok = op("s", lambda s: s.activation(row(R_S3), row(R_S3),
                                                      AF.Exp, scale=-1.0))
                op("v", lambda v: v.tensor_tensor(row(R_BE), row(R_DN),
                                                  row(R_S3), ALU.mult),
                   waits=[w(rtok)])
                drain("v")
                op("v", lambda v: v.tensor_tensor(row(R_BE), row(R_BE),
                                                  row(R_ND), ALU.mult))
                drain("v")
                op("v", lambda v: v.tensor_tensor(row(R_BE), row(R_BE),
                                                  row(R_S2), ALU.add))
                drain("v")
                ndv = row(R_ND).unsqueeze(2).broadcast_to([128, MNH, 128])
                op("v", lambda v: v.tensor_tensor(v3(rnm[:]), v3(rst[:]), ndv,
                                                  ALU.mult))
                drain("v")
                for h in range(MNH):
                    def fd10(v, h=h):
                        return v.scalar_tensor_tensor(
                            pst[:, h * 128:(h + 1) * 128],
                            pst[:, h * 128:(h + 1) * 128],
                            row(R_BE)[:, h:h + 1],
                            rnm[:, h * 128:(h + 1) * 128], ALU.mult, ALU.add)
                    op("v", fd10, waits=[w(G["tp_last"])] if h == 0 else [])
                drain("v")
                op("v", lambda v: v.tensor_copy(row(R_D), row(R_DN)))

            if STAGE < 5:
                continue
            state_tok = G.get("akvb_ev")
            emit_mv(c, xst, vfm, akvb, False, state_tok)
            sst = None
            for h in range(MNH):
                def fss(v, h=h):
                    return v.scalar_tensor_tensor(
                        junk[:, h * 128:(h + 1) * 128],
                        qst[:, h * 128:(h + 1) * 128], 1.0,
                        qst[:, h * 128:(h + 1) * 128], ALU.mult, ALU.mult,
                        accum_out=row(R_S1)[:, h:h + 1])
                sst = op("v", fss, waits=[w(G[f"qev_{h // 4}"])]
                         if h % 4 == 0 else [])
            ln1 = op("s", lambda s: s.activation(row(R_S2), row(R_S1), AF.Ln,
                                                 bias=1e-6,
                                                 scale=1.0 / 128.0),
                     waits=[w(sst)])
            drain("s")
            ln2 = op("s", lambda s: s.activation(row(R_S2), row(R_S2), AF.Exp,
                                                 scale=-0.5))
            s2v = row(R_S2).unsqueeze(2).broadcast_to([128, MNH, 128])
            op("v", lambda v: v.tensor_tensor(v3(qst[:]), v3(qst[:]), s2v,
                                              ALU.mult),
               waits=[w(ln2)])
            drain("v")
            o2 = op("v", lambda v: v.tensor_tensor(obuf[:], qst[:], wnb[:],
                                                   ALU.mult),
                    waits=[w(G.get("obuf_rd"))])
            G["qst_rd"] = o2
            ottok = None
            for h in range(MNH):
                def fot(t, h=h):
                    return t.transpose(pbt[:, h * 128:(h + 1) * 128],
                                       obuf[:, h * 128:(h + 1) * 128],
                                       identtb[:])
                ottok = op("t", fot, waits=[w(o2), w(G.get("pafm_ev")),
                                            w(G.get("pbt_ev"))]
                           if h == 0 else [])
            G["obuf_rd"] = ottok
            oev = op("s", lambda s: s.activation(obuf2[:], pbt[:], AF.Copy),
                     waits=[w(ottok), w(G.get("oimg_dma"))])
            G["pbt_ev"] = oev
            op("y", lambda y, c=c: y.dma_start(
                out=oimg.rearrange("p (h t) -> p h t", t=T)
                [:, :, c * 128:(c + 1) * 128],
                in_=obuf2[:].rearrange("p (h j) -> p h j", j=128)
            ).then_inc(ose.h, 16), waits=[w(oev)], sem=ose, inc=16)
            G["oimg_dma"] = (ose, ose.n)

            if STAGE < 6:
                continue
            kw = op("v", lambda v, c=c: v.tensor_tensor(
                v3(kwall[:]), v3(ktm[:]), bcv(wcall, c), ALU.mult),
                waits=[w(w3_tok)])
            drain("v")
            eb = op("v", lambda v: v.tensor_tensor(ebuf[:], kwall[:], ktm[:],
                                                   ALU.mult))
            kkg = [None, None]
            for h in range(MNH):
                bk = ps[h // 4]

                def fkk(t, h=h, bk=bk):
                    return t.matmul(bk[:, (h % 4) * 128:(h % 4) * 128 + 128],
                                    kwall[:, h * 128:(h + 1) * 128],
                                    ktm[:, h * 128:(h + 1) * 128],
                                    start=(h % 4 == 0), stop=(h % 4 == 3))
                w_ = [w(kw)]
                if h == 0:
                    w_.append(w(G["E1_1"]))
                kkg[h // 4] = op("t", fkk, waits=w_)
            kvg = [None, None]
            for h in range(MNH):
                bk = ps[2 + h // 4]

                def fkv(t, h=h, bk=bk):
                    return t.matmul(bk[:, (h % 4) * 128:(h % 4) * 128 + 128],
                                    vtm[:, h * 128:(h + 1) * 128],
                                    kwall[:, h * 128:(h + 1) * 128],
                                    start=(h % 4 == 0), stop=(h % 4 == 3))
                w_ = [w(G["qev_1"])] if h == 0 else []
                kvg[h // 4] = op("t", fkv, waits=w_)
            ds0 = op("t", lambda t: t.matmul(ps[4][0:1, :], ones_col_b[:],
                                             ebuf[:, 0:512], start=True,
                                             stop=True),
                     waits=[w(eb), w(G["pfm_ev"])])
            ds1 = op("t", lambda t: t.matmul(ps[5][0:1, :], ones_col_b[:],
                                             ebuf[:, 512:1024], start=True,
                                             stop=True))
            ek = None
            for h in range(MNH):
                bk = ps[h // 4]

                def fek(v, h=h, bk=bk, c=c):
                    return v.scalar_tensor_tensor(
                        akk[:, h * 128:(h + 1) * 128],
                        akk[:, h * 128:(h + 1) * 128],
                        decbig[:, c * MNH + h:c * MNH + h + 1],
                        bk[:, (h % 4) * 128:(h % 4) * 128 + 128],
                        ALU.mult, ALU.add)
                ek = op("v", fek, waits=[w(kkg[h // 4]), w(d2_tok)]
                        if h % 4 == 0 else [])
            G["akk_ev"] = ek
            ev_ = None
            for h in range(MNH):
                bk = ps[2 + h // 4]

                def fev_(v, h=h, bk=bk, c=c):
                    return v.scalar_tensor_tensor(
                        akv[:, h * 128:(h + 1) * 128],
                        akv[:, h * 128:(h + 1) * 128],
                        decbig[:, c * MNH + h:c * MNH + h + 1],
                        bk[:, (h % 4) * 128:(h % 4) * 128 + 128],
                        ALU.mult, ALU.add)
                ev_ = op("v", fev_, waits=[w(kvg[h // 4])]
                         if h % 4 == 0 else [])
            G["akv_ev"] = ev_
            ed = None
            for h in range(MNH):
                bk = ps[4 + h // 4]

                def fed(v, h=h, bk=bk, c=c):
                    return v.scalar_tensor_tensor(
                        dstate[0:1, h * 128:(h + 1) * 128],
                        dstate[0:1, h * 128:(h + 1) * 128],
                        decbig[0:1, c * MNH + h:c * MNH + h + 1],
                        bk[0:1, (h % 4) * 128:(h % 4) * 128 + 128],
                        ALU.mult, ALU.add)
                ed = op("v", fed, waits=[w(ds0 if h < 4 else ds1)]
                        if h % 4 == 0 else [])
            G["dst_ev"] = ed
            drain("v")
            akkbev = op("s", lambda s: s.activation(akkb[:], akk[:], AF.Copy),
                        waits=[w(G["akk_ev"]), w(ed)])
            G["akkb_ev"] = akkbev
            akvbev = op("s", lambda s: s.activation(akvb[:], akv[:], AF.Copy),
                        waits=[w(G["akv_ev"])])
            G["akvb_ev"] = akvbev
            G["pfm_ev"] = akvbev

        op("y", lambda y: y.wait_ge(ose.h, ose.n))

        def replay(key, eng):
            for fn, waits, sem_, incby in seq[key]:
                for tok in waits:
                    s_, v_ = tok
                    eng.wait_ge(s_.h, v_)
                inst = fn(eng)
                if sem_ is not None:
                    inst.then_inc(sem_.h, incby)

        with nc.Block() as b:
            @b.tensor
            def _(t):
                replay("t", t)

            @b.vector
            def _(v):
                replay("v", v)

            @b.scalar
            def _(s):
                replay("s", s)

            @b.sync
            def _(sy):
                replay("y", sy)

            @b.gpsimd
            def _(g):
                replay("g", g)

    return nc


def prep_l1attn_host(inp, hh):
    PIh = PI
    Wq = np.asarray(inp["L1_attn_Wq"])[PIh][:, hh * 1024:(hh + 1) * 1024]
    Wk = np.asarray(inp["L1_attn_Wk"])[PIh][:, hh * 1024:(hh + 1) * 1024]
    Wv = np.asarray(inp["L1_attn_Wv"])[PIh][:, hh * 1024:(hh + 1) * 1024]
    Wa = np.asarray(inp["L1_attn_Wa"])[PIh][:, hh * 8:hh * 8 + 8]
    Wb = np.asarray(inp["L1_attn_Wb"])[PIh][:, hh * 8:hh * 8 + 8]
    ba = np.asarray(inp["L1_attn_ba"])[hh * 8:hh * 8 + 8]
    bb = np.asarray(inp["L1_attn_bb"])[hh * 8:hh * 8 + 8]
    lam = np.asarray(inp["L1_attn_lambda_params"]).reshape(16, 128)
    onww = np.tile(np.asarray(inp["L1_attn_o_norm_weight"]).reshape(1, 128),
                   (1, MNH)).reshape(-1)
    sel = np.zeros((MNH, MNH * 128), np.float32)
    for h in range(MNH):
        sel[h, h * 128:(h + 1) * 128] = 1.0
    penl_np = np.triu(np.ones((128, 128), np.float32), 1).astype(BF)
    penr_np = (-1e9 * np.eye(128, dtype=np.float32)).astype(BF)
    return {
        "wq": img_kn(np.ascontiguousarray(Wq)),
        "wk": img_kn(np.ascontiguousarray(Wk)),
        "wv": img_kn(np.ascontiguousarray(Wv)),
        "waba": np.ascontiguousarray(
            Wa.reshape(16, 128, 8).transpose(1, 0, 2).reshape(128, 128)),
        "wabb": np.ascontiguousarray(
            Wb.reshape(16, 128, 8).transpose(1, 0, 2).reshape(128, 128)),
        "abain": np.ascontiguousarray(ba.reshape(MNH, 1)),
        "abbin": np.ascontiguousarray(bb.reshape(MNH, 1)),
        "lamin": np.ascontiguousarray(lam[hh * 8:hh * 8 + 8].T),
        "onw": np.ascontiguousarray(onww.reshape(1, MNH * 128)),
        "sel8": sel,
        "penl": penl_np,
        "penr": penr_np,
        "identb": np.eye(128, dtype=np.float32).astype(BF),
        "identf": np.eye(128, dtype=np.float32),
    }


def _np_silu(x):
    return x / (1.0 + np.exp(-x))


def l1attn_ref_np(xt, inp, n_iter=30):
    xt = np.asarray(xt, np.float64)
    T_ = xt.shape[0]
    Wq = np.asarray(inp["L1_attn_Wq"], np.float64)
    Wk = np.asarray(inp["L1_attn_Wk"], np.float64)
    Wv = np.asarray(inp["L1_attn_Wv"], np.float64)
    Wa = np.asarray(inp["L1_attn_Wa"], np.float64)
    Wb = np.asarray(inp["L1_attn_Wb"], np.float64)
    ba = np.asarray(inp["L1_attn_ba"], np.float64)
    bb = np.asarray(inp["L1_attn_bb"], np.float64)
    lamp = np.asarray(inp["L1_attn_lambda_params"], np.float64)
    onww = np.asarray(inp["L1_attn_o_norm_weight"], np.float64)

    q = _np_silu(xt @ Wq).reshape(T_, 16, 128)
    kk = _np_silu(xt @ Wk).reshape(T_, 16, 128)
    vv = _np_silu(xt @ Wv).reshape(T_, 16, 128)

    def l2n(u):
        n = np.sqrt((u * u).sum(-1, keepdims=True))
        return u / np.maximum(n, 1e-12)
    q = l2n(q)
    kk = l2n(kk)
    beta = 1.0 / (1.0 + np.exp(-(xt @ Wb + bb)))
    ga = xt @ Wa + ba
    g = -np.log1p(np.exp(-np.abs(ga))) + np.minimum(ga, 0.0)
    lamb = np.log1p(np.exp(-np.abs(lamp))) + np.maximum(lamp, 0.0) + 0.25
    lamb = lamb.reshape(16, 128)

    def cg(A, b):
        def mv(u):
            return np.einsum('hij,hj->hi', A, u) + lamb * u
        diag = np.einsum('hii->hi', A)
        x = b / (diag + lamb + 1e-8)
        r = b - mv(x)
        p = r.copy()
        d = (r * r).sum(-1)
        done = np.zeros(16, bool)
        for _ in range(n_iter):
            done = done | (d < 1e-10)
            qv = mv(p)
            pq = (p * qv).sum(-1)
            done = done | (np.abs(pq) < 1e-10)
            alpha = np.where(done, 0.0, d / np.where(done, 1.0, pq))[:, None]
            xn = x + alpha * p
            rn = r - alpha * qv
            dn = (rn * rn).sum(-1)
            bet = np.where(done, 0.0, dn / np.where(done, 1.0, d))[:, None]
            m = done[:, None]
            p = np.where(m, p, rn + bet * p)
            x = np.where(m, x, xn)
            r = np.where(m, r, rn)
            d = np.where(done[:], d, dn)
        return x

    Hkk = np.zeros((16, 128, 128))
    Hkv = np.zeros((16, 128, 128))
    o = np.zeros((T_, 16, 128))
    for t in range(T_):
        dec = np.exp(g[t])[:, None, None]
        kb = kk[t] * beta[t][:, None]
        Hkk = Hkk * dec + kb[:, :, None] * kk[t][:, None, :]
        Hkv = Hkv * dec + kb[:, :, None] * vv[t][:, None, :]
        qs = cg(Hkk, q[t])
        o[t] = np.einsum('hij,hj->hi', Hkv, qs)
    rms = np.sqrt((o * o).mean(-1, keepdims=True) + 1e-6)
    o = o / rms * onww[None, None, :]
    return o.reshape(T_, 2048)


def np_layernorm(x, eps=1e-5):
    x = np.asarray(x, np.float64)
    m = x.mean(axis=-1, keepdims=True)
    v = x.var(axis=-1, keepdims=True)
    return (x - m) / np.sqrt(v + eps)



def build_wo(T=1024):
    nc = bass.Bass("TRN2", target_bir_lowering=False)
    oin = nc.dram_tensor("oin", [128, NCH * T], BF16, kind="ExternalInput").ap()
    xres = nc.dram_tensor("xres", [128, NCH * T], F32, kind="ExternalInput").ap()
    wo = nc.dram_tensor("wo", [NCH, 128, NCH * 128], F32,
                        kind="ExternalInput").ap()
    xout = nc.dram_tensor("xout", [128, NCH * T], F32,
                          kind="ExternalOutput").ap()
    NTH = T // 512

    with ExitStack() as ctx:
        k = K(nc, ctx)
        ob = k.sbuf("ob", [128, NCH * T], BF16)
        wobuf = k.sbuf("wobuf", [128, NCH * NCH * 128], BF16)
        xrt = [k.sbuf(f"xrt{i}", [128, 512], F32) for i in range(2)]
        ot = [k.sbuf(f"ot{i}", [128, 512], F32) for i in range(2)]
        pb = [k.psum(f"pb{i}") for i in range(2)]
        dse = k.sem()
        wge = k.sem()
        xse4 = [k.sem(), k.sem()]
        ose4 = [k.sem(), k.sem()]
        mse = k.sem()
        vse = k.sem()

        with nc.Block() as b:
            @b.sync
            def _(sync):
                sync.dma_start(out=ob[:], in_=oin).then_inc(dse.h, 16)
                dse.n += 16

            @b.gpsimd
            def _(g):
                for i in range(NCH):
                    g.dma_start(out=wobuf[:, i * NCH * 128:(i + 1) * NCH * 128],
                                in_=wo[i]).then_inc(wge.h, 16)
                    wge.n += 16

        with nc.Block() as b:
            nseg = NCH * NTH

            @b.sync
            def _(sync):
                for i in range(nseg):
                    dc, th = i // NTH, i % NTH
                    bi = i % 2
                    if i >= 2:
                        sync.wait_ge(vse.h, i - 2 + 1)
                        sync.wait_ge(xse4[bi].h, 16 * (i // 2))
                    sync.dma_start(
                        out=xrt[bi][:],
                        in_=xres[:, dc * T + th * 512: dc * T + th * 512 + 512]
                    ).then_inc(xse4[bi].h, 16)
                    xse4[bi].n += 16
                    if i >= 1:
                        bj = (i - 1) % 2
                        sync.wait_ge(vse.h, i)
                        if i >= 3:
                            sync.wait_ge(ose4[bj].h, 16 * ((i - 1) // 2))
                        dcp, thp = (i - 1) // NTH, (i - 1) % NTH
                        sync.dma_start(
                            out=xout[:, dcp * T + thp * 512:
                                     dcp * T + thp * 512 + 512],
                            in_=ot[bj][:]).then_inc(ose4[bj].h, 16)
                        ose4[bj].n += 16
                bj = (nseg - 1) % 2
                sync.wait_ge(vse.h, nseg)
                sync.dma_start(
                    out=xout[:, (NCH - 1) * T + (NTH - 1) * 512:
                             (NCH - 1) * T + (NTH - 1) * 512 + 512],
                    in_=ot[bj][:]).then_inc(ose4[bj].h, 16)
                ose4[bj].n += 16
                for x in ose4:
                    sync.wait_ge(x.h, x.n)

            @b.tensor
            def _(t):
                t.wait_ge(dse.h, 16)
                t.wait_ge(wge.h, 16 * NCH)
                for i in range(nseg):
                    dc, th = i // NTH, i % NTH
                    if i >= 2:
                        t.wait_ge(vse.h, i - 2 + 1)
                    for kc in range(NCH):
                        inst = t.matmul(
                            pb[i % 2][:],
                            wobuf[:, dc * NCH * 128 + kc * 128:
                                  dc * NCH * 128 + kc * 128 + 128],
                            ob[:, kc * T + th * 512: kc * T + th * 512 + 512],
                            start=(kc == 0), stop=(kc == NCH - 1))
                    mse.inc(inst)

            @b.vector
            def _(v):
                for i in range(nseg):
                    bi = i % 2
                    v.wait_ge(mse.h, i + 1)
                    v.wait_ge(xse4[bi].h, 16 * (i // 2 + 1))
                    if i >= 2:
                        v.wait_ge(ose4[bi].h, 16 * (i // 2))
                    inst = v.tensor_tensor(ot[bi][:], xrt[bi][:], pb[i % 2][:],
                                           ALU.add)
                    vse.inc(inst)
    return nc
def prep_l0attn_host(inp):
    Win = np.asarray(inp["L0_attn_W_in"], np.float32)
    Wout = np.asarray(inp["L0_attn_W_out"], np.float32)
    WinT = np.ascontiguousarray(Win.T[PI])
    WinT = np.concatenate([WinT, np.zeros((D, 66 * 128 - 8384), np.float32)], axis=1)
    WoutT = np.ascontiguousarray(Wout.T[:, PI])
    cw = np.asarray(inp["L0_attn_conv_w"], np.float32)
    cb = np.asarray(inp["L0_attn_conv_b"], np.float32)
    cwt = np.zeros((128, 33 * 4), np.float32)
    cbt = np.zeros((128, 33), np.float32)
    for xc in range(33):
        for j in range(4):
            cwt[:, xc * 4 + j] = cw[128 * xc:128 * xc + 128, j]
        cbt[:, xc] = cb[128 * xc:128 * xc + 128]
    Dv = np.asarray(inp["L0_attn_D"], np.float32)
    drep = np.zeros((128, 32), np.float32)
    for j in range(32):
        drep[0:64, j] = Dv[2 * j]
        drep[64:128, j] = Dv[2 * j + 1]
    sel = np.zeros((64, 32 * 128), np.float32)
    for r in range(64):
        sel[r, (r % 32) * 128:(r % 32) * 128 + 128] = 1.0
    penl = np.zeros((128, 128), np.float32)
    for kk in range(128):
        penl[kk, kk + 1:] = 1.0
    penr = np.zeros((128, 4 * 128), np.float32)
    for q in range(4):
        penr[:, q * 128:(q + 1) * 128] = -1e9 * np.eye(128, dtype=np.float32)
    return {
        "win": img_kn(WinT),
        "wout": img_kn(WoutT),
        "cwt": cwt, "cbt": cbt,
        "dtb": np.asarray(inp["L0_attn_dt_bias"], np.float32).reshape(64, 1),
        "alog": np.asarray(inp["L0_attn_A_log"], np.float32).reshape(64, 1),
        "drep": drep,
        "nwt": col_tile(np.asarray(inp["L0_attn_norm_w"], np.float32)),
        "sel32": sel,
        "penl": penl.astype(BF), "penr": penr.astype(BF),
        "identb": np.eye(128, dtype=np.float32).astype(BF),
        "identf": np.eye(128, dtype=np.float32),
    }


def l0attn_ref_np(x, inp):
    x = np.asarray(x, np.float64)
    Tn = x.shape[0]
    m = x.mean(-1, keepdims=True)
    v = x.var(-1, keepdims=True)
    xx = (x - m) / np.sqrt(v + 1e-5)
    Win = np.asarray(inp["L0_attn_W_in"], np.float64)
    zxbcdt = xx @ Win.T
    z = zxbcdt[:, :4096]
    xBC = zxbcdt[:, 4096:8320]
    dtraw = zxbcdt[:, 8320:]
    cw = np.asarray(inp["L0_attn_conv_w"], np.float64)
    cb = np.asarray(inp["L0_attn_conv_b"], np.float64)
    xp = np.concatenate([np.zeros((3, 4224)), xBC], 0)
    conv = cb + sum(xp[kk:kk + Tn, :] * cw[:, kk] for kk in range(4))
    xBC2 = conv / (1 + np.exp(-conv))
    xh = xBC2[:, :4096].reshape(Tn, 64, 64)
    Bm = xBC2[:, 4096:4160]
    Cm = xBC2[:, 4160:4224]
    dtbias = np.asarray(inp["L0_attn_dt_bias"], np.float64)
    dt = np.logaddexp(0, dtraw + dtbias)
    A = -np.exp(np.asarray(inp["L0_attn_A_log"], np.float64))
    dA = np.exp(dt * A)
    h = np.zeros((64, 64, 64))
    ys = np.zeros((Tn, 64, 64))
    for t in range(Tn):
        h = h * dA[t][:, None, None] + dt[t][:, None, None] * \
            (xh[t][:, :, None] * Bm[t][None, None, :])
        ys[t] = h @ Cm[t]
    Dv = np.asarray(inp["L0_attn_D"], np.float64)
    y = ys + Dv[None, :, None] * xh
    y = y.reshape(Tn, 4096)
    y = y * (z / (1 + np.exp(-z)))
    y = y / np.sqrt((y ** 2).mean(-1, keepdims=True) + 1e-5) * \
        np.asarray(inp["L0_attn_norm_w"], np.float64)
    return x + y @ np.asarray(inp["L0_attn_W_out"], np.float64).T


def _img_half(img, th, T=2048):
    return np.ascontiguousarray(
        img.reshape(128, NCH, T)[:, :, th * 1024:(th + 1) * 1024]
        .reshape(128, NCH * 1024))


def _img_merge(h0, h1, T=2048):
    out = np.empty((128, NCH * T), h0.dtype)
    v = out.reshape(128, NCH, T)
    v[:, :, 0:1024] = h0.reshape(128, NCH, 1024)
    v[:, :, 1024:2048] = h1.reshape(128, NCH, 1024)
    return out


def kernel(**inputs):
    x0 = np.asarray(inputs["L0_attn_x"], np.float32)
    B = x0.shape[0]

    hp_dec = prep_dec1_host(inputs)
    res = run_bass_kernel_spmd(build_dec1(), dec1_in_maps(hp_dec),
                               core_ids=list(range(8)))
    dg, du, dd = assemble_dec1(res.results)

    hp0 = prep_l0attn_host(inputs)
    in_maps = []
    for b in range(B):
        m = {"xin": img_stream(np.ascontiguousarray(x0[b].T[PI]))}
        m.update(hp0)
        in_maps.append(m)
    res = run_bass_kernel_spmd(build_l0attn(2048, 512), in_maps,
                               core_ids=list(range(B)))
    x1img = [np.asarray(res.results[b]["xout"]).reshape(128, NCH * 2048)
             for b in range(B)]

    hpf = prep_l0ffn_weights(inputs)
    in_maps = []
    for c in range(2 * B):
        b, th = c // 2, c % 2
        m = {"xin": _img_half(x1img[b], th)}
        m.update(hpf)
        in_maps.append(m)
    res = run_bass_kernel_spmd(build_l0ffn(1024), in_maps,
                               core_ids=list(range(2 * B)))
    x2img = [_img_merge(np.asarray(res.results[2 * b]["xout"]).reshape(128, -1),
                        np.asarray(res.results[2 * b + 1]["xout"]).reshape(128, -1))
             for b in range(B)]

    hpm = [prep_l1attn_host(inputs, hh) for hh in range(2)]
    in_maps = []
    for c in range(2 * B):
        b, hh = c // 2, c % 2
        m = {"xin": x2img[b]}
        m.update(hpm[hh])
        in_maps.append(m)
    res = run_bass_kernel_spmd(build_l1attn(2048), in_maps,
                               core_ids=list(range(2 * B)))
    oimgs = [np.asarray(res.results[c]["oimg"]).reshape(128, 8, 2048)
             for c in range(2 * B)]

    Wo = np.asarray(inputs["L1_attn_Wo"], np.float32)
    wo_img = img_kn(np.ascontiguousarray(Wo[:, PI]))
    in_maps = []
    for c in range(2 * B):
        b, th = c // 2, c % 2
        ofull = np.concatenate([oimgs[2 * b], oimgs[2 * b + 1]], axis=1)
        oin = np.ascontiguousarray(
            ofull[:, :, th * 1024:(th + 1) * 1024].reshape(128, NCH * 1024))
        in_maps.append({"oin": oin, "xres": _img_half(x2img[b], th),
                        "wo": wo_img})
    res = run_bass_kernel_spmd(build_wo(1024), in_maps,
                               core_ids=list(range(2 * B)))
    x3h = [np.asarray(res.results[c]["xout"]).reshape(128, -1)
           for c in range(2 * B)]

    in_maps = [{"xin": x3h[c], "dg": dg, "du": du, "dd": dd}
               for c in range(2 * B)]
    res = run_bass_kernel_spmd(build_l1ffn(1024), in_maps,
                               core_ids=list(range(2 * B)))

    out = np.empty((B, 2048, 2048), np.float32)
    for c in range(2 * B):
        b, th = c // 2, c % 2
        ximg = np.asarray(res.results[c]["xout"]).reshape(128, -1)
        out[b, th * 1024:(th + 1) * 1024, :] = unimg_stream(ximg, D)[PI_INV].T
    return out



# version 02
# speedup vs baseline: 1.0567x; 1.0567x over previous
import numpy as np
import ml_dtypes
from contextlib import ExitStack

import concourse.bass as bass
import concourse.mybir as mybir
from concourse.bass_utils import run_bass_kernel_spmd

F32 = mybir.dt.float32
F32R = mybir.dt.float32r
BF16 = mybir.dt.bfloat16
U8 = mybir.dt.uint8
I32 = mybir.dt.int32
I8 = mybir.dt.int8
AF = mybir.ActivationFunctionType
ALU = mybir.AluOpType

D = 2048
NCH = D // 128
PI = np.r_[0:D:2, 1:D:2]
PI_INV = np.argsort(PI)
MLP1 = 16384
PIM = np.r_[0:MLP1:2, 1:MLP1:2]
BF = ml_dtypes.bfloat16

LN_EPS = 1e-5



def img_kn(WT):
    Kk, Nn = WT.shape
    assert Kk % 128 == 0 and Nn % 128 == 0
    return np.ascontiguousarray(
        WT.reshape(Kk // 128, 128, Nn // 128, 128).transpose(2, 1, 0, 3)
        .reshape(Nn // 128, 128, (Kk // 128) * 128))


def img_stream(XT):
    Kk, T = XT.shape
    assert Kk % 128 == 0
    return np.ascontiguousarray(
        XT.reshape(Kk // 128, 128, T).transpose(1, 0, 2).reshape(128, (Kk // 128) * T))


def unimg_stream(img, Kk):
    P, CT = img.shape
    T = CT // (Kk // 128)
    return img.reshape(128, Kk // 128, T).transpose(1, 0, 2).reshape(Kk, T)


def col_tile(v):
    n = v.shape[0]
    assert n % 128 == 0
    return np.ascontiguousarray(v.reshape(n // 128, 128).T)



class Sem:
    def __init__(self, handle):
        self.h = handle
        self.n = 0

    def inc(self, inst, by=1):
        inst.then_inc(self.h, by)
        self.n += by
        return self.n


class K:
    def __init__(self, nc, ctx):
        self.nc = nc
        self.ctx = ctx
        self._id = 0

    def sem(self):
        self._id += 1
        return Sem(self.ctx.enter_context(self.nc.semaphore(f"s{self._id}")))

    def sbuf(self, name, shape, dtype):
        return self.ctx.enter_context(self.nc.sbuf_tensor(name, shape, dtype))

    def psum(self, name):
        return self.ctx.enter_context(self.nc.psum_tensor(name, [128, 512], F32))



HID0 = 5632
NH0 = HID0 // 128


def build_l0ffn(T=1024):
    nc = bass.Bass("TRN2", target_bir_lowering=False)
    NT = T // 512

    xin = nc.dram_tensor("xin", [128, NCH * T], F32, kind="ExternalInput").ap()
    wg = nc.dram_tensor("wg", [NH0, 128, NCH * 128], U8, kind="ExternalInput").ap()
    wu = nc.dram_tensor("wu", [NH0, 128, NCH * 128], U8, kind="ExternalInput").ap()
    wd = nc.dram_tensor("wd", [NH0, 128, NCH * 128], U8, kind="ExternalInput").ap()
    sg = nc.dram_tensor("sg", [128, NH0], F32, kind="ExternalInput").ap()
    su = nc.dram_tensor("su", [128, NH0], F32, kind="ExternalInput").ap()
    sd = nc.dram_tensor("sd", [128, NCH], F32, kind="ExternalInput").ap()
    xout = nc.dram_tensor("xout", [128, NCH * T], F32, kind="ExternalOutput").ap()

    with ExitStack() as ctx:
        k = K(nc, ctx)
        xs = k.sbuf("xs", [128, NCH * T], F32)
        xh = k.sbuf("xh", [128, NCH * 512], BF16)
        inter = k.sbuf("inter", [128, NH0 * 512], BF16)
        sq = inter
        gsc = [k.sbuf(f"gsc{i}", [128, 512], BF16) for i in range(2)]
        usc = [k.sbuf(f"usc{i}", [128, 512], BF16) for i in range(2)]
        t1g = [k.sbuf(f"t1g{i}", [128, 512], F32) for i in range(2)]
        t1u = [k.sbuf(f"t1u{i}", [128, 512], F32) for i in range(2)]
        dscbig = k.sbuf("dscbig", [128, 8 * 512], F32)
        rows = k.sbuf("rows", [1, 512 * 4], F32)
        sxb = k.sbuf("sxb", [128, 512], F32)
        sib = k.sbuf("sib", [128, 512], F32)
        sgt = k.sbuf("sgt", [128, NH0], F32)
        sut = k.sbuf("sut", [128, NH0], F32)
        sdt = k.sbuf("sdt", [128, NCH], F32)
        ones_col_f = k.sbuf("ones_col_f", [128, 1], F32)
        ones_col_b = k.sbuf("ones_col_b", [128, 1], BF16)
        ones_row_f = k.sbuf("ones_row_f", [1, 128], F32)
        cpool = k.sbuf("cpool", [128, 2], F32)
        nc.const_aps.register(F32, LN_EPS, cpool[:, 0:1])
        wgt = [k.sbuf(f"wgt{i}", [128, NCH * 128], BF16) for i in range(2)]
        wut = [k.sbuf(f"wut{i}", [128, NCH * 128], BF16) for i in range(2)]
        wdt = [k.sbuf(f"wdt{i}", [128, NCH * 128], BF16) for i in range(2)]

        pb = [k.psum(f"pb{i}") for i in range(8)]

        dse = k.sem()
        wse = [k.sem(), k.sem()]
        sse = k.sem()
        mse = k.sem()
        ese = k.sem()
        ase = k.sem()
        tse = k.sem()
        xse = k.sem()

        def row(i):
            return rows[0:1, i * 512:(i + 1) * 512]

        with nc.Block() as b:
            @b.sync
            def _(sync):
                for c in range(NCH):
                    sync.dma_start(out=xs[:, c * T:(c + 1) * T],
                                   in_=xin[:, c * T:(c + 1) * T]).then_inc(dse.h, 16)
                sync.dma_start(out=sgt[:], in_=sg).then_inc(dse.h, 16)
                sync.dma_start(out=sut[:], in_=su).then_inc(dse.h, 16)
                sync.dma_start(out=sdt[:], in_=sd).then_inc(dse.h, 16)
                dse.n += 16 * (NCH + 3)
                sync.wait_ge(dse.h, dse.n)

            @b.vector
            def _(v):
                v.memset(cpool[:, 0:1], LN_EPS)
                v.memset(ones_col_f[:], 1.0)
                v.memset(ones_col_b[:], 1.0)
                v.memset(ones_row_f[:], 1.0)

        for tt in range(NT):
            t0 = tt * 512
            ROW_MU, ROW_RR, ROW_MS, ROW_M2 = 0, 1, 2, 3
            ROW_VAR, ROW_LNV, ROW_SX, ROW_SI = 2, 3, 2, 2

            with nc.Block() as b:
                s_base = sse.n

                @b.scalar
                def _(s):
                    for c in range(NCH):
                        inst = s.activation(sq[:, c * 512:(c + 1) * 512],
                                            xs[:, c * T + t0: c * T + t0 + 512],
                                            AF.Square)
                        sse.inc(inst)

                @b.tensor
                def _(t):
                    for c in range(NCH):
                        inst = t.matmul(pb[0][0:1, :], ones_col_f[:],
                                        xs[:, c * T + t0: c * T + t0 + 512],
                                        start=(c == 0), stop=(c == NCH - 1))
                    for c in range(NCH):
                        t.wait_ge(sse.h, s_base + c + 1)
                        inst = t.matmul(pb[1][0:1, :], ones_col_b[:],
                                        sq[:, c * 512:(c + 1) * 512],
                                        start=(c == 0), stop=(c == NCH - 1))

            with nc.Block() as b:
                @b.vector
                def _(v):
                    inv_n = 1.0 / D
                    v.tensor_scalar(row(ROW_MU), pb[0][0:1, :], inv_n, None, ALU.mult)
                    v.tensor_scalar(row(ROW_MS), pb[1][0:1, :], inv_n, None, ALU.mult)
                    v.drain()
                    v.tensor_tensor(row(ROW_M2), row(ROW_MU), row(ROW_MU), ALU.mult)
                    v.drain()
                    inst = v.tensor_tensor(row(ROW_VAR), row(ROW_MS), row(ROW_M2), ALU.subtract)
                    xse.inc(inst)

                @b.scalar
                def _(s):
                    s.wait_ge(xse.h, xse.n)
                    s.activation(row(ROW_LNV), row(ROW_VAR), AF.Ln, bias=LN_EPS)
                    s.drain()
                    s.activation(row(ROW_RR), row(ROW_LNV), AF.Exp, scale=-0.5)

            with nc.Block() as b:
                @b.tensor
                def _(t):
                    t.matmul(pb[2][:], ones_row_f[:], row(ROW_MU), start=True, stop=True)
                    t.matmul(pb[3][:], ones_row_f[:], row(ROW_RR), start=True, stop=True)

            with nc.Block() as b:
                @b.vector
                def _(v):
                    for c in range(NCH):
                        v.tensor_tensor(xh[:, c * 512:(c + 1) * 512],
                                        xs[:, c * T + t0: c * T + t0 + 512],
                                        pb[2][:], ALU.subtract)
                    v.drain()
                    for c in range(NCH):
                        v.tensor_tensor(xh[:, c * 512:(c + 1) * 512],
                                        xh[:, c * 512:(c + 1) * 512],
                                        pb[3][:], ALU.mult)

            with nc.Block() as b:
                @b.tensor
                def _(t):
                    for c in range(NCH):
                        t.matmul(pb[4][0:1, :], ones_col_b[:],
                                 xh[:, c * 512:(c + 1) * 512],
                                 start=(c == 0), stop=(c == NCH - 1))
            with nc.Block() as b:
                @b.vector
                def _(v):
                    v.tensor_copy(row(ROW_SX), pb[4][0:1, :])
            with nc.Block() as b:
                @b.tensor
                def _(t):
                    t.matmul(pb[5][:], ones_row_f[:], row(ROW_SX), start=True, stop=True)
            with nc.Block() as b:
                @b.vector
                def _(v):
                    v.tensor_copy(sxb[:], pb[5][:])

            with nc.Block() as b:
                w_base = [wse[0].n, wse[1].n]
                m_base, e_base, a_base, t_base = mse.n, ese.n, ase.n, tse.n

                @b.gpsimd
                def _(g):
                    for c in range(NH0):
                        bi = c % 2
                        if c >= 2:
                            g.wait_ge(mse.h, m_base + (c - 2) + 1)
                            g.wait_ge(wse[bi].h, w_base[bi] + 32 * (c // 2))
                        g.dma_start(out=wgt[bi][:], in_=wg[c]).then_inc(wse[bi].h, 16)
                        g.dma_start(out=wut[bi][:], in_=wu[c]).then_inc(wse[bi].h, 16)
                        wse[bi].n += 32

                @b.tensor
                def _(t):
                    for c in range(NH0):
                        bi = c % 2
                        t.wait_ge(wse[bi].h, w_base[bi] + 32 * (c // 2 + 1))
                        if c >= 2:
                            t.wait_ge(ese.h, e_base + (c - 2) + 1)
                        gb, ub = pb[bi], pb[2 + bi]
                        for kc in range(NCH):
                            t.matmul(gb[:], wgt[bi][:, kc * 128:(kc + 1) * 128],
                                     xh[:, kc * 512:(kc + 1) * 512],
                                     start=(kc == 0), stop=(kc == NCH - 1))
                        for kc in range(NCH):
                            inst = t.matmul(ub[:], wut[bi][:, kc * 128:(kc + 1) * 128],
                                            xh[:, kc * 512:(kc + 1) * 512],
                                            start=(kc == 0), stop=(kc == NCH - 1))
                        mse.inc(inst)

                @b.vector
                def _(v):
                    def mult_inter(c):
                        bi = c % 2
                        v.wait_ge(ase.h, a_base + c + 1)
                        inst = v.tensor_tensor(inter[:, c * 512:(c + 1) * 512],
                                               gsc[bi][:], usc[bi][:], ALU.mult)
                        tse.inc(inst)

                    for c in range(NH0):
                        bi = c % 2
                        gb, ub = pb[bi], pb[2 + bi]
                        v.wait_ge(mse.h, m_base + c + 1)
                        if c >= 2:
                            v.wait_ge(ase.h, a_base + (c - 2) + 1)
                        v.scalar_tensor_tensor(t1g[bi][:], sxb[:], -128.0, gb[:],
                                               ALU.mult, ALU.add)
                        inst = v.scalar_tensor_tensor(t1u[bi][:], sxb[:], -128.0, ub[:],
                                                      ALU.mult, ALU.add)
                        ese.inc(inst)
                        if c >= 1:
                            mult_inter(c - 1)
                    mult_inter(NH0 - 1)

                @b.scalar
                def _(s):
                    for c in range(NH0):
                        bi = c % 2
                        s.wait_ge(ese.h, e_base + c + 1)
                        if c >= 2:
                            s.wait_ge(tse.h, t_base + (c - 2) + 1)
                        s.activation(gsc[bi][:], t1g[bi][:], AF.Silu, scale=sgt[:, c:c + 1])
                        inst = s.activation(usc[bi][:], t1u[bi][:], AF.Copy,
                                            scale=sut[:, c:c + 1])
                        ase.inc(inst)

            with nc.Block() as b:
                @b.tensor
                def _(t):
                    for c in range(NH0):
                        t.matmul(pb[6][0:1, :], ones_col_b[:],
                                 inter[:, c * 512:(c + 1) * 512],
                                 start=(c == 0), stop=(c == NH0 - 1))
            with nc.Block() as b:
                @b.vector
                def _(v):
                    v.tensor_copy(row(ROW_SI), pb[6][0:1, :])
            with nc.Block() as b:
                @b.tensor
                def _(t):
                    t.matmul(pb[7][:], ones_row_f[:], row(ROW_SI), start=True, stop=True)
            with nc.Block() as b:
                @b.vector
                def _(v):
                    v.tensor_copy(sib[:], pb[7][:])

            for dh in range(2):
                with nc.Block() as b:
                    w_base = [wse[0].n, wse[1].n]
                    m_base = mse.n

                    @b.gpsimd
                    def _(g):
                        for mc in range(NH0):
                            bi = mc % 2
                            if mc >= 2:
                                g.wait_ge(mse.h, m_base + (mc - 2) + 1)
                                g.wait_ge(wse[bi].h, w_base[bi] + 16 * (mc // 2))
                            g.dma_start(out=wdt[bi][:], in_=wd[mc]).then_inc(wse[bi].h, 16)
                            wse[bi].n += 16

                    @b.tensor
                    def _(t):
                        for mc in range(NH0):
                            bi = mc % 2
                            t.wait_ge(wse[bi].h, w_base[bi] + 16 * (mc // 2 + 1))
                            for dc in range(8):
                                inst = t.matmul(
                                    pb[dc][:],
                                    wdt[bi][:, (dh * 8 + dc) * 128:(dh * 8 + dc + 1) * 128],
                                    inter[:, mc * 512:(mc + 1) * 512],
                                    start=(mc == 0), stop=(mc == NH0 - 1))
                            mse.inc(inst)

                with nc.Block() as b:
                    @b.vector
                    def _(v):
                        for dc in range(8):
                            v.scalar_tensor_tensor(dscbig[:, dc * 512:(dc + 1) * 512],
                                                   sib[:], -128.0, pb[dc][:],
                                                   ALU.mult, ALU.add)
                        v.drain()
                        for dc in range(8):
                            ci = dh * 8 + dc
                            v.scalar_tensor_tensor(
                                xs[:, ci * T + t0: ci * T + t0 + 512],
                                dscbig[:, dc * 512:(dc + 1) * 512], sdt[:, ci:ci + 1],
                                xs[:, ci * T + t0: ci * T + t0 + 512],
                                ALU.mult, ALU.add)

        with nc.Block() as b:
            @b.sync
            def _(sync):
                for c in range(NCH):
                    sync.dma_start(out=xout[:, c * T:(c + 1) * T],
                                   in_=xs[:, c * T:(c + 1) * T]).then_inc(dse.h, 16)
                dse.n += 16 * NCH
                sync.wait_ge(dse.h, dse.n)
    return nc



def bview(packed_i32):
    a = packed_i32.view(np.uint8).reshape(*packed_i32.shape, 4)[..., 0]
    return np.ascontiguousarray(a)


def prep_l0ffn_weights(inp):
    wgq = bview(np.asarray(inp["L0_ffn_wq_gate"]))
    wuq = bview(np.asarray(inp["L0_ffn_wq_up"]))
    wdq = bview(np.asarray(inp["L0_ffn_wq_down"]))
    wgT = np.ascontiguousarray(wgq.T[PI])
    wuT = np.ascontiguousarray(wuq.T[PI])
    wdT = np.ascontiguousarray(wdq.T)
    wdT = np.ascontiguousarray(wdT[:, PI])
    return {
        "wg": img_kn(wgT).astype(np.uint8),
        "wu": img_kn(wuT).astype(np.uint8),
        "wd": np.ascontiguousarray(wdT.reshape(NH0, 128, NCH * 128)),
        "sg": col_tile(np.asarray(inp["L0_ffn_s_gate"], np.float32)),
        "su": col_tile(np.asarray(inp["L0_ffn_s_up"], np.float32)),
        "sd": col_tile(np.asarray(inp["L0_ffn_s_down"], np.float32)[PI]),
    }


def run_l0ffn(x_pi_T_shards, wmaps, T=1024):
    nc = build_l0ffn(T)
    in_maps = []
    for s in x_pi_T_shards:
        m = {"xin": img_stream(s)}
        m.update(wmaps)
        in_maps.append(m)
    res = run_bass_kernel_spmd(nc, in_maps, core_ids=list(range(len(in_maps))))
    return [unimg_stream(np.asarray(r["xout"]), D) for r in res.results]



NDEC = 16


def build_dec1():
    nc = bass.Bass("TRN2", target_bir_lowering=False)
    pg = nc.dram_tensor("pg", [NDEC, 128, 1024], U8, kind="ExternalInput").ap()
    pu = nc.dram_tensor("pu", [NDEC, 128, 1024], U8, kind="ExternalInput").ap()
    pd = nc.dram_tensor("pd", [NDEC // 2, 128, 2048], U8, kind="ExternalInput").ap()
    scg = nc.dram_tensor("scg", [NDEC, 8, 1024], F32, kind="ExternalInput").ap()
    scu = nc.dram_tensor("scu", [NDEC, 8, 1024], F32, kind="ExternalInput").ap()
    scd = nc.dram_tensor("scd", [NDEC // 2, 8, 2048], F32, kind="ExternalInput").ap()
    sel4 = nc.dram_tensor("sel4", [4, 128], F32, kind="ExternalInput").ap()
    sel8 = nc.dram_tensor("sel8", [8, 128], F32, kind="ExternalInput").ap()
    dg = nc.dram_tensor("dg", [NDEC, 128, 2048], BF16, kind="ExternalOutput").ap()
    du = nc.dram_tensor("du", [NDEC, 128, 2048], BF16, kind="ExternalOutput").ap()
    dd = nc.dram_tensor("dd", [NDEC, 128, 2048], BF16, kind="ExternalOutput").ap()

    with ExitStack() as ctx:
        k = K(nc, ctx)
        w32 = [k.sbuf(f"w32_{i}", [128, 2048], I32) for i in range(2)]
        jl = k.sbuf("jl", [128, 2048], I32)
        jh = k.sbuf("jh", [128, 2048], I32)
        sl = k.sbuf("sl", [128, 2048], I32)
        sh = k.sbuf("sh", [128, 2048], I32)
        p1 = [k.sbuf(f"p1_{i}", [128, 2048], F32) for i in range(2)]
        m2 = [k.sbuf(f"m2_{i}", [128, 2048], F32) for i in range(2)]
        m4 = [k.sbuf(f"m4_{i}", [128, 2048], F32) for i in range(2)]
        sgn = [k.sbuf(f"sgn_{i}", [128, 2048], F32) for i in range(2)]
        qq = [k.sbuf(f"qq_{i}", [128, 2048], F32) for i in range(2)]
        scr = k.sbuf("scr", [8, 2048], F32)
        outb = [k.sbuf(f"outb{i}", [128, 2048], BF16) for i in range(2)]
        sel4t = k.sbuf("sel4t", [4, 128], F32)
        sel8t = k.sbuf("sel8t", [8, 128], F32)
        cpool = k.sbuf("cpool", [128, 2], F32)
        nc.const_aps.register(F32, -4.0, cpool[:, 0:1])
        nc.const_aps.register(F32, -2.0, cpool[:, 1:2])
        pb = [k.psum(f"pb{i}") for i in range(4)]

        dse = k.sem()
        gse = k.sem()
        sse = k.sem()
        ose = k.sem()
        mse = k.sem()
        ise = k.sem()
        ase_ = k.sem()
        vse = k.sem()

        with nc.Block() as b:
            @b.sync
            def _(sync):
                sync.dma_start(out=sel4t[:], in_=sel4).then_inc(dse.h, 16)
                sync.dma_start(out=sel8t[:], in_=sel8).then_inc(dse.h, 16)
                dse.n += 32
                sync.wait_ge(dse.h, dse.n)

            @b.vector
            def _(v):
                v.memset(cpool[:, 0:1], -4.0)
                v.memset(cpool[:, 1:2], -2.0)

        def decode_block(src_ap, W, nsel, selt, sc_ap, dsts):
            with nc.Block() as b:
                NW = W // 512

                @b.gpsimd
                def _(g):
                    g.dma_start(out=w32[0][:, 0:W], in_=src_ap).then_inc(gse.h, 16)
                    gse.n += 16
                    g.wait_ge(gse.h, gse.n)

                @b.sync
                def _(sync):
                    sync.dma_start(out=scr[0:nsel, 0:W], in_=sc_ap).then_inc(sse.h, 16)
                    sse.n += 16
                    sync.wait_ge(sse.h, sse.n)

            with nc.Block() as b:
                @b.tensor
                def _(t):
                    for q in range(NW):
                        inst = t.matmul(pb[q][:, 0:512], selt,
                                        scr[0:nsel, q * 512:(q + 1) * 512],
                                        start=True, stop=True)
                    mse.inc(inst)

                @b.vector
                def _(v):
                    w = w32[0][:, 0:W]
                    v.tensor_scalar(jl[:, 0:W], w, 7, None, ALU.bitwise_and)
                    v.tensor_scalar(jh[:, 0:W], w, 4, 7, ALU.logical_shift_right,
                                    ALU.bitwise_and)
                    v.tensor_scalar(sl[:, 0:W], w, 3, 1, ALU.logical_shift_right,
                                    ALU.bitwise_and)
                    v.tensor_scalar(sh[:, 0:W], w, 7, 1, ALU.logical_shift_right,
                                    ALU.bitwise_and)
                    v.drain()
                    v.tensor_scalar(p1[0][:, 0:W], jl[:, 0:W], 0.5, None, ALU.mult)
                    v.tensor_scalar(m2[0][:, 0:W], jl[:, 0:W], 1.0, -2.0, ALU.mult, ALU.add)
                    v.tensor_scalar(m4[0][:, 0:W], jl[:, 0:W], 2.0, -8.0, ALU.mult, ALU.add)
                    v.tensor_scalar(p1[1][:, 0:W], jh[:, 0:W], 0.5, None, ALU.mult)
                    v.tensor_scalar(m2[1][:, 0:W], jh[:, 0:W], 1.0, -2.0, ALU.mult, ALU.add)
                    v.tensor_scalar(m4[1][:, 0:W], jh[:, 0:W], 2.0, -8.0, ALU.mult, ALU.add)
                    v.tensor_scalar(sgn[0][:, 0:W], sl[:, 0:W], -2.0, 1.0, ALU.mult, ALU.add)
                    v.tensor_scalar(sgn[1][:, 0:W], sh[:, 0:W], -2.0, 1.0, ALU.mult, ALU.add)
                    v.drain()
                    v.tensor_tensor(p1[0][:, 0:W], p1[0][:, 0:W], m2[0][:, 0:W], ALU.max)
                    v.tensor_tensor(p1[1][:, 0:W], p1[1][:, 0:W], m2[1][:, 0:W], ALU.max)
                    v.drain()
                    v.tensor_tensor(p1[0][:, 0:W], p1[0][:, 0:W], m4[0][:, 0:W], ALU.max)
                    v.tensor_tensor(p1[1][:, 0:W], p1[1][:, 0:W], m4[1][:, 0:W], ALU.max)
                    v.wait_ge(mse.h, mse.n)
                    for lh in range(2):
                        for q in range(NW):
                            v.tensor_tensor(qq[lh][:, q * 512:(q + 1) * 512],
                                            sgn[lh][:, q * 512:(q + 1) * 512],
                                            pb[q][:, 0:512], ALU.mult)
                    v.drain()
                    if len(dsts) == 1:
                        v.tensor_tensor(outb[0][:, 0:W], p1[0][:, 0:W],
                                        qq[0][:, 0:W], ALU.mult)
                        inst = v.tensor_tensor(outb[0][:, W:2 * W], p1[1][:, 0:W],
                                               qq[1][:, 0:W], ALU.mult)
                    else:
                        v.tensor_tensor(outb[0][:, 0:W], p1[0][:, 0:W],
                                        qq[0][:, 0:W], ALU.mult)
                        inst = v.tensor_tensor(outb[1][:, 0:W], p1[1][:, 0:W],
                                               qq[1][:, 0:W], ALU.mult)
                    vse.inc(inst)

            with nc.Block() as b:
                @b.sync
                def _(sync):
                    if len(dsts) == 1:
                        sync.dma_start(out=dsts[0], in_=outb[0][:]).then_inc(ose.h, 16)
                        ose.n += 16
                    else:
                        sync.dma_start(out=dsts[0], in_=outb[0][:, 0:W]).then_inc(ose.h, 16)
                        sync.dma_start(out=dsts[1], in_=outb[1][:, 0:W]).then_inc(ose.h, 16)
                        ose.n += 32
                    sync.wait_ge(ose.h, ose.n)

        for i in range(NDEC):
            decode_block(pg[i], 1024, 8, sel8t[:], scg[i], [dg[i]])
        for i in range(NDEC):
            decode_block(pu[i], 1024, 8, sel8t[:], scu[i], [du[i]])
        for i in range(NDEC // 2):
            decode_block(pd[i], 2048, 8, sel8t[:], scd[i], [dd[i], dd[8 + i]])
    return nc


def prep_dec1_host(inp):
    gP = bview(np.asarray(inp["L1_ffn_gate_W_packed"]))
    uP = bview(np.asarray(inp["L1_ffn_up_W_packed"]))
    dP = bview(np.asarray(inp["L1_ffn_down_W_packed"]))
    gS = np.asarray(inp["L1_ffn_gate_scales"], np.float32)
    uS = np.asarray(inp["L1_ffn_up_scales"], np.float32)
    dS = np.asarray(inp["L1_ffn_down_scales"], np.float32)

    gPT = np.ascontiguousarray(gP.T[:, PIM])
    uPT = np.ascontiguousarray(uP.T[:, PIM])
    dPT = np.ascontiguousarray(dP.T[:, PI])
    gST = np.ascontiguousarray(gS.T[:, PIM])
    uST = np.ascontiguousarray(uS.T[:, PIM])
    dST = np.ascontiguousarray(dS.T[:, PI])

    gPB = img_kn(gPT)
    uPB = img_kn(uPT)
    dPB = np.ascontiguousarray(dPT.reshape(64, 128, D))

    def g_st(ST):
        out = np.empty((128, 8, 8, 128), np.float32)
        for c in range(128):
            for a in range(8):
                out[c, :, a, :] = ST[8 * a:8 * a + 8, 128 * c:128 * c + 128]
        return out.reshape(128, 8, 1024)
    gSS = g_st(gST)
    uSS = g_st(uST)
    dSS = np.ascontiguousarray(dST.reshape(64, 8, D))

    sel4 = np.zeros((4, 128), np.float32)
    for p in range(128):
        sel4[p // 32, p] = 1.0
    sel8 = np.zeros((8, 128), np.float32)
    for p in range(128):
        sel8[p // 16, p] = 1.0
    return dict(gPB=gPB, uPB=uPB, dPB=dPB, gSS=gSS, uSS=uSS, dSS=dSS,
                sel4=sel4, sel8=sel8)


def dec1_in_maps(hp):
    maps = []
    for j in range(8):
        maps.append({
            "pg": hp["gPB"][16 * j:16 * j + 16],
            "pu": hp["uPB"][16 * j:16 * j + 16],
            "pd": hp["dPB"][8 * j:8 * j + 8],
            "scg": hp["gSS"][16 * j:16 * j + 16],
            "scu": hp["uSS"][16 * j:16 * j + 16],
            "scd": hp["dSS"][8 * j:8 * j + 8],
            "sel4": hp["sel4"], "sel8": hp["sel8"],
        })
    return maps


def assemble_dec1(results):
    dg = np.concatenate([np.asarray(r["dg"]) for r in results], axis=0)
    du = np.concatenate([np.asarray(r["du"]) for r in results], axis=0)
    dd_lo = np.concatenate([np.asarray(r["dd"])[0:8] for r in results], axis=0)
    dd_hi = np.concatenate([np.asarray(r["dd"])[8:16] for r in results], axis=0)
    dd = np.concatenate([dd_lo, dd_hi], axis=0)
    return dg, du, dd



NH1 = 128


def build_l1ffn(T=1024):
    nc = bass.Bass("TRN2", target_bir_lowering=False)
    NT = T // 512

    xin = nc.dram_tensor("xin", [128, NCH * T], F32, kind="ExternalInput").ap()
    dg = nc.dram_tensor("dg", [NH1, 128, NCH * 128], BF16, kind="ExternalInput").ap()
    du = nc.dram_tensor("du", [NH1, 128, NCH * 128], BF16, kind="ExternalInput").ap()
    dd = nc.dram_tensor("dd", [NH1, 128, NCH * 128], BF16, kind="ExternalInput").ap()
    xout = nc.dram_tensor("xout", [128, NCH * T], F32, kind="ExternalOutput").ap()

    with ExitStack() as ctx:
        k = K(nc, ctx)
        xh = k.sbuf("xh", [128, NCH * 512], BF16)
        inter = k.sbuf("inter", [128, NH1 * 512], BF16)
        sq = inter
        xtile = [k.sbuf(f"xtile{i}", [128, 512], F32) for i in range(4)]
        g2 = [k.sbuf(f"g2_{i}", [128, 512], F32) for i in range(2)]
        uu = [k.sbuf(f"uu_{i}", [128, 512], BF16) for i in range(2)]
        gl = [k.sbuf(f"gl_{i}", [128, 512], BF16) for i in range(2)]
        xres = [k.sbuf(f"xres{i}", [128, 512], F32) for i in range(4)]
        oref = [k.sbuf(f"oref{i}", [128, 512], F32) for i in range(4)]
        rows = k.sbuf("rows", [1, 512 * 4], F32)
        ones_col_f = k.sbuf("ones_col_f", [128, 1], F32)
        ones_col_b = k.sbuf("ones_col_b", [128, 1], BF16)
        ones_row_f = k.sbuf("ones_row_f", [1, 128], F32)
        cpool = k.sbuf("cpool", [128, 2], F32)
        nc.const_aps.register(F32, LN_EPS, cpool[:, 0:1])
        wgt = [k.sbuf(f"wgt{i}", [128, NCH * 128], BF16) for i in range(2)]
        wut = [k.sbuf(f"wut{i}", [128, NCH * 128], BF16) for i in range(2)]
        wdt = [k.sbuf(f"wdt{i}", [128, 8 * 128], BF16) for i in range(2)]

        pb = [k.psum(f"pb{i}") for i in range(8)]

        dse = k.sem()
        dse4 = [k.sem() for _ in range(4)]
        ose4 = [k.sem() for _ in range(4)]
        wse = [k.sem(), k.sem()]
        wse2 = [k.sem(), k.sem()]
        sse = k.sem()
        pse = k.sem()
        mse = k.sem()
        a1se = k.sem()
        v1se = k.sem()
        a2se = k.sem()
        v2se = k.sem()
        vse = k.sem()
        ose = k.sem()
        xse = k.sem()

        def row(i):
            return rows[0:1, i * 512:(i + 1) * 512]

        with nc.Block() as b:
            @b.vector
            def _(v):
                v.memset(cpool[:, 0:1], LN_EPS)
                v.memset(ones_col_f[:], 1.0)
                v.memset(ones_col_b[:], 1.0)
                v.memset(ones_row_f[:], 1.0)

        for tt in range(NT):
            t0 = tt * 512
            ROW_MU, ROW_RR, ROW_MS, ROW_M2 = 0, 1, 2, 3
            ROW_VAR, ROW_LNV = 2, 3

            with nc.Block() as b:
                d_base = [x.n for x in dse4]
                s_base, p_base = sse.n, pse.n

                @b.sync
                def _(sync):
                    for c in range(NCH):
                        bi = c % 4
                        if c >= 4:
                            sync.wait_ge(sse.h, s_base + (c - 4) + 1)
                            sync.wait_ge(pse.h, p_base + (c - 4) + 1)
                            sync.wait_ge(dse4[bi].h, d_base[bi] + 16 * (c // 4))
                        sync.dma_start(out=xtile[bi][:],
                                       in_=xin[:, c * T + t0: c * T + t0 + 512]
                                       ).then_inc(dse4[bi].h, 16)
                        dse4[bi].n += 16

                @b.scalar
                def _(s):
                    for c in range(NCH):
                        bi = c % 4
                        s.wait_ge(dse4[bi].h, d_base[bi] + 16 * (c // 4 + 1))
                        inst = s.activation(sq[:, c * 512:(c + 1) * 512],
                                            xtile[bi][:], AF.Square)
                        sse.inc(inst)

                @b.tensor
                def _(t):
                    s_base2 = s_base
                    for c in range(NCH):
                        bi = c % 4
                        t.wait_ge(dse4[bi].h, d_base[bi] + 16 * (c // 4 + 1))
                        inst = t.matmul(pb[0][0:1, :], ones_col_f[:], xtile[bi][:],
                                        start=(c == 0), stop=(c == NCH - 1))
                        pse.inc(inst)
                    for c in range(NCH):
                        t.wait_ge(sse.h, s_base2 + c + 1)
                        t.matmul(pb[1][0:1, :], ones_col_b[:],
                                 sq[:, c * 512:(c + 1) * 512],
                                 start=(c == 0), stop=(c == NCH - 1))

            with nc.Block() as b:
                @b.vector
                def _(v):
                    inv_n = 1.0 / D
                    v.tensor_scalar(row(ROW_MU), pb[0][0:1, :], inv_n, None, ALU.mult)
                    v.tensor_scalar(row(ROW_MS), pb[1][0:1, :], inv_n, None, ALU.mult)
                    v.drain()
                    v.tensor_tensor(row(ROW_M2), row(ROW_MU), row(ROW_MU), ALU.mult)
                    v.drain()
                    inst = v.tensor_tensor(row(ROW_VAR), row(ROW_MS), row(ROW_M2),
                                           ALU.subtract)
                    xse.inc(inst)

                @b.scalar
                def _(s):
                    s.wait_ge(xse.h, xse.n)
                    s.activation(row(ROW_LNV), row(ROW_VAR), AF.Ln, bias=LN_EPS)
                    s.drain()
                    s.activation(row(ROW_RR), row(ROW_LNV), AF.Exp, scale=-0.5)

            with nc.Block() as b:
                @b.tensor
                def _(t):
                    t.matmul(pb[2][:], ones_row_f[:], row(ROW_MU), start=True, stop=True)
                    t.matmul(pb[3][:], ones_row_f[:], row(ROW_RR), start=True, stop=True)

            with nc.Block() as b:
                d_base = [x.n for x in dse4]
                v_base = vse.n

                @b.sync
                def _(sync):
                    for c in range(NCH):
                        bi = c % 4
                        if c >= 4:
                            sync.wait_ge(vse.h, v_base + (c - 4) + 1)
                            sync.wait_ge(dse4[bi].h, d_base[bi] + 16 * (c // 4))
                        sync.dma_start(out=xtile[bi][:],
                                       in_=xin[:, c * T + t0: c * T + t0 + 512]
                                       ).then_inc(dse4[bi].h, 16)
                        dse4[bi].n += 16

                @b.vector
                def _(v):
                    for c in range(NCH):
                        bi = c % 4
                        v.wait_ge(dse4[bi].h, d_base[bi] + 16 * (c // 4 + 1))
                        inst = v.tensor_tensor(xh[:, c * 512:(c + 1) * 512],
                                               xtile[bi][:], pb[2][:], ALU.subtract)
                        vse.inc(inst)
                    v.drain()
                    for c in range(NCH):
                        v.tensor_tensor(xh[:, c * 512:(c + 1) * 512],
                                        xh[:, c * 512:(c + 1) * 512],
                                        pb[3][:], ALU.mult)

            with nc.Block() as b:
                w_base = [wse[0].n, wse[1].n]
                m_base, a1_base, v1_base, a2_base, v2_base = (
                    mse.n, a1se.n, v1se.n, a2se.n, v2se.n)

                @b.gpsimd
                def _(g):
                    for c in range(NH1):
                        bi = c % 2
                        if c >= 2:
                            g.wait_ge(mse.h, m_base + (c - 2) + 1)
                            g.wait_ge(wse[bi].h, w_base[bi] + 32 * (c // 2))
                        g.dma_start(out=wgt[bi][:], in_=dg[c]).then_inc(wse[bi].h, 16)
                        g.dma_start(out=wut[bi][:], in_=du[c]).then_inc(wse[bi].h, 16)
                        wse[bi].n += 32

                @b.tensor
                def _(t):
                    for c in range(NH1):
                        bi = c % 2
                        t.wait_ge(wse[bi].h, w_base[bi] + 32 * (c // 2 + 1))
                        if c >= 2:
                            t.wait_ge(v2se.h, v2_base + (c - 2) + 1)
                        gb, ub = pb[bi], pb[2 + bi]
                        for kc in range(NCH):
                            t.matmul(gb[:], wgt[bi][:, kc * 128:(kc + 1) * 128],
                                     xh[:, kc * 512:(kc + 1) * 512],
                                     start=(kc == 0), stop=(kc == NCH - 1))
                        for kc in range(NCH):
                            inst = t.matmul(ub[:], wut[bi][:, kc * 128:(kc + 1) * 128],
                                            xh[:, kc * 512:(kc + 1) * 512],
                                            start=(kc == 0), stop=(kc == NCH - 1))
                        mse.inc(inst)

                @b.scalar
                def _(s):
                    for c in range(NH1):
                        bi = c % 2
                        gb, ub = pb[bi], pb[2 + bi]
                        s.wait_ge(mse.h, m_base + c + 1)
                        if c >= 2:
                            s.wait_ge(v2se.h, v2_base + (c - 2) + 1)
                        inst = s.activation(g2[bi][:], gb[:], AF.Square)
                        a1se.inc(inst)
                        s.wait_ge(v1se.h, v1_base + c + 1)
                        s.drain()
                        s.activation(g2[bi][:], g2[bi][:], AF.Sigmoid)
                        inst = s.activation(uu[bi][:], ub[:], AF.Copy)
                        a2se.inc(inst)

                @b.vector
                def _(v):
                    for c in range(NH1):
                        bi = c % 2
                        gb = pb[bi]
                        v.wait_ge(a1se.h, a1_base + c + 1)
                        v.tensor_scalar(g2[bi][:], g2[bi][:], 0.07135775452626324,
                                        1.5957691216057308, ALU.mult, ALU.add)
                        v.drain()
                        inst = v.tensor_tensor(g2[bi][:], g2[bi][:], gb[:], ALU.mult)
                        v1se.inc(inst)
                        v.wait_ge(a2se.h, a2_base + c + 1)
                        v.tensor_tensor(gl[bi][:], g2[bi][:], gb[:], ALU.mult)
                        v.drain()
                        inst = v.tensor_tensor(inter[:, c * 512:(c + 1) * 512],
                                               gl[bi][:], uu[bi][:], ALU.mult)
                        v2se.inc(inst)

            for dh in range(2):
                with nc.Block() as b:
                    w2_base = [wse2[0].n, wse2[1].n]
                    m_base = mse.n

                    @b.gpsimd
                    def _(g):
                        for mc in range(NH1):
                            bi = mc % 2
                            if mc >= 2:
                                g.wait_ge(mse.h, m_base + (mc - 2) + 1)
                                g.wait_ge(wse2[bi].h, w2_base[bi] + 16 * (mc // 2))
                            g.dma_start(out=wdt[bi][:, 0:1024],
                                        in_=dd[mc][:, dh * 1024:(dh + 1) * 1024]
                                        ).then_inc(wse2[bi].h, 16)
                            wse2[bi].n += 16

                    @b.tensor
                    def _(t):
                        for mc in range(NH1):
                            bi = mc % 2
                            t.wait_ge(wse2[bi].h, w2_base[bi] + 16 * (mc // 2 + 1))
                            for dc in range(8):
                                inst = t.matmul(
                                    pb[dc][:],
                                    wdt[bi][:, dc * 128:(dc + 1) * 128],
                                    inter[:, mc * 512:(mc + 1) * 512],
                                    start=(mc == 0), stop=(mc == NH1 - 1))
                            mse.inc(inst)

                with nc.Block() as b:
                    d_base = [x.n for x in dse4]
                    o_base = [x.n for x in ose4]
                    v_base = vse.n

                    def store(j, sync):
                        bj = j % 4
                        cj = dh * 8 + j
                        sync.wait_ge(vse.h, v_base + j + 1)
                        if j >= 4:
                            sync.wait_ge(ose4[bj].h, o_base[bj] + 16 * (j // 4))
                        sync.dma_start(
                            out=xout[:, cj * T + t0: cj * T + t0 + 512],
                            in_=oref[bj][:]).then_inc(ose4[bj].h, 16)
                        ose4[bj].n += 16

                    @b.sync
                    def _(sync):
                        for j in range(8):
                            bi = j % 4
                            ci = dh * 8 + j
                            if j >= 4:
                                sync.wait_ge(vse.h, v_base + (j - 4) + 1)
                                sync.wait_ge(dse4[bi].h, d_base[bi] + 16 * (j // 4))
                            sync.dma_start(
                                out=xres[bi][:],
                                in_=xin[:, ci * T + t0: ci * T + t0 + 512]
                                ).then_inc(dse4[bi].h, 16)
                            dse4[bi].n += 16
                            if j >= 1:
                                store(j - 1, sync)
                        store(7, sync)
                        for x in ose4:
                            sync.wait_ge(x.h, x.n)

                    @b.vector
                    def _(v):
                        for j in range(8):
                            bi = j % 4
                            v.wait_ge(dse4[bi].h, d_base[bi] + 16 * (j // 4 + 1))
                            if j >= 4:
                                v.wait_ge(ose4[bi].h, o_base[bi] + 16 * (j // 4))
                            inst = v.tensor_tensor(
                                oref[bi][:], xres[bi][:], pb[j][:], ALU.add)
                            vse.inc(inst)
    return nc



NIN = 66
NXBC = 33
CHUNK = 128


def build_l0attn(T=2048, TILE=512):
    nc = bass.Bass("TRN2", target_bir_lowering=False)
    NT = T // TILE
    NCK = TILE // CHUNK

    xin = nc.dram_tensor("xin", [128, NCH * T], F32, kind="ExternalInput").ap()
    win = nc.dram_tensor("win", [NIN, 128, NCH * 128], F32, kind="ExternalInput").ap()
    wout = nc.dram_tensor("wout", [NCH, 128, 32 * 128], F32, kind="ExternalInput").ap()
    cwt = nc.dram_tensor("cwt", [128, NXBC * 4], F32, kind="ExternalInput").ap()
    cbt = nc.dram_tensor("cbt", [128, NXBC], F32, kind="ExternalInput").ap()
    dtb = nc.dram_tensor("dtb", [64, 1], F32, kind="ExternalInput").ap()
    alog = nc.dram_tensor("alog", [64, 1], F32, kind="ExternalInput").ap()
    drep = nc.dram_tensor("drep", [128, 32], F32, kind="ExternalInput").ap()
    nwt = nc.dram_tensor("nwt", [128, 32], F32, kind="ExternalInput").ap()
    sel32 = nc.dram_tensor("sel32", [64, 32 * 128], F32, kind="ExternalInput").ap()
    penl = nc.dram_tensor("penl", [128, 128], BF16, kind="ExternalInput").ap()
    penr = nc.dram_tensor("penr", [128, 4 * 128], BF16, kind="ExternalInput").ap()
    identb = nc.dram_tensor("identb", [128, 128], BF16, kind="ExternalInput").ap()
    identf = nc.dram_tensor("identf", [128, 128], F32, kind="ExternalInput").ap()
    zspill = nc.dram_tensor("zspill", [128, 32 * TILE], BF16).ap()
    lbounce = nc.dram_tensor("lbounce", [9, 64], F32).ap()
    xout = nc.dram_tensor("xout", [128, NCH * T], F32, kind="ExternalOutput").ap()

    with ExitStack() as ctx:
        k = K(nc, ctx)
        xh = k.sbuf("xh", [128, NCH * 512], BF16)
        mesall = xh
        sq = k.sbuf("sq", [128, NCH * 512], BF16)
        wstall = sq
        xbcp = k.sbuf("xbcp", [128, NXBC * 516], BF16)
        catall = xbcp
        xtmv = xbcp
        bwall = xbcp
        XTM0 = 8192
        BW0 = 12288
        xbc = k.sbuf("xbc", [128, NXBC * TILE], BF16)
        ybuf = k.sbuf("ybuf", [128, 32 * TILE], BF16)
        hbuf = k.sbuf("hbuf", [128, 32 * 64], F32)
        hbuf_bf = k.sbuf("hbuf_bf", [128, 32 * 64], BF16)
        xtile = [k.sbuf(f"xtile{i}", [128, 512], F32) for i in range(2)]
        xres = [k.sbuf(f"xres{i}", [128, 512], F32) for i in range(2)]
        oref = [k.sbuf(f"oref{i}", [128, 512], F32) for i in range(2)]
        dsc = [k.sbuf(f"dsc{i}", [128, 512], F32) for i in range(2)]
        tails = k.sbuf("tails", [128, NXBC * 3], BF16)
        gct = k.sbuf("gct", [128, 128], BF16)
        btm = k.sbuf("btm", [128, 128], BF16)
        cdup = k.sbuf("cdup", [128, 128], BF16)
        dtraw = k.sbuf("dtraw", [64, TILE], F32)
        dtv = k.sbuf("dtv", [64, TILE], F32)
        lam = k.sbuf("lam", [64, TILE], F32)
        lamc = k.sbuf("lamc", [64, TILE], F32)
        lcarry = k.sbuf("lcarry", [64, 1], F32)
        ldstack = k.sbuf("ldstack", [128, TILE], F32)
        ldtm = k.sbuf("ldtm", [128, NCK * 128], F32)
        lsm = k.sbuf("lsm", [64, 2 * NCK + 1], F32)
        lnext_bc = k.sbuf("lnext_bc", [128, NCK * 64], F32)
        lstart_bc = k.sbuf("lstart_bc", [128, NCK * 64], F32)
        kap_bc = k.sbuf("kap_bc", [128, 64], F32)
        decfull = k.sbuf("decfull", [128, NCK * 64], F32)
        decpair = k.sbuf("decpair", [128, NCK * 32], F32)
        kappair = k.sbuf("kappair", [128, 32], F32)
        wall = k.sbuf("wall", [128, NCK * 64], F32)
        rows = k.sbuf("rows", [1, 512], F32)
        selt = k.sbuf("selt", [64, 32 * 128], F32)
        penlt = k.sbuf("penlt", [128, 128], BF16)
        penrt = k.sbuf("penrt", [128, 4 * 128], BF16)
        identtb = k.sbuf("identtb", [128, 128], BF16)
        identtf = k.sbuf("identtf", [128, 128], F32)
        cwtt = k.sbuf("cwtt", [128, NXBC * 4], F32)
        cbtt = k.sbuf("cbtt", [128, NXBC], F32)
        dtbt = k.sbuf("dtbt", [64, 1], F32)
        acol = k.sbuf("acol", [64, 1], F32)
        drept = k.sbuf("drept", [128, 32], F32)
        nwtt = k.sbuf("nwtt", [128, 32], F32)
        ones_col_f = k.sbuf("ones_col_f", [128, 1], F32)
        ones_col_b = k.sbuf("ones_col_b", [128, 1], BF16)
        ones_row_f = k.sbuf("ones_row_f", [1, 128], F32)
        ones_row_b = k.sbuf("ones_row_b", [1, 128], BF16)
        cpool = k.sbuf("cpool", [128, 2], F32)
        nc.const_aps.register(F32, LN_EPS, cpool[:, 0:1])
        wblk = [k.sbuf(f"wblk{i}", [128, 8 * 128], BF16) for i in range(2)]
        woblk = [k.sbuf(f"woblk{i}", [128, 8 * 128], BF16) for i in range(2)]

        pb = [k.psum(f"pb{i}") for i in range(7)]
        pbt = ctx.enter_context(nc.psum_tensor("pbt", [128, 1024], BF16))

        dse4 = [k.sem() for _ in range(4)]
        ose4 = [k.sem() for _ in range(4)]
        wse = [k.sem(), k.sem()]
        zse = [k.sem(), k.sem()]
        z2se = [k.sem(), k.sem()]
        lse = k.sem()
        sse = k.sem()
        pse = k.sem()
        mse = k.sem()
        m2se = k.sem()
        m3se = k.sem()
        vse = k.sem()
        v2se = k.sem()
        v3se = k.sem()
        ase_ = k.sem()
        a2se = k.sem()
        xse = k.sem()
        use = k.sem()

        def row(i):
            return rows[0:1, i * 512:(i + 1) * 512]

        ROW_MU, ROW_RR = 0, 1

        def emit_ln(t0):
            with nc.Block() as b:
                d_base = [x.n for x in dse4]
                s_base, p_base = sse.n, pse.n

                @b.sync
                def _(sync):
                    for c in range(NCH):
                        bi = c % 2
                        if c >= 2:
                            sync.wait_ge(sse.h, s_base + (c - 2) + 1)
                            sync.wait_ge(pse.h, p_base + (c - 2) + 1)
                            sync.wait_ge(dse4[bi].h, d_base[bi] + 16 * (c // 2))
                        sync.dma_start(out=xtile[bi][:],
                                       in_=xin[:, c * T + t0: c * T + t0 + 512]
                                       ).then_inc(dse4[bi].h, 16)
                        dse4[bi].n += 16

                @b.scalar
                def _(s):
                    for c in range(NCH):
                        bi = c % 2
                        s.wait_ge(dse4[bi].h, d_base[bi] + 16 * (c // 2 + 1))
                        inst = s.activation(sq[:, c * 512:(c + 1) * 512],
                                            xtile[bi][:], AF.Square)
                        sse.inc(inst)

                @b.tensor
                def _(t):
                    for c in range(NCH):
                        bi = c % 2
                        t.wait_ge(dse4[bi].h, d_base[bi] + 16 * (c // 2 + 1))
                        inst = t.matmul(pb[0][0:1, :], ones_col_f[:], xtile[bi][:],
                                        start=(c == 0), stop=(c == NCH - 1))
                        pse.inc(inst)
                    for c in range(NCH):
                        t.wait_ge(sse.h, s_base + c + 1)
                        t.matmul(pb[1][0:1, :], ones_col_b[:],
                                 sq[:, c * 512:(c + 1) * 512],
                                 start=(c == 0), stop=(c == NCH - 1))

            with nc.Block() as b:
                @b.vector
                def _(v):
                    inv_n = 1.0 / D
                    v.tensor_scalar(row(ROW_MU), pb[0][0:1, :], inv_n, None, ALU.mult)
                    v.tensor_scalar(dsc[0][0:1, 0:512], pb[1][0:1, :], inv_n, None,
                                    ALU.mult)
                    v.drain()
                    v.tensor_tensor(dsc[1][0:1, 0:512], row(ROW_MU), row(ROW_MU),
                                    ALU.mult)
                    v.drain()
                    inst = v.tensor_tensor(dsc[0][0:1, 0:512], dsc[0][0:1, 0:512],
                                           dsc[1][0:1, 0:512], ALU.subtract)
                    xse.inc(inst)

                @b.scalar
                def _(s):
                    s.wait_ge(xse.h, xse.n)
                    s.activation(dsc[0][0:1, 0:512], dsc[0][0:1, 0:512], AF.Ln,
                                 bias=LN_EPS)
                    s.drain()
                    s.activation(dsc[1][0:1, 0:512], dsc[0][0:1, 0:512], AF.Exp, scale=-0.5)

            with nc.Block() as b:
                @b.tensor
                def _(t):
                    t.matmul(pb[2][:], ones_row_f[:], row(ROW_MU), start=True, stop=True)
                    t.matmul(pb[3][:], ones_row_f[:], dsc[1][0:1, 0:512], start=True, stop=True)

            with nc.Block() as b:
                d_base = [x.n for x in dse4]
                v_base = vse.n

                @b.sync
                def _(sync):
                    for c in range(NCH):
                        bi = c % 2
                        if c >= 2:
                            sync.wait_ge(vse.h, v_base + (c - 2) + 1)
                            sync.wait_ge(dse4[bi].h, d_base[bi] + 16 * (c // 2))
                        sync.dma_start(out=xtile[bi][:],
                                       in_=xin[:, c * T + t0: c * T + t0 + 512]
                                       ).then_inc(dse4[bi].h, 16)
                        dse4[bi].n += 16

                @b.vector
                def _(v):
                    for c in range(NCH):
                        bi = c % 2
                        v.wait_ge(dse4[bi].h, d_base[bi] + 16 * (c // 2 + 1))
                        inst = v.tensor_tensor(xh[:, c * 512:(c + 1) * 512],
                                               xtile[bi][:], pb[2][:], ALU.subtract)
                        vse.inc(inst)
                    v.drain()
                    for c in range(NCH):
                        v.tensor_tensor(xh[:, c * 512:(c + 1) * 512],
                                        xh[:, c * 512:(c + 1) * 512],
                                        pb[3][:], ALU.mult)

        with nc.Block() as b:
            @b.sync
            def _(sync):
                for ap_d, ap_s in [(selt[:], sel32), (penlt[:], penl), (penrt[:], penr),
                                   (identtb[:], identb), (identtf[:], identf),
                                   (cwtt[:], cwt), (cbtt[:], cbt),
                                   (dtbt[:], dtb), (acol[:], alog), (drept[:], drep),
                                   (nwtt[:], nwt)]:
                    sync.dma_start(out=ap_d, in_=ap_s).then_inc(use.h, 16)
                    use.n += 16
                sync.wait_ge(use.h, use.n)

            @b.vector
            def _(v):
                v.memset(cpool[:, 0:1], LN_EPS)
                v.memset(ones_col_f[:], 1.0)
                v.memset(ones_col_b[:], 1.0)
                v.memset(ones_row_f[:], 1.0)
                v.memset(ones_row_b[:], 1.0)
                v.memset(hbuf[:], 0.0)
                v.memset(hbuf_bf[:], 0.0)
                v.memset(lcarry[:], 0.0)
                v.memset(tails[:], 0.0)

        with nc.Block() as b:
            @b.scalar
            def _(s):
                s.activation(acol[:], acol[:], AF.Exp)
                s.drain()
                s.activation(acol[:], acol[:], AF.Copy, scale=-1.0)

        for tt in range(NT):
            t0 = tt * TILE

            emit_ln(t0)

            with nc.Block() as b:
                w_base = [wse[0].n, wse[1].n]
                m_base, v_base = mse.n, vse.n
                z_base = [zse[0].n, zse[1].n]

                @b.gpsimd
                def _(g):
                    for c in range(NIN):
                        for hf in range(2):
                            i = 2 * c + hf
                            bi = i % 2
                            if i >= 2:
                                g.wait_ge(mse.h, m_base + (i - 2) + 1)
                                g.wait_ge(wse[bi].h, w_base[bi] + 16 * (i // 2))
                            g.dma_start(out=wblk[bi][:],
                                        in_=win[c][:, hf * 1024:(hf + 1) * 1024]
                                        ).then_inc(wse[bi].h, 16)
                            wse[bi].n += 16

                @b.tensor
                def _(t):
                    for c in range(NIN):
                        bank = pb[4 + c % 2]
                        if c >= 2:
                            t.wait_ge(vse.h, v_base + (c - 2) + 1)
                        for hf in range(2):
                            i = 2 * c + hf
                            bi = i % 2
                            t.wait_ge(wse[bi].h, w_base[bi] + 16 * (i // 2 + 1))
                            for kc in range(8):
                                inst = t.matmul(bank[:],
                                                wblk[bi][:, kc * 128:(kc + 1) * 128],
                                                xh[:, (hf * 8 + kc) * 512:
                                                   (hf * 8 + kc) * 512 + 512],
                                                start=(hf == 0 and kc == 0),
                                                stop=(hf == 1 and kc == 7))
                            mse.inc(inst)

                @b.vector
                def _(v):
                    for c in range(NIN):
                        bi = c % 2
                        bank = pb[4 + bi]
                        v.wait_ge(mse.h, m_base + 2 * c + 2)
                        if c < 32:
                            if c >= 2:
                                v.wait_ge(zse[c % 2].h, z_base[c % 2] + 16 * (c // 2))
                            inst = v.tensor_copy(
                                ybuf[:, (c % 2) * 512:(c % 2) * 512 + 512], bank[:])
                        elif c < 65:
                            xc = c - 32
                            inst = v.tensor_copy(
                                xbcp[:, xc * 516 + 3: xc * 516 + 3 + 512], bank[:])
                        else:
                            inst = v.tensor_copy(dtraw[0:64, :], bank[0:64, :])
                        vse.inc(inst)

                @b.sync
                def _(sync):
                    for c in range(32):
                        sync.wait_ge(vse.h, v_base + c + 1)
                        if c >= 2:
                            sync.wait_ge(zse[c % 2].h, z_base[c % 2] + 16 * (c // 2))
                        sync.dma_start(out=zspill[:, c * TILE:(c + 1) * TILE],
                                       in_=ybuf[:, (c % 2) * 512:(c % 2) * 512 + 512]
                                       ).then_inc(zse[c % 2].h, 16)
                        zse[c % 2].n += 16
                    sync.wait_ge(zse[0].h, zse[0].n)
                    sync.wait_ge(zse[1].h, zse[1].n)

            with nc.Block() as b:
                @b.vector
                def _(v):
                    v.tensor_copy(xbcp[:].rearrange("p (c q) -> p c q", q=516)[:, :, 0:3],
                                  tails[:].rearrange("p (c q) -> p c q", q=3))

            with nc.Block() as b:
                v_base2, a_base2 = vse.n, ase_.n

                @b.vector
                def _(v):
                    for xc in range(NXBC):
                        bi = xc % 2
                        base = xc * 516
                        if xc >= 2:
                            v.wait_ge(ase_.h, a_base2 + (xc - 2) + 1)
                        v.tensor_scalar(dsc[bi][:], xbcp[:, base + 0: base + 512],
                                        cwtt[:, xc * 4: xc * 4 + 1], None, ALU.mult)
                        inst = None
                        for j in range(1, 4):
                            v.drain()
                            inst = v.scalar_tensor_tensor(
                                dsc[bi][:], xbcp[:, base + j: base + j + 512],
                                cwtt[:, xc * 4 + j: xc * 4 + j + 1], dsc[bi][:],
                                ALU.mult, ALU.add)
                        vse.inc(inst)
                    v.tensor_copy(tails[:].rearrange("p (c q) -> p c q", q=3),
                                  xbcp[:].rearrange("p (c q) -> p c q", q=516)
                                  [:, :, 512:515])

                @b.scalar
                def _(s):
                    for xc in range(NXBC):
                        bi = xc % 2
                        s.wait_ge(vse.h, v_base2 + xc + 1)
                        inst = s.activation(xbc[:, xc * TILE: xc * TILE + 512],
                                            dsc[bi][:], AF.Silu, bias=cbtt[:, xc:xc + 1])
                        ase_.inc(inst)

            with nc.Block() as b:
                @b.scalar
                def _(s):
                    s.activation(dtv[:], dtraw[:], AF.Abs, bias=dtbt[:])
                    s.drain()
                    s.activation(dtv[:], dtv[:], AF.Exp, scale=-1.0)
                    s.drain()
                    s.activation(dtv[:], dtv[:], AF.Ln, bias=1.0)
                    s.drain()
                    inst = s.activation(lam[:], dtraw[:], AF.Relu, bias=dtbt[:])
                    ase_.inc(inst)

                @b.vector
                def _(v):
                    v.wait_ge(ase_.h, ase_.n)
                    v.tensor_tensor(dtv[:], dtv[:], lam[:], ALU.add)
                    v.drain()
                    v.tensor_scalar(lam[:], dtv[:], acol[:], None, ALU.mult)
                    v.drain()
                    v.tensor_tensor_scan(lamc[:], lam[:], lam[:], lcarry[:],
                                         ALU.add, ALU.bypass)
                    v.drain()
                    v.tensor_copy(lcarry[:], lamc[:, TILE - 1:TILE])
                    v.tensor_copy(ldstack[0:64, :], lamc[:])
                    v.tensor_copy(ldstack[64:128, :], dtv[:])
                    lview = lamc[:].rearrange("p (c q) -> p c q", q=128)
                    v.tensor_copy(lsm[:, NCK:2 * NCK], lview[:, :, 0:1]
                                  .rearrange("p c o -> p (c o)"))
                    v.tensor_copy(lsm[:, 0:NCK - 1], lview[:, 1:, 0:1]
                                  .rearrange("p c o -> p (c o)"))
                    v.tensor_copy(lsm[:, NCK - 1:NCK], lamc[:, TILE - 1:TILE])
                    inst = v.tensor_copy(lsm[:, 2 * NCK:2 * NCK + 1], lam[:, 0:1])
                    xse.inc(inst)

            with nc.Block() as b:
                @b.tensor
                def _(t):
                    t.transpose(pb[1][0:9, 0:64], lsm[:], identtf[0:64, 0:64])
                    for tc in range(NCK):
                        t.transpose(pb[2 + tc % 2][:, (tc // 2) * 128:(tc // 2) * 128 + 128],
                                    ldstack[:, tc * 128:(tc + 1) * 128], identtf[:])

            with nc.Block() as b:
                @b.vector
                def _(v):
                    v.tensor_copy(dsc[0][0:9, 0:64], pb[1][0:9, 0:64])
                    for tc in range(NCK):
                        v.tensor_copy(ldtm[:, tc * 128:(tc + 1) * 128],
                                      pb[2 + tc % 2][:, (tc // 2) * 128:(tc // 2) * 128 + 128])

            with nc.Block() as b:
                @b.sync
                def _(sync):
                    sync.dma_start(out=lbounce, in_=dsc[0][0:9, 0:64]).then_inc(lse.h, 16)
                    lse.n += 16
                    sync.wait_ge(lse.h, lse.n)
                    sync.dma_start(out=lnext_bc[:],
                                   in_=lbounce[0:NCK, :].unsqueeze(0).broadcast_to(
                                       [128, NCK, 64]).rearrange("p c h -> p (c h)")
                                   ).then_inc(lse.h, 16)
                    sync.dma_start(out=lstart_bc[:],
                                   in_=lbounce[NCK:2 * NCK, :].unsqueeze(0).broadcast_to(
                                       [128, NCK, 64]).rearrange("p c h -> p (c h)")
                                   ).then_inc(lse.h, 16)
                    sync.dma_start(out=kap_bc[:],
                                   in_=lbounce[2 * NCK:2 * NCK + 1, :].unsqueeze(0)
                                   .broadcast_to([128, 1, 64]).rearrange("p c h -> p (c h)")
                                   ).then_inc(lse.h, 16)
                    lse.n += 48
                    sync.wait_ge(lse.h, lse.n)

            with nc.Block() as b:
                @b.vector
                def _(v):
                    v.tensor_tensor(decfull[:], lnext_bc[:], lstart_bc[:], ALU.subtract)
                    inst = None
                    for tc in range(NCK):
                        inst = v.tensor_tensor(wall[:, tc * 64:(tc + 1) * 64],
                                               lnext_bc[:, tc * 64:(tc + 1) * 64],
                                               ldtm[:, tc * 128: tc * 128 + 64],
                                               ALU.subtract)
                    xse.inc(inst)

                @b.scalar
                def _(s):
                    s.wait_ge(xse.h, xse.n)
                    s.activation(decfull[:], decfull[:], AF.Exp)
                    inst = s.activation(wall[:], wall[:], AF.Exp)
                    a2se.inc(inst)

            with nc.Block() as b:
                @b.vector
                def _(v):
                    for tc in range(NCK):
                        v.tensor_tensor(wall[:, tc * 64:(tc + 1) * 64],
                                        wall[:, tc * 64:(tc + 1) * 64],
                                        ldtm[:, tc * 128 + 64: tc * 128 + 128], ALU.mult)
                    dv = decfull[:].rearrange("p (c h) -> p c h", h=64)
                    v.tensor_copy(decpair[0:64, :].rearrange("p (c j) -> p c j", j=32),
                                  dv[0:64, :, 0:64:2])
                    v.tensor_copy(decpair[64:128, :].rearrange("p (c j) -> p c j", j=32),
                                  dv[64:128, :, 1:64:2])
                    v.tensor_copy(kappair[0:64, :], kap_bc[0:64, 0:64:2])
                    v.tensor_copy(kappair[64:128, :], kap_bc[64:128, 1:64:2])
                    v.drain()
                    v.tensor_tensor(hbuf[:].rearrange("p (j q) -> p j q", q=64),
                                    hbuf[:].rearrange("p (j q) -> p j q", q=64),
                                    kappair[:].rearrange("p (j o) -> p j o", o=1)
                                    .broadcast_to([128, 32, 64]), ALU.mult)
                    v.drain()
                    v.tensor_copy(hbuf_bf[:], hbuf[:])

            for tc in range(NCK):
                ck0 = tc * CHUNK
                with nc.Block() as b:
                    m_base, v_base = m2se.n, v2se.n
                    l_base = lse.n

                    @b.tensor
                    def _(t):
                        t.wait_ge(lse.h, l_base + 32)
                        inst = t.matmul(pb[0][:, 0:128],
                                        xbc[0:64, 32 * TILE + ck0: 32 * TILE + ck0 + 128],
                                        cdup[0:64, :],
                                        start=True, stop=True)
                        inst = t.transpose(pbt[:, 0:128],
                                           xbc[:, 32 * TILE + ck0: 32 * TILE + ck0 + 128],
                                           identtb[:])
                        m2se.inc(inst)
                        for r in range(4):
                            t.wait_ge(v2se.h, v_base + r + 1)
                            for q in range(8):
                                xc = r * 8 + q
                                inst = t.transpose(
                                    pbt[:, q * 128:(q + 1) * 128],
                                    xbc[:, xc * TILE + ck0: xc * TILE + ck0 + 128],
                                    identtb[:])
                            m2se.inc(inst)

                    @b.vector
                    def _(v):
                        v.wait_ge(m2se.h, m_base + 1)
                        v.tensor_copy(gct[:], pb[0][:, 0:128])
                        inst = v.tensor_copy(btm[:], pbt[:, 0:128])
                        v2se.inc(inst)
                        for r in range(4):
                            v.wait_ge(m2se.h, m_base + 2 + r)
                            inst = v.tensor_copy(
                                xtmv[:, XTM0 + r * 1024: XTM0 + (r + 1) * 1024], pbt[:])
                            v2se.inc(inst)
                        v.drain()
                        v.tensor_tensor(
                            xbcp[:, BW0:BW0 + 4096].rearrange("p (h n) -> p h n", n=64),
                            btm[:, 0:64].rearrange("p (o n) -> p o n", o=1)
                            .broadcast_to([128, 64, 64]),
                            wall[:, tc * 64:(tc + 1) * 64]
                            .rearrange("p (h o) -> p h o", o=1)
                            .broadcast_to([128, 64, 64]), ALU.mult)

                    @b.sync
                    def _(sync):
                        sync.dma_start(out=cdup[0:64, :],
                                       in_=xbc[64:128, 32 * TILE + ck0:
                                               32 * TILE + ck0 + 128]
                                       ).then_inc(lse.h, 16)
                        sync.dma_start(out=cdup[64:128, :],
                                       in_=xbc[64:128, 32 * TILE + ck0:
                                               32 * TILE + ck0 + 128]
                                       ).then_inc(lse.h, 16)
                        lse.n += 32
                        sync.wait_ge(lse.h, lse.n)

                with nc.Block() as b:
                    m_base, v_base, a_base = mse.n, vse.n, ase_.n
                    m3_base = m3se.n

                    @b.tensor
                    def _(t):
                        for g in range(16):
                            bk = pb[2 + g % 2]
                            if g >= 2:
                                t.wait_ge(vse.h, v_base + 2 * (g - 2) + 1)
                            for hh in range(4):
                                h = 4 * g + hh
                                rb = (h // 32) * 32
                                h32 = h % 32
                                t.matmul(bk[:, hh * 128:(hh + 1) * 128],
                                         selt[rb:rb + 32, h32 * 128:(h32 + 1) * 128],
                                         lamc[rb:rb + 32, ck0:ck0 + 128],
                                         start=(hh == 0), stop=False)
                            inst = t.matmul(bk[:], penlt[:], penrt[:],
                                            start=False, stop=True)
                            mse.inc(inst)
                            if g >= 1:
                                t.wait_ge(ase_.h, a_base + g)
                                inst = t.matmul(pb[4 + (g - 1) % 2][:],
                                                ones_row_b[:],
                                                mesall[0:1, (g - 1) * 512:g * 512],
                                                start=True, stop=True)
                                m3se.inc(inst)
                        t.wait_ge(ase_.h, a_base + 16)
                        inst = t.matmul(pb[4 + 15 % 2][:], ones_row_b[:],
                                        mesall[0:1, 15 * 512:16 * 512],
                                        start=True, stop=True)
                        m3se.inc(inst)

                    @b.vector
                    def _(v):
                        for g in range(16):
                            v.wait_ge(mse.h, m_base + g + 1)
                            bk = pb[2 + g % 2]
                            lcols = ldtm[:, tc * 128 + 4 * g: tc * 128 + 4 * g + 4]
                            dcols = ldtm[:, tc * 128 + 64 + 4 * g:
                                         tc * 128 + 64 + 4 * g + 4]
                            inst = v.tensor_tensor(dsc[g % 2][:].rearrange(
                                "p (h q) -> p h q", q=128),
                                bk[:].rearrange("p (h q) -> p h q", q=128),
                                lcols.rearrange("p (h o) -> p h o", o=1)
                                .broadcast_to([128, 4, 128]), ALU.subtract)
                            vse.inc(inst)
                            v.wait_ge(ase_.h, a_base + g + 1)
                            v.tensor_tensor(wstall[:, g * 512:(g + 1) * 512]
                                            .rearrange("p (h q) -> p h q", q=128),
                                            mesall[:, g * 512:(g + 1) * 512]
                                            .rearrange("p (h q) -> p h q", q=128),
                                            dcols.rearrange("p (h o) -> p h o", o=1)
                                            .broadcast_to([128, 4, 128]), ALU.mult)
                            v.drain()
                            inst = v.tensor_tensor(
                                wstall[:, g * 512:(g + 1) * 512]
                                .rearrange("p (h q) -> p h q", q=128),
                                wstall[:, g * 512:(g + 1) * 512]
                                .rearrange("p (h q) -> p h q", q=128),
                                gct[:].rearrange("p (o q) -> p o q", o=1)
                                .broadcast_to([128, 4, 128]), ALU.mult)
                            vse.inc(inst)
                            v.wait_ge(m3se.h, m3_base + g + 1)
                            inst = v.tensor_tensor(
                                catall[:, g * 512:(g + 1) * 512]
                                .rearrange("p (h q) -> p h q", q=128),
                                cdup[:].rearrange("p (o q) -> p o q", o=1)
                                .broadcast_to([128, 4, 128]),
                                pb[4 + g % 2][:].rearrange("p (h q) -> p h q", q=128),
                                ALU.mult)
                            v3se.inc(inst)

                    @b.scalar
                    def _(s):
                        for g in range(16):
                            s.wait_ge(vse.h, v_base + 2 * g + 1)
                            inst = s.activation(mesall[:, g * 512:(g + 1) * 512],
                                                dsc[g % 2][:], AF.Exp)
                            ase_.inc(inst)

                with nc.Block() as b:
                    m_base, v_base = mse.n, vse.n

                    @b.tensor
                    def _(t):
                        for g in range(16):
                            yb = pb[2 + g % 2]
                            ib = pb[4 + g % 2]
                            if g >= 2:
                                t.wait_ge(vse.h, v_base + 2 * (g - 1))
                            for hh in range(2):
                                for jj in range(2):
                                    j = 2 * g + jj
                                    h = 2 * j + hh
                                    t.matmul(yb[hh * 64:hh * 64 + 64,
                                                jj * 128:jj * 128 + 128],
                                             xtmv[:, XTM0 + j * 128 + hh * 64:
                                                  XTM0 + j * 128 + hh * 64 + 64],
                                             wstall[:, h * 128:(h + 1) * 128],
                                             start=(jj == 0), stop=False)
                                    inst = t.matmul(
                                        yb[hh * 64:hh * 64 + 64,
                                           jj * 128:jj * 128 + 128],
                                        hbuf_bf[hh * 64:hh * 64 + 64,
                                                j * 64:(j + 1) * 64],
                                        catall[hh * 64:hh * 64 + 64,
                                               h * 128:(h + 1) * 128],
                                        start=False,
                                        stop=(jj == 1))
                            mse.inc(inst)
                            for hh in range(2):
                                for jj in range(2):
                                    j = 2 * g + jj
                                    h = 2 * j + hh
                                    inst = t.matmul(
                                        ib[hh * 64:hh * 64 + 64,
                                           jj * 128 + 0: jj * 128 + 64],
                                        xbcp[:, BW0 + h * 64: BW0 + h * 64 + 64],
                                        xtmv[:, XTM0 + j * 128 + hh * 64:
                                             XTM0 + j * 128 + hh * 64 + 64],
                                        start=(jj == 0), stop=(jj == 1))
                            mse.inc(inst)

                    @b.vector
                    def _(v):
                        for g in range(16):
                            yb = pb[2 + g % 2]
                            ib = pb[4 + g % 2]
                            v.wait_ge(mse.h, m_base + 2 * g + 2)
                            for jj in range(2):
                                j = 2 * g + jj
                                v.scalar_tensor_tensor(
                                    ybuf[:, j * TILE + ck0: j * TILE + ck0 + 128],
                                    xbc[:, j * TILE + ck0: j * TILE + ck0 + 128],
                                    drept[:, j:j + 1],
                                    yb[:, jj * 128:(jj + 1) * 128],
                                    ALU.mult, ALU.add)
                                inst = v.scalar_tensor_tensor(
                                    hbuf[:, j * 64:(j + 1) * 64],
                                    hbuf[:, j * 64:(j + 1) * 64],
                                    decpair[:, tc * 32 + j: tc * 32 + j + 1],
                                    ib[:, jj * 128: jj * 128 + 64],
                                    ALU.mult, ALU.add)
                                vse.inc(inst)
                        v.drain()
                        v.tensor_copy(hbuf_bf[:], hbuf[:])

            with nc.Block() as b:
                z_base = [z2se[0].n, z2se[1].n]
                v_base, a_base = vse.n, ase_.n

                @b.gpsimd
                def _(g):
                    for c in range(32):
                        bi = c % 2
                        if c >= 2:
                            g.wait_ge(ase_.h, a_base + (c - 2) + 1)
                            g.wait_ge(z2se[bi].h, z_base[bi] + 16 * (c // 2))
                        g.dma_start(out=dsc[bi][:].bitcast(BF16)[:, 0:512],
                                    in_=zspill[:, c * TILE:(c + 1) * TILE]
                                    ).then_inc(z2se[bi].h, 16)
                        z2se[bi].n += 16

                @b.scalar
                def _(s):
                    for c in range(32):
                        bi = c % 2
                        s.wait_ge(z2se[bi].h, z_base[bi] + 16 * (c // 2 + 1))
                        if c >= 2:
                            s.wait_ge(vse.h, v_base + (c - 2) + 1)
                        inst = s.activation(dsc[bi][:].bitcast(BF16)[:, 512:1024],
                                            dsc[bi][:].bitcast(BF16)[:, 0:512], AF.Silu)
                        ase_.inc(inst)

                @b.vector
                def _(v):
                    for c in range(32):
                        bi = c % 2
                        v.wait_ge(ase_.h, a_base + c + 1)
                        inst = v.tensor_tensor(ybuf[:, c * TILE:(c + 1) * TILE],
                                               ybuf[:, c * TILE:(c + 1) * TILE],
                                               dsc[bi][:].bitcast(BF16)[:, 512:1024],
                                               ALU.mult)
                        vse.inc(inst)

            with nc.Block() as b:
                s_base = sse.n

                @b.scalar
                def _(s):
                    p_base0 = pse.n
                    for c in range(32):
                        if c >= 16:
                            s.wait_ge(pse.h, p_base0 + (c - 16) + 1)
                        inst = s.activation(sq[:, (c % 16) * 512:(c % 16) * 512 + 512],
                                            ybuf[:, c * TILE:(c + 1) * TILE], AF.Square)
                        sse.inc(inst)

                @b.tensor
                def _(t):
                    for c in range(32):
                        t.wait_ge(sse.h, s_base + c + 1)
                        inst = t.matmul(pb[1][0:1, :], ones_col_b[:],
                                        sq[:, (c % 16) * 512:(c % 16) * 512 + 512],
                                        start=(c == 0), stop=(c == 31))
                        pse.inc(inst)


            with nc.Block() as b:
                @b.vector
                def _(v):
                    inst = v.tensor_scalar(dsc[0][0:1, 0:512], pb[1][0:1, :],
                                           1.0 / 4096.0, None, ALU.mult)
                    xse.inc(inst)

                @b.scalar
                def _(s):
                    s.wait_ge(xse.h, xse.n)
                    s.activation(dsc[0][0:1, 0:512], dsc[0][0:1, 0:512], AF.Ln,
                                 bias=LN_EPS)
                    s.drain()
                    s.activation(dsc[1][0:1, 0:512], dsc[0][0:1, 0:512], AF.Exp,
                                 scale=-0.5)

            with nc.Block() as b:
                @b.tensor
                def _(t):
                    t.matmul(pb[2][:], ones_row_f[:], dsc[1][0:1, 0:512],
                             start=True, stop=True)

            with nc.Block() as b:
                @b.vector
                def _(v):
                    for c in range(32):
                        v.tensor_tensor(ybuf[:, c * TILE:(c + 1) * TILE],
                                        ybuf[:, c * TILE:(c + 1) * TILE],
                                        pb[2][:], ALU.mult)
                    v.drain()
                    for c in range(32):
                        v.tensor_scalar(ybuf[:, c * TILE:(c + 1) * TILE],
                                        ybuf[:, c * TILE:(c + 1) * TILE],
                                        nwtt[:, c:c + 1], None, ALU.mult)

            with nc.Block() as b:
                w_base = [wse[0].n, wse[1].n]
                m_base, v_base = mse.n, vse.n
                d_base = [x.n for x in dse4]
                o_base = [x.n for x in ose4]

                @b.gpsimd
                def _(g):
                    for dc in range(NCH):
                        for qt in range(4):
                            i = 4 * dc + qt
                            bi = i % 2
                            if i >= 2:
                                g.wait_ge(mse.h, m_base + (i - 2) + 1)
                                g.wait_ge(wse[bi].h, w_base[bi] + 16 * (i // 2))
                            g.dma_start(out=woblk[bi][:],
                                        in_=wout[dc][:, qt * 1024:(qt + 1) * 1024]
                                        ).then_inc(wse[bi].h, 16)
                            wse[bi].n += 16

                @b.tensor
                def _(t):
                    for dc in range(NCH):
                        bank = pb[4 + dc % 2]
                        if dc >= 2:
                            t.wait_ge(vse.h, v_base + (dc - 2) + 1)
                        for qt in range(4):
                            i = 4 * dc + qt
                            bi = i % 2
                            t.wait_ge(wse[bi].h, w_base[bi] + 16 * (i // 2 + 1))
                            for kc in range(8):
                                inst = t.matmul(
                                    bank[:],
                                    woblk[bi][:, kc * 128:(kc + 1) * 128],
                                    ybuf[:, (qt * 8 + kc) * TILE:
                                         (qt * 8 + kc) * TILE + 512],
                                    start=(qt == 0 and kc == 0),
                                    stop=(qt == 3 and kc == 7))
                            mse.inc(inst)

                @b.sync
                def _(sync):
                    for dc in range(NCH):
                        bi = dc % 2
                        if dc >= 2:
                            sync.wait_ge(vse.h, v_base + (dc - 2) + 1)
                            sync.wait_ge(dse4[bi].h, d_base[bi] + 16 * (dc // 2))
                        sync.dma_start(out=xres[bi][:],
                                       in_=xin[:, dc * T + t0: dc * T + t0 + 512]
                                       ).then_inc(dse4[bi].h, 16)
                        dse4[bi].n += 16
                        if dc >= 1:
                            bj = (dc - 1) % 2
                            sync.wait_ge(vse.h, v_base + dc)
                            if dc >= 3:
                                sync.wait_ge(ose4[bj].h, o_base[bj] + 16 * ((dc - 1) // 2))
                            sync.dma_start(
                                out=xout[:, (dc - 1) * T + t0: (dc - 1) * T + t0 + 512],
                                in_=oref[bj][:]).then_inc(ose4[bj].h, 16)
                            ose4[bj].n += 16
                    bj = 15 % 2
                    sync.wait_ge(vse.h, v_base + 16)
                    sync.dma_start(out=xout[:, 15 * T + t0: 15 * T + t0 + 512],
                                   in_=oref[bj][:]).then_inc(ose4[bj].h, 16)
                    ose4[bj].n += 16
                    for x in ose4:
                        sync.wait_ge(x.h, x.n)

                @b.vector
                def _(v):
                    for dc in range(NCH):
                        bi = dc % 2
                        v.wait_ge(mse.h, m_base + 4 * dc + 4)
                        v.wait_ge(dse4[bi].h, d_base[bi] + 16 * (dc // 2 + 1))
                        if dc >= 2:
                            v.wait_ge(ose4[bi].h, o_base[bi] + 16 * (dc // 2))
                        inst = v.tensor_tensor(oref[bi][:], xres[bi][:],
                                               pb[4 + dc % 2][:], ALU.add)
                        vse.inc(inst)
    return nc

MNH = 8


def build_l1attn(T=2048, NITER=30, NCK_RUN=None, STAGE=6, MS=4):
    nc = bass.Bass("TRN2", target_bir_lowering=False)
    NCKT = T // 128
    NT = T // 512

    xin = nc.dram_tensor("xin", [128, NCH * T], F32, kind="ExternalInput").ap()
    wq = nc.dram_tensor("wq", [MNH, 128, NCH * 128], F32, kind="ExternalInput").ap()
    wk = nc.dram_tensor("wk", [MNH, 128, NCH * 128], F32, kind="ExternalInput").ap()
    wv = nc.dram_tensor("wv", [MNH, 128, NCH * 128], F32, kind="ExternalInput").ap()
    waba = nc.dram_tensor("waba", [128, NCH * MNH], F32, kind="ExternalInput").ap()
    wabb = nc.dram_tensor("wabb", [128, NCH * MNH], F32, kind="ExternalInput").ap()
    abain = nc.dram_tensor("abain", [MNH, 1], F32, kind="ExternalInput").ap()
    abbin = nc.dram_tensor("abbin", [MNH, 1], F32, kind="ExternalInput").ap()
    lamin = nc.dram_tensor("lamin", [128, MNH], F32, kind="ExternalInput").ap()
    onw = nc.dram_tensor("onw", [1, MNH * 128], F32, kind="ExternalInput").ap()
    sel8 = nc.dram_tensor("sel8", [MNH, MNH * 128], F32, kind="ExternalInput").ap()
    penl = nc.dram_tensor("penl", [128, 128], BF16, kind="ExternalInput").ap()
    penr = nc.dram_tensor("penr", [128, 128], BF16, kind="ExternalInput").ap()
    identb = nc.dram_tensor("identb", [128, 128], BF16, kind="ExternalInput").ap()
    identf = nc.dram_tensor("identf", [128, 128], F32, kind="ExternalInput").ap()
    kspill = nc.dram_tensor("kspill", [NT, 128, MNH, 512], BF16).ap()
    vspill = nc.dram_tensor("vspill", [NT, 128, MNH, 512], BF16).ap()
    qspill = nc.dram_tensor("qspill", [NT, 128, MNH, 512], BF16).ap()
    gbounce = nc.dram_tensor("gbounce", [NCKT + 1, MNH], F32).ap()
    abounce = nc.dram_tensor("abounce", [NCKT * MNH, 128], F32).ap()
    lrbounce = nc.dram_tensor("lrbounce", [MNH, 128], F32).ap()
    oimg = nc.dram_tensor("oimg", [128, MNH * T], BF16, kind="ExternalOutput").ap()

    with ExitStack() as ctx:
        k = K(nc, ctx)
        W = MNH * 128

        xh = k.sbuf("xh", [128, NCH * T], BF16)
        xst = k.sbuf("xst", [128, W], F32)
        rst = k.sbuf("rst", [128, W], F32)
        pst = k.sbuf("pst", [128, W], F32)
        qst = k.sbuf("qst", [128, W], F32)
        bqt = k.sbuf("bqt", [128, W], F32)
        rnm = k.sbuf("rnm", [128, W], F32)
        junk = k.sbuf("junk", [128, W], F32)
        akk = k.sbuf("akk", [128, W], F32)
        akv = k.sbuf("akv", [128, W], F32)
        akkb = k.sbuf("akkb", [128, W], BF16)
        akvb = k.sbuf("akvb", [128, W], BF16)
        kfm = k.sbuf("kfm", [128, W], BF16)
        vfm = k.sbuf("vfm", [128, W], BF16)
        ktm = k.sbuf("ktm", [128, W], BF16)
        vtm = k.sbuf("vtm", [128, W], BF16)
        k2tm = k.sbuf("k2tm", [128, W], BF16)
        maskb = k.sbuf("maskb", [128, W], BF16)
        kwall = k.sbuf("kwall", [128, W], BF16)
        ebuf = k.sbuf("ebuf", [128, W], BF16)
        aftile = k.sbuf("aftile", [128, W], BF16)
        pfm = k.sbuf("pfm", [128, W], BF16)
        pafm = k.sbuf("pafm", [128, W], BF16)
        stall = k.sbuf("stall", [128, W], BF16)
        obuf = k.sbuf("obuf", [128, W], BF16)
        obuf2 = k.sbuf("obuf2", [128, W], BF16)
        lamd = k.sbuf("lamd", [128, W], BF16)
        wnb = k.sbuf("wnb", [128, W], F32)
        dstate = k.sbuf("dstate", [1, W], F32)
        lr8 = k.sbuf("lr8", [1, W], F32)
        arow1 = k.sbuf("arow1", [1, W], F32)
        gmg = k.sbuf("gmg", [MNH, T], F32)
        gmb = k.sbuf("gmb", [MNH, T], F32)
        gsm = k.sbuf("gsm", [MNH, NCKT + 1], F32)
        gzero = k.sbuf("gzero", [MNH, 1], F32)
        ghatb = k.sbuf("ghatb", [MNH, 128], F32)
        gbtm = k.sbuf("gbtm", [128, NCKT * 2 * MNH], F32)
        gstart_bc = k.sbuf("gstart_bc", [128, NCKT * MNH], F32)
        gnext_bc = k.sbuf("gnext_bc", [128, NCKT * MNH], F32)
        ghall = k.sbuf("ghall", [128, NCKT * MNH], F32)
        acall = k.sbuf("acall", [128, NCKT * MNH], F32)
        wcall = k.sbuf("wcall", [128, NCKT * MNH], F32)
        decbig = k.sbuf("decbig", [128, NCKT * MNH], F32)
        atrp = k.sbuf("atrp", [128, 128], F32)
        lamct = k.sbuf("lamct", [128, MNH], F32)
        lamct2 = k.sbuf("lamct2", [128, MNH], F32)
        lsc = k.sbuf("lsc", [128, MNH], F32)
        lr8t = k.sbuf("lr8t", [MNH, 128], F32)
        gs1 = k.sbuf("gs1", [MNH, 512], F32)
        gs2 = k.sbuf("gs2", [MNH, 512], F32)
        abta = k.sbuf("abta", [MNH, 1], F32)
        abtb = k.sbuf("abtb", [MNH, 1], F32)
        abtan = k.sbuf("abtan", [MNH, 1], F32)
        wabat = k.sbuf("wabat", [128, NCH * MNH], BF16)
        wabbt = k.sbuf("wabbt", [128, NCH * MNH], BF16)
        rowsg = k.sbuf("rowsg", [128, 16 * MNH], F32)
        rows = k.sbuf("rows", [1, 2 * 512], F32)
        xtile = [k.sbuf(f"xtile{i}", [128, 512], F32) for i in range(2)]
        tsc = [k.sbuf(f"tsc{i}", [128, 512], F32) for i in range(2)]
        sqb = [k.sbuf(f"sqb{i}", [128, 512], BF16) for i in range(2)]
        qtile = [k.sbuf(f"qtile{i}", [128, 512], BF16) for i in range(2)]
        wblk = [k.sbuf(f"wblk{i}", [128, NCH * 128], BF16) for i in range(2)]
        selt = k.sbuf("selt", [MNH, MNH * 128], F32)
        penlt = k.sbuf("penlt", [128, 128], BF16)
        penrt = k.sbuf("penrt", [128, 128], BF16)
        identtb = k.sbuf("identtb", [128, 128], BF16)
        identtf = k.sbuf("identtf", [128, 128], F32)
        ones_col_f = k.sbuf("ones_col_f", [128, 1], F32)
        ones_col_b = k.sbuf("ones_col_b", [128, 1], BF16)
        ones_row_f = k.sbuf("ones_row_f", [1, 128], F32)
        cpool = k.sbuf("cpool", [128, 6], F32)
        nc.const_aps.register(F32, LN_EPS, cpool[:, 0:1])
        nc.const_aps.register(F32, 1e-30, cpool[:, 1:2])
        nc.const_aps.register(F32, 1.0, cpool[:, 2:3])
        nc.const_aps.register(F32, 1e-6, cpool[:, 3:4])

        ps = [ctx.enter_context(nc.psum_tensor(f"ps{i}", [128, 512], F32))
              for i in range(6)]
        pbt = ctx.enter_context(nc.psum_tensor("pbt", [128, 1024], BF16))
        ps7 = ctx.enter_context(nc.psum_tensor("ps7", [128, 512], F32))

        mse = k.sem()
        vse = k.sem()
        ase_ = k.sem()
        wse = [k.sem(), k.sem()]
        spse = [k.sem(), k.sem()]
        lnse = [k.sem(), k.sem()]
        kse = k.sem()
        vlse = k.sem()
        qlse = k.sem()
        ose = k.sem()
        gse = k.sem()
        g2se = k.sem()
        aese = k.sem()
        ar2se = k.sem()
        afse = k.sem()
        lrse = k.sem()
        use = k.sem()
        uge = k.sem()

        R_PQ, R_DN, R_D, R_ND, R_AL, R_NA, R_BE, R_S1, R_S2, R_S3 = range(10)

        def row(i):
            return rowsg[:, i * MNH:(i + 1) * MNH]

        def bcv(buf, c0):
            return (buf[:, c0 * MNH:(c0 + 1) * MNH].unsqueeze(2)
                    .broadcast_to([128, MNH, 128]))

        def v3(ap2d):
            return ap2d.rearrange("p (h j) -> p h j", j=128)

        seq = {"t": [], "v": [], "s": [], "y": [], "g": []}
        esem = {"t": mse, "v": vse, "s": ase_}

        def op(eng, fn, waits=(), sem=None, inc=0, noinc=False):
            wl = [x for x in waits if x is not None]
            if noinc:
                seq[eng].append((fn, wl, None, 0))
                s_ = esem.get(eng)
                return (s_, s_.n if s_ is not None else 0)
            if sem is None and eng in esem:
                sem, inc = esem[eng], 1
            if sem is not None and inc:
                sem.n += inc
            store_sem = sem if eng in esem else None
            seq[eng].append((fn, wl, store_sem, inc))
            return (sem, sem.n if sem is not None else 0)

        def drain(eng):
            op(eng, lambda e: e.drain(), noinc=True)

        G = {}

        def w(tok):
            return tok if (tok is not None and tok[0] is not None and
                           tok[1] > 0) else None

        with nc.Block() as b:
            @b.sync
            def _(sync):
                for ap_d, ap_s in [(selt[:], sel8), (penlt[:], penl),
                                   (penrt[:], penr), (identtb[:], identb),
                                   (identtf[:], identf), (abta[:], abain),
                                   (abtb[:], abbin), (lamct[:], lamin)]:
                    sync.dma_start(out=ap_d, in_=ap_s).then_inc(use.h, 16)
                    use.n += 16
                sync.dma_start(out=wnb[:],
                               in_=onw.unsqueeze(0).broadcast_to([128, 1, W])
                               .rearrange("p o q -> p (o q)")).then_inc(use.h, 16)
                use.n += 16
                sync.wait_ge(use.h, use.n)

            @b.gpsimd
            def _(g):
                g.dma_start(out=wabat[:], in_=waba).then_inc(uge.h, 16)
                g.dma_start(out=wabbt[:], in_=wabb).then_inc(uge.h, 16)
                uge.n += 32
                g.wait_ge(uge.h, uge.n)

            @b.vector
            def _(v):
                v.memset(cpool[:, 0:1], LN_EPS)
                v.memset(cpool[:, 1:2], 1e-30)
                v.memset(cpool[:, 2:3], 1.0)
                v.memset(cpool[:, 3:4], 1e-6)
                v.memset(akk[:], 0.0)
                v.memset(akv[:], 0.0)
                v.memset(akkb[:], 0.0)
                v.memset(akvb[:], 0.0)
                v.memset(dstate[:], 0.0)
                v.memset(gzero[:], 0.0)
                v.memset(ones_col_f[:], 1.0)
                v.memset(ones_col_b[:], 1.0)
                v.memset(ones_row_f[:], 1.0)
                v.wait_ge(use.h, use.n)
                v.drain()
                v.tensor_scalar(abtan[:], abta[:], -1.0, None, ALU.mult)


        op("s", lambda s: s.activation(lsc[:], lamct[:], AF.Abs))
        drain("s")
        op("s", lambda s: s.activation(lsc[:], lsc[:], AF.Exp, scale=-1.0))
        drain("s")
        op("s", lambda s: s.activation(lsc[:], lsc[:], AF.Ln, bias=1.0))
        drain("s")
        t2 = op("s", lambda s: s.activation(lamct2[:], lamct[:], AF.Relu))
        op("v", lambda v: v.tensor_tensor(lamct2[:], lamct2[:], lsc[:],
                                          ALU.add), waits=[w(t2)])
        drain("v")
        t4 = op("v", lambda v: v.tensor_scalar(lamct2[:], lamct2[:], 0.25,
                                               None, ALU.add))
        drain("v")
        for h in range(MNH):
            def f(v, h=h):
                return v.tensor_scalar(lamd[:, h * 128:(h + 1) * 128],
                                       identtb[:], lamct2[:, h:h + 1], None,
                                       ALU.mult)
            op("v", f)
        t5 = op("t", lambda t: t.transpose(ps7[0:MNH, 0:128], lamct2[:],
                                           identtf[:]), waits=[w(t4)])
        t6 = op("v", lambda v: v.tensor_scalar(lr8t[:], ps7[0:MNH, 0:128],
                                               1e-8, None, ALU.add),
                waits=[w(t5)])
        op("y", lambda y: y.dma_start(out=lrbounce, in_=lr8t[:]).then_inc(
            lrse.h, 16), waits=[w(t6)], sem=lrse, inc=16)
        lrtok = (lrse, lrse.n)
        op("y", lambda y: y.dma_start(
            out=lr8[:],
            in_=lrbounce.rearrange("h t -> (h t)").unsqueeze(0)).then_inc(
            lrse.h, 16), waits=[w(lrtok)], sem=lrse, inc=16)
        lr8_tok = (lrse, lrse.n)

        lnuse = [0, 0]
        for tt in range(NT):
            t0 = tt * 512
            sq_toks, mt1_toks, mt2_toks = [], [], []
            for c in range(NCH):
                bi = c % 2
                waits = []
                if c >= 2:
                    waits += [w(sq_toks[c - 2]), w(mt1_toks[c - 2])]
                elif tt > 0:
                    waits += [w(G["lnlast"])]
                lnuse[bi] += 1

                def f(y, bi=bi, c=c, t0=t0):
                    return y.dma_start(
                        out=xtile[bi][:],
                        in_=xin[:, c * T + t0: c * T + t0 + 512]).then_inc(
                        lnse[bi].h, 16)
                op("y", f, waits=waits, sem=lnse[bi], inc=16)
                dtok = (lnse[bi], 16 * lnuse[bi])

                def fs(s, bi=bi, c=c, t0=t0):
                    return s.activation(xh[:, c * T + t0: c * T + t0 + 512],
                                        xtile[bi][:], AF.Square)
                sq_toks.append(op("s", fs, waits=[w(dtok)]))

                def fm(t, bi=bi, c=c):
                    return t.matmul(ps[2][0:1, :], ones_col_f[:], xtile[bi][:],
                                    start=(c == 0), stop=(c == NCH - 1))
                mt1_toks.append(op("t", fm, waits=[w(dtok)] + (
                    [w(G.get("lnrow"))] if c == 0 else [])))

                def fm2(t, c=c, t0=t0):
                    return t.matmul(ps[3][0:1, :], ones_col_b[:],
                                    xh[:, c * T + t0: c * T + t0 + 512],
                                    start=(c == 0), stop=(c == NCH - 1))
                mt2_toks.append(op("t", fm2, waits=[w(sq_toks[-1])]))

            inv_n = 1.0 / D
            op("v", lambda v: v.tensor_scalar(rows[0:1, 0:512], ps[2][0:1, :],
                                              inv_n, None, ALU.mult),
               waits=[w(mt1_toks[-1])])
            rtok = op("v", lambda v: v.tensor_scalar(
                rows[0:1, 512:1024], ps[3][0:1, :], inv_n, None, ALU.mult),
                waits=[w(mt2_toks[-1])])
            G["lnrow"] = rtok
            drain("v")
            op("v", lambda v: v.tensor_tensor(junk[0:1, 0:512],
                                              rows[0:1, 0:512],
                                              rows[0:1, 0:512], ALU.mult))
            drain("v")
            vtok = op("v", lambda v: v.tensor_tensor(
                rows[0:1, 512:1024], rows[0:1, 512:1024], junk[0:1, 0:512],
                ALU.subtract))
            op("s", lambda s: s.activation(rows[0:1, 512:1024],
                                           rows[0:1, 512:1024], AF.Ln,
                                           bias=LN_EPS), waits=[w(vtok)])
            drain("s")
            etok = op("s", lambda s: s.activation(rows[0:1, 512:1024],
                                                  rows[0:1, 512:1024], AF.Exp,
                                                  scale=-0.5))
            m1 = op("t", lambda t: t.matmul(ps[4][:], ones_row_f[:],
                                            rows[0:1, 0:512], start=True,
                                            stop=True),
                    waits=[w(vtok), w(G.get("lnbc"))])
            m2 = op("t", lambda t: t.matmul(ps[5][:], ones_row_f[:],
                                            rows[0:1, 512:1024], start=True,
                                            stop=True), waits=[w(etok)])
            ntoks = []
            for c in range(NCH):
                bi = c % 2
                waits = [w(m2)] if c == 0 else []
                if c >= 2:
                    waits.append(w(ntoks[c - 2]))
                lnuse[bi] += 1

                def f(y, bi=bi, c=c, t0=t0):
                    return y.dma_start(
                        out=xtile[bi][:],
                        in_=xin[:, c * T + t0: c * T + t0 + 512]).then_inc(
                        lnse[bi].h, 16)
                op("y", f, waits=waits, sem=lnse[bi], inc=16)
                dtok = (lnse[bi], 16 * lnuse[bi])

                def fv(v, bi=bi, c=c):
                    return v.tensor_tensor(
                        junk[:, (c % 2) * 512:(c % 2) * 512 + 512],
                        xtile[bi][:], ps[4][:], ALU.subtract)
                op("v", fv, waits=[w(dtok), w(m1), w(m2)])
                drain("v")

                def fv2(v, c=c, t0=t0):
                    return v.tensor_tensor(
                        xh[:, c * T + t0: c * T + t0 + 512],
                        junk[:, (c % 2) * 512:(c % 2) * 512 + 512],
                        ps[5][:], ALU.mult)
                ntoks.append(op("v", fv2, waits=[w(mt2_toks[-1])]
                              if c == 0 else []))
            G["lnbc"] = ntoks[-1]
            G["lnlast"] = ntoks[-1]

        spuse = [0, 0]
        nlin = 0
        for ti, (wsrc, spill, do_l2) in enumerate(
                [(wq, qspill, True), (wk, kspill, True), (wv, vspill, False)]):
            for h in range(MNH):
                bi = ti * MNH + h
                slot = bi % 2
                waits = []
                if bi >= 2:
                    waits.append(w(G[f"blkMM_{bi-2}_{NT-1}"]))

                def f(g, slot=slot, wsrc=wsrc, h=h):
                    return g.dma_start(out=wblk[slot][:],
                                       in_=wsrc[h]).then_inc(wse[slot].h, 16)
                op("g", f, waits=waits, sem=wse[slot], inc=16)
                wtok = (wse[slot], wse[slot].n)

                for sub in range(NT):
                    n = bi * NT + sub
                    bnk = ps[n % 2]
                    waits = [w(wtok)]
                    if n == 0:
                        waits.append(w(G["lnlast"]))
                    if n >= 2:
                        np_, ns_ = (n - 2) // NT, (n - 2) % NT
                        waits.append(w(G[f"silu_{np_}_{ns_}"]))
                    for kc in range(NCH):
                        def fm(t, slot=slot, kc=kc, sub=sub, bnk=bnk):
                            return t.matmul(
                                bnk[:],
                                wblk[slot][:, kc * 128:(kc + 1) * 128],
                                xh[:, kc * T + sub * 512: kc * T + sub * 512 + 512],
                                start=(kc == 0), stop=(kc == NCH - 1))
                        mtok = op("t", fm, waits=waits if kc == 0 else [])
                    G[f"blkMM_{bi}_{sub}"] = mtok

                    ss = n % 2
                    qslot = nlin % 2
                    nlin += 1
                    spuse[qslot] += 1
                    sptok = ((spse[qslot], 16 * (spuse[qslot] - 1))
                             if spuse[qslot] > 1 else None)
                    if do_l2:
                        def fs(s, ss=ss, bnk=bnk):
                            return s.activation(tsc[ss][:], bnk[:], AF.Silu)
                        sil = op("s", fs, waits=[w(mtok),
                                                 w(G.get(f"tscr_{ss}"))])
                        G[f"silu_{bi}_{sub}"] = sil
                        drain("s")

                        def fs2(s, ss=ss):
                            return s.activation(sqb[ss][:], tsc[ss][:],
                                                AF.Square)
                        sq2 = op("s", fs2, waits=[w(G.get(f"sqbrd_{ss}"))])

                        def fd(t, ss=ss, n=n):
                            return t.matmul(ps[2 + n % 2][0:1, :],
                                            ones_col_b[:], sqb[ss][:],
                                            start=True, stop=True)
                        dtok = op("t", fd, waits=[w(sq2),
                                                  w(G.get(f"lnr_{n%2}"))])
                        G[f"sqbrd_{ss}"] = dtok

                        def fl(s, ss=ss, n=n):
                            return s.activation(
                                rows[0:1, ss * 512:ss * 512 + 512],
                                ps[2 + n % 2][0:1, :], AF.Ln, bias=1e-30)
                        op("s", fl, waits=[w(dtok),
                                           w(G.get(f"rowsrd_{ss}"))])
                        drain("s")

                        def fe(s, ss=ss):
                            return s.activation(
                                rows[0:1, ss * 512:ss * 512 + 512],
                                rows[0:1, ss * 512:ss * 512 + 512],
                                AF.Exp, scale=-0.5)
                        etok = op("s", fe)
                        G[f"lnr_{n%2}"] = etok

                        def fb(t, ss=ss, n=n):
                            return t.matmul(ps[4 + n % 2][:], ones_row_f[:],
                                            rows[0:1, ss * 512:ss * 512 + 512],
                                            start=True, stop=True)
                        btok = op("t", fb, waits=[w(etok),
                                                  w(G.get(f"qh_{n%2}"))])
                        G[f"rowsrd_{ss}"] = btok

                        def fq(v, ss=ss, n=n, qslot=qslot):
                            return v.tensor_tensor(qtile[qslot][:], tsc[ss][:],
                                                   ps[4 + n % 2][:], ALU.mult)
                        qh = op("v", fq, waits=[w(btok), w(sptok)])
                        G[f"qh_{n%2}"] = qh
                        G[f"tscr_{ss}"] = qh
                        wr = w(qh)
                    else:
                        def fs(s, qslot=qslot, bnk=bnk):
                            return s.activation(qtile[qslot][:], bnk[:],
                                                AF.Silu)
                        sil = op("s", fs, waits=[w(mtok), w(sptok)])
                        G[f"silu_{bi}_{sub}"] = sil
                        wr = w(sil)

                    def fy(y, spill=spill, sub=sub, h=h, qslot=qslot):
                        return y.dma_start(out=spill[sub, :, h, :],
                                           in_=qtile[qslot][:]).then_inc(
                            spse[qslot].h, 16)
                    op("y", fy, waits=[wr], sem=spse[qslot], inc=16)
                    G[f"sp_{qslot}"] = (spse[qslot], spse[qslot].n)
        last_silu = G[f"silu_{3*MNH-1}_{NT-1}"]

        for seg in range(NT):
            pa = ps[seg % 2]
            pbb = ps[2 + seg % 2]
            wa = [w(G.get(f"gev_{seg-2}")), w(G.get(f"gstt_{seg-2}"))]
            if seg == 0:
                wa.append(w(last_silu))
                wa.append(w(G.get("qh_0")))
                wa.append(w(G.get("qh_1")))
            for kc in range(NCH):
                def fa(t, kc=kc, seg=seg, pa=pa):
                    return t.matmul(
                        pa[0:MNH, :], wabat[:, kc * MNH:(kc + 1) * MNH],
                        xh[:, kc * T + seg * 512: kc * T + seg * 512 + 512],
                        start=(kc == 0), stop=(kc == NCH - 1))
                atok = op("t", fa, waits=wa if kc == 0 else [])
            for kc in range(NCH):
                def fbm(t, kc=kc, seg=seg, pbb=pbb):
                    return t.matmul(
                        pbb[0:MNH, :], wabbt[:, kc * MNH:(kc + 1) * MNH],
                        xh[:, kc * T + seg * 512: kc * T + seg * 512 + 512],
                        start=(kc == 0), stop=(kc == NCH - 1))
                btok = op("t", fbm, waits=[w(G.get(f"bev_{seg-2}"))]
                          if kc == 0 else [])

            def f1(s, pa=pa):
                return s.activation(gs1[:], pa[0:MNH, :], AF.Relu,
                                    bias=abtan[:], scale=-1.0)
            op("s", f1, waits=[w(atok), w(G.get(f"gstt_{seg-1}"))])

            def f2(s, pa=pa):
                return s.activation(gs2[:], pa[0:MNH, :], AF.Abs,
                                    bias=abta[:])
            G[f"gev_{seg}"] = op("s", f2)
            drain("s")
            op("s", lambda s: s.activation(gs2[:], gs2[:], AF.Exp, scale=-1.0))
            drain("s")
            etok = op("s", lambda s: s.activation(gs2[:], gs2[:], AF.Ln,
                                                  bias=1.0))

            def fg(v, seg=seg):
                return v.scalar_tensor_tensor(
                    gmg[0:MNH, seg * 512:(seg + 1) * 512], gs1[:], -1.0,
                    gs2[:], ALU.mult, ALU.subtract)
            G[f"gstt_{seg}"] = op("v", fg, waits=[w(etok)])

            def fbb(s, pbb=pbb, seg=seg):
                return s.activation(gmb[0:MNH, seg * 512:(seg + 1) * 512],
                                    pbb[0:MNH, :], AF.Sigmoid, bias=abtb[:])
            G[f"bev_{seg}"] = op("s", fbb, waits=[w(btok)])

        sc = op("v", lambda v: v.tensor_tensor_scan(
            gmg[0:MNH, :], gmg[0:MNH, :], gmg[0:MNH, :], gzero[:],
            ALU.add, ALU.bypass),
            waits=[w(G[f"bev_{NT-1}"]), w(G[f"gstt_{NT-1}"])])
        drain("v")
        op("v", lambda v: v.tensor_copy(gsm[0:MNH, 0:NCKT],
                                        gmg[0:MNH, 0:T:128]))
        sc2 = op("v", lambda v: v.tensor_copy(gsm[0:MNH, NCKT:NCKT + 1],
                                              gmg[0:MNH, T - 1:T]))
        op("y", lambda y: y.dma_start(out=gbounce.rearrange("c h -> h c"),
                                      in_=gsm[:],
                                      allow_slow_non_contiguous=True
                                      ).then_inc(gse.h, 16),
           waits=[w(sc2)], sem=gse, inc=16)
        gtok = (gse, gse.n)
        op("y", lambda y: y.dma_start(
            out=gstart_bc[:].rearrange("p (c h) -> p c h", h=MNH),
            in_=gbounce[0:NCKT, :].unsqueeze(0)
            .broadcast_to([128, NCKT, MNH])
        ).then_inc(g2se.h, 16), waits=[w(gtok)], sem=g2se, inc=16)
        op("y", lambda y: y.dma_start(
            out=gnext_bc[:].rearrange("p (c h) -> p c h", h=MNH),
            in_=gbounce[1:NCKT + 1, :].unsqueeze(0)
            .broadcast_to([128, NCKT, MNH])
        ).then_inc(g2se.h, 16), sem=g2se, inc=16)
        gbc_tok = (g2se, g2se.n)

        tr_last = None
        for c2 in range(NCKT):
            def ftr(t, c2=c2):
                return t.transpose(ps[4][:, c2 * 16:c2 * 16 + 8],
                                   gmg[0:MNH, c2 * 128:(c2 + 1) * 128],
                                   identtf[0:MNH, 0:MNH])
            op("t", ftr, waits=[w(sc2), w(G.get("qh_0")), w(G.get("qh_1"))]
               if c2 == 0 else [])

            def ftr2(t, c2=c2):
                return t.transpose(ps[4][:, c2 * 16 + 8:c2 * 16 + 16],
                                   gmb[0:MNH, c2 * 128:(c2 + 1) * 128],
                                   identtf[0:MNH, 0:MNH])
            tr_last = op("t", ftr2)
        gbtm_tok = op("v", lambda v: v.tensor_copy(gbtm[:, 0:NCKT * 16],
                                                   ps[4][:, 0:NCKT * 16]),
                      waits=[w(tr_last)])

        gbtmG = gbtm[:, 0:NCKT * 16].rearrange("p (c k) -> p c k", k=16)[:, :, 0:8]
        gbtmB = gbtm[:, 0:NCKT * 16].rearrange("p (c k) -> p c k", k=16)[:, :, 8:16]
        gsv = gstart_bc[:].rearrange("p (c h) -> p c h", h=MNH)
        gnv = gnext_bc[:].rearrange("p (c h) -> p c h", h=MNH)
        ghv = ghall[:].rearrange("p (c h) -> p c h", h=MNH)
        wcv = wcall[:].rearrange("p (c h) -> p c h", h=MNH)

        gh_tok = op("v", lambda v: v.tensor_tensor(ghv, gbtmG, gsv,
                                                   ALU.subtract),
                    waits=[w(gbtm_tok), w(gbc_tok)])
        drain("v")
        ac_tok = op("s", lambda s: s.activation(acall[:], ghall[:], AF.Exp),
                    waits=[w(gh_tok)])
        w1_tok = op("v", lambda v: v.tensor_tensor(wcv, gnv, gbtmG,
                                                   ALU.subtract))
        w2_tok = op("s", lambda s: s.activation(wcall[:], wcall[:], AF.Exp),
                    waits=[w(w1_tok)])
        w3_tok = op("v", lambda v: v.tensor_tensor(wcv, wcv, gbtmB, ALU.mult),
                    waits=[w(w2_tok)])
        d1_tok = op("v", lambda v: v.tensor_tensor(decbig[:], gnext_bc[:],
                                                   gstart_bc[:],
                                                   ALU.subtract))
        d2_tok = op("s", lambda s: s.activation(decbig[:], decbig[:], AF.Exp),
                    waits=[w(d1_tok)])
        at_tok = op("t", lambda t: t.transpose(ps[5][0:NCKT * MNH, 0:128],
                                               acall[:], identtf[:]),
                    waits=[w(ac_tok)])
        av_tok = op("v", lambda v: v.tensor_copy(atrp[0:NCKT * MNH, 0:128],
                                                 ps[5][0:NCKT * MNH, 0:128]),
                    waits=[w(at_tok)])
        op("y", lambda y: y.dma_start(out=abounce,
                                      in_=atrp[0:NCKT * MNH, 0:128]).then_inc(
            aese.h, 16), waits=[w(av_tok)], sem=aese, inc=16)
        arow_tok = (aese, aese.n)

        HGRP = MNH // 4

        def sl(buf, g):
            return buf[:, g * 512:(g + 1) * 512]

        def rg(i, g):
            return rowsg[:, i * MNH + 4 * g:i * MNH + 4 * g + 4]

        def emit_mv_g(c, g, src, m1lhs, state_bf, with_lambda, evac_q,
                      state_tok, src_tok=None):
            Sb, Qb, Tb = ps[g], ps[2 + g], ps[4 + g]
            tp_last = None
            for hh4 in range(4):
                h = 4 * g + hh4

                def ftp(t, h=h, hh4=hh4, Tb=Tb):
                    return t.transpose(Tb[:, hh4 * 128:hh4 * 128 + 128],
                                       src[:, h * 128:(h + 1) * 128],
                                       identtf[:])
                w_ = []
                if hh4 == 0:
                    w_ = [w(G.get(f"pfmev_{g}")), w(src_tok),
                          w(G.get(f"srcw_{g}"))]
                tp_last = op("t", ftp, waits=w_)
            G[f"tp_{g}"] = tp_last
            ev = op("s", lambda s: s.activation(sl(pfm, g), Tb[:], AF.Copy),
                    waits=[w(tp_last), w(G.get(f"pfmrd_{g}"))])
            G[f"pfmev_{g}"] = ev
            pa = op("v", lambda v: v.tensor_tensor(sl(pafm, g), sl(pfm, g),
                                                   sl(aftile, g), ALU.mult),
                    waits=[w(ev), w(G.get(f"afrd_{g}")), w(G.get("af_dma"))])
            G[f"pfmrd_{g}"] = pa
            m1l = None
            for hh4 in range(4):
                h = 4 * g + hh4

                def fm1(t, h=h, hh4=hh4, Sb=Sb):
                    return t.matmul(Sb[:, hh4 * 128:hh4 * 128 + 128],
                                    m1lhs[:, h * 128:(h + 1) * 128],
                                    pfm[:, h * 128:(h + 1) * 128],
                                    start=(hh4 == 0), stop=(hh4 == 3))
                w_ = [w(ev), w(G.get(f"E1_{g}"))] if hh4 == 0 else []
                m1l = op("t", fm1, waits=w_)
            e1 = op("v", lambda v: v.tensor_tensor(sl(stall, g), Sb[:],
                                                   sl(maskb, g), ALU.mult),
                    waits=[w(m1l), w(G.get(f"stallrd_{g}"))])
            G[f"E1_{g}"] = e1
            qstop = None
            for hh4 in range(4):
                h = 4 * g + hh4

                def fm2(t, h=h, hh4=hh4, Qb=Qb):
                    return t.matmul(Qb[:, hh4 * 128:hh4 * 128 + 128],
                                    stall[:, h * 128:(h + 1) * 128],
                                    ktm[:, h * 128:(h + 1) * 128],
                                    start=(hh4 == 0), stop=False)
                w_ = [w(e1)]
                if hh4 == 0:
                    w_ += [w(G.get(f"qrd_{g}"))]
                op("t", fm2, waits=w_)

                def fm3(t, h=h, hh4=hh4, Qb=Qb):
                    return t.matmul(Qb[:, hh4 * 128:hh4 * 128 + 128],
                                    pafm[:, h * 128:(h + 1) * 128],
                                    state_bf[:, h * 128:(h + 1) * 128],
                                    start=False,
                                    stop=(not with_lambda and hh4 == 3))
                qstop = op("t", fm3, waits=[w(pa)] + (
                    [w(state_tok)] if hh4 == 0 else []))
                if with_lambda:
                    def fm4(t, h=h, hh4=hh4, Qb=Qb):
                        return t.matmul(Qb[:, hh4 * 128:hh4 * 128 + 128],
                                        pfm[:, h * 128:(h + 1) * 128],
                                        lamd[:, h * 128:(h + 1) * 128],
                                        start=False, stop=(hh4 == 3))
                    qstop = op("t", fm4)
            G[f"qstop_{g}"] = qstop
            if evac_q:
                qe = op("s", lambda s: s.activation(sl(qst, g), Qb[:],
                                                    AF.Copy),
                        waits=[w(qstop), w(G.get(f"qstrd_{g}"))])
                G[f"qev_{g}"] = qe
                G[f"qrd_{g}"] = qe

        def emit_post_g(c, g):
            Qb = ps[2 + g]
            d6 = None
            for hh4 in range(4):
                h = 4 * g + hh4

                def fd6(v, h=h, hh4=hh4, Qb=Qb):
                    return v.scalar_tensor_tensor(
                        junk[:, h * 128:(h + 1) * 128],
                        pst[:, h * 128:(h + 1) * 128], 1.0,
                        Qb[:, hh4 * 128:hh4 * 128 + 128], ALU.mult, ALU.mult,
                        accum_out=rowsg[:, R_PQ * MNH + h:R_PQ * MNH + h + 1])
                d6 = op("v", fd6, waits=[w(G[f"qstop_{g}"])]
                        if hh4 == 0 else [])
            drain("v")
            op("v", lambda v: v.tensor_scalar(rg(R_S1, g), rg(R_D, g), 1e-10,
                                              None, ALU.is_ge))
            op("v", lambda v: v.tensor_tensor(rg(R_S2, g), rg(R_PQ, g),
                                              rg(R_PQ, g), ALU.mult))
            drain("v")
            op("v", lambda v: v.tensor_tensor(rg(R_ND, g), rg(R_ND, g),
                                              rg(R_S1, g), ALU.mult))
            op("v", lambda v: v.tensor_scalar(rg(R_S2, g), rg(R_S2, g), 1e-20,
                                              None, ALU.is_ge))
            drain("v")
            op("v", lambda v: v.tensor_tensor(rg(R_ND, g), rg(R_ND, g),
                                              rg(R_S2, g), ALU.mult))
            drain("v")
            op("v", lambda v: v.tensor_scalar(rg(R_S1, g), rg(R_ND, g), -1.0,
                                              1.0, ALU.mult, ALU.add))
            op("v", lambda v: v.tensor_tensor(rg(R_S3, g), rg(R_PQ, g),
                                              rg(R_ND, g), ALU.mult))
            drain("v")
            op("v", lambda v: v.tensor_tensor(rg(R_S3, g), rg(R_S3, g),
                                              rg(R_S1, g), ALU.add))
            drain("v")
            s3t = op("v", lambda v: v.tensor_copy(rg(R_S2, g),
                                                  rg(R_S3, g)))
            op("s", lambda s_: s_.activation(rg(R_S2, g), rg(R_S2, g), AF.Ln),
               waits=[w(s3t)])
            drain("s")
            rquick = op("s", lambda s_: s_.activation(rg(R_S2, g),
                                                      rg(R_S2, g), AF.Exp,
                                                      scale=-1.0))
            op("v", lambda v: v.tensor_tensor(rg(R_AL, g), rg(R_D, g),
                                              rg(R_S2, g), ALU.mult),
               waits=[w(rquick)])
            drain("v")
            op("v", lambda v: v.tensor_tensor(rg(R_AL, g), rg(R_AL, g),
                                              rg(R_ND, g), ALU.mult))
            drain("v")
            op("v", lambda v: v.tensor_scalar(rg(R_NA, g), rg(R_AL, g), -1.0,
                                              None, ALU.mult))
            drain("v")
            for hh4 in range(4):
                h = 4 * g + hh4

                def fd7(v, h=h):
                    return v.scalar_tensor_tensor(
                        xst[:, h * 128:(h + 1) * 128],
                        pst[:, h * 128:(h + 1) * 128],
                        rowsg[:, R_AL * MNH + h:R_AL * MNH + h + 1],
                        xst[:, h * 128:(h + 1) * 128], ALU.mult, ALU.add)
                op("v", fd7)
            d8 = None
            for hh4 in range(4):
                h = 4 * g + hh4

                def fd8(v, h=h, hh4=hh4, Qb=Qb):
                    return v.scalar_tensor_tensor(
                        rst[:, h * 128:(h + 1) * 128],
                        Qb[:, hh4 * 128:hh4 * 128 + 128],
                        rowsg[:, R_NA * MNH + h:R_NA * MNH + h + 1],
                        rst[:, h * 128:(h + 1) * 128], ALU.mult, ALU.add)
                d8 = op("v", fd8)
            G[f"qrd_{g}"] = d8
            drain("v")
            d9 = None
            for hh4 in range(4):
                h = 4 * g + hh4

                def fd9(s_, h=h):
                    return s_.activation(
                        stall[:, h * 128:(h + 1) * 128],
                        rst[:, h * 128:(h + 1) * 128], AF.Square,
                        accum_out=rowsg[:, R_DN * MNH + h:R_DN * MNH + h + 1])
                d9 = op("s", fd9, waits=[w(d8)] if hh4 == 0 else [])
            G[f"stallrd_{g}"] = d9
            op("v", lambda v: v.tensor_scalar(rg(R_S2, g), rg(R_ND, g), -1.0,
                                              1.0, ALU.mult, ALU.add))
            op("v", lambda v: v.tensor_tensor(rg(R_S3, g), rg(R_D, g),
                                              rg(R_ND, g), ALU.mult))
            drain("v")
            op("v", lambda v: v.tensor_tensor(rg(R_S3, g), rg(R_S3, g),
                                              rg(R_S2, g), ALU.add))
            drain("v")
            s3t = op("v", lambda v: v.tensor_copy(rg(R_S1, g),
                                                  rg(R_S3, g)))
            op("s", lambda s_: s_.activation(rg(R_S1, g), rg(R_S1, g), AF.Ln),
               waits=[w(s3t)])
            drain("s")
            rquick = op("s", lambda s_: s_.activation(rg(R_S1, g),
                                                      rg(R_S1, g), AF.Exp,
                                                      scale=-1.0))
            dbw = op("v", lambda v: v.tensor_tensor(rg(R_BE, g), rg(R_DN, g),
                                                    rg(R_S1, g), ALU.mult),
                     waits=[w(d9), w(rquick)])
            drain("v")
            op("v", lambda v: v.tensor_tensor(rg(R_BE, g), rg(R_BE, g),
                                              rg(R_ND, g), ALU.mult))
            drain("v")
            op("v", lambda v: v.tensor_tensor(rg(R_BE, g), rg(R_BE, g),
                                              rg(R_S2, g), ALU.add))
            drain("v")
            ndv = (rowsg[:, R_ND * MNH + 4 * g:R_ND * MNH + 4 * g + 4]
                   .unsqueeze(2).broadcast_to([128, 4, 128]))
            op("v", lambda v: v.tensor_tensor(
                sl(rnm, g).rearrange("p (h j) -> p h j", j=128),
                sl(rst, g).rearrange("p (h j) -> p h j", j=128), ndv,
                ALU.mult))
            drain("v")
            d10 = None
            for hh4 in range(4):
                h = 4 * g + hh4

                def fd10(v, h=h):
                    return v.scalar_tensor_tensor(
                        pst[:, h * 128:(h + 1) * 128],
                        pst[:, h * 128:(h + 1) * 128],
                        rowsg[:, R_BE * MNH + h:R_BE * MNH + h + 1],
                        rnm[:, h * 128:(h + 1) * 128], ALU.mult, ALU.add)
                d10 = op("v", fd10, waits=[w(G[f"tp_{g}"])]
                         if hh4 == 0 else [])
            drain("v")
            dcp = op("v", lambda v: v.tensor_copy(rg(R_D, g), rg(R_DN, g)))
            G[f"srcw_{g}"] = dcp

        for c in range(NCKT if NCK_RUN is None else NCK_RUN):
            prev_pe = (mse, mse.n)
            prev_ve = (vse, vse.n)
            ldw = [w(prev_pe), w(prev_ve)]
            if c == 0:
                ldw += [w(G.get("sp_0")), w(G.get("sp_1"))]
            op("y", lambda y, c=c: y.dma_start(
                out=kfm[:],
                in_=kspill[c // 4, :, :, (c % 4) * 128:(c % 4) * 128 + 128]
            ).then_inc(kse.h, 16), waits=ldw, sem=kse, inc=16)
            ktok = (kse, kse.n)
            op("y", lambda y, c=c: y.dma_start(
                out=vfm[:],
                in_=vspill[c // 4, :, :, (c % 4) * 128:(c % 4) * 128 + 128]
            ).then_inc(vlse.h, 16), sem=vlse, inc=16)
            vtok = (vlse, vlse.n)
            op("y", lambda y, c=c: y.dma_start(
                out=ebuf[:],
                in_=qspill[c // 4, :, :, (c % 4) * 128:(c % 4) * 128 + 128]
            ).then_inc(qlse.h, 16), sem=qlse, inc=16)
            qtok = (qlse, qlse.n)
            op("y", lambda y, c=c: y.dma_start(
                out=arow1[:],
                in_=abounce[c * MNH:(c + 1) * MNH, :]
                .rearrange("r t -> (r t)").unsqueeze(0)
            ).then_inc(ar2se.h, 16), waits=[w(arow_tok)] if c == 0 else [],
                sem=ar2se, inc=16)
            artok = (ar2se, ar2se.n)
            op("g", lambda g_, c=c: g_.dma_start(
                out=aftile[:].rearrange("p (h j) -> p h j", j=128),
                in_=abounce[c * MNH:(c + 1) * MNH, :].unsqueeze(0)
                .broadcast_to([128, MNH, 128])
            ).then_inc(afse.h, 16), waits=[w(arow_tok)] if c == 0 else
                [w(G.get("pfmrd_1"))], sem=afse, inc=16)
            G["af_dma"] = (afse, afse.n)

            for src_tok, srcb, dst in [(ktok, kfm, ktm), (vtok, vfm, vtm),
                                       (qtok, ebuf, bqt)]:
                ttok = None
                for h in range(MNH):
                    def ftr(t, srcb=srcb, h=h):
                        return t.transpose(pbt[:, h * 128:(h + 1) * 128],
                                           srcb[:, h * 128:(h + 1) * 128],
                                           identtb[:])
                    w_ = []
                    if h == 0:
                        w_ = [w(src_tok), w(G.get("pbt_ev"))]
                    ttok = op("t", ftr, waits=w_)

                def fev(s, dst=dst):
                    return s.activation(dst[:], pbt[:], AF.Copy)
                G["pbt_ev"] = op("s", fev, waits=[w(ttok)])
                if srcb is ebuf:
                    G["ebuf_rd"] = ttok
            drain("s")
            k2_tok = op("s", lambda s: s.activation(k2tm[:], ktm[:],
                                                    AF.Square))

            ghtok = op("v", lambda v, c=c: v.tensor_scalar(
                ghatb[:], gmg[0:MNH, c * 128:(c + 1) * 128],
                gsm[0:MNH, c:c + 1], None, ALU.subtract),
                waits=[w(prev_pe)])
            msk_evs = []
            for g_ in range(2):
                bnk = ps[g_]
                ptok = None
                for hh4 in range(4):
                    h = 4 * g_ + hh4

                    def fsel(t, h=h, bnk=bnk, hh4=hh4):
                        return t.matmul(
                            bnk[:, hh4 * 128:hh4 * 128 + 128],
                            selt[0:MNH, h * 128:(h + 1) * 128],
                            ghatb[:], start=(hh4 == 0), stop=False)
                    w_ = [w(ghtok)]
                    if hh4 == 0:
                        w_ += [w(G.get(f"E1_{g_}")), w(G.get("akk_ev"))]
                    op("t", fsel, waits=w_)

                    def fpen(t, hh4=hh4, bnk=bnk):
                        return t.matmul(bnk[:, hh4 * 128:hh4 * 128 + 128],
                                        penlt[:], penrt[:], start=False,
                                        stop=(hh4 == 3))
                    ptok = op("t", fpen)
                for hh4 in range(4):
                    h = 4 * g_ + hh4

                    def fd(v, h=h, bnk=bnk, hh4=hh4, c=c):
                        return v.tensor_scalar(
                            junk[:, h * 128:(h + 1) * 128],
                            bnk[:, hh4 * 128:hh4 * 128 + 128],
                            ghall[:, c * MNH + h:c * MNH + h + 1], None,
                            ALU.subtract)
                    dtok = op("v", fd, waits=[w(ptok), w(gbc_tok)])

                    def fex(s_, h=h):
                        return s_.activation(stall[:, h * 128:(h + 1) * 128],
                                             junk[:, h * 128:(h + 1) * 128],
                                             AF.Exp)
                    extok = op("s", fex,
                               waits=[w(dtok), w(G.get(f"stallrd_{g_}"))])

                    def fmk(v, h=h, c=c):
                        return v.tensor_scalar(
                            maskb[:, h * 128:(h + 1) * 128],
                            stall[:, h * 128:(h + 1) * 128],
                            gbtm[:, c * 16 + 8 + h:c * 16 + 9 + h], None,
                            ALU.mult)
                    msk_evs.append(op("v", fmk, waits=[w(extok)]))
                G[f"E1_{g_}"] = msk_evs[-1]
                G[f"stallrd_{g_}"] = msk_evs[-1]

            x0toks = []
            for g_ in range(2):
                Qb = ps[2 + g_]
                dgl = None
                for hh4 in range(4):
                    h = 4 * g_ + hh4

                    def fdg(t, h=h, hh4=hh4, Qb=Qb):
                        return t.matmul(Qb[:, hh4 * 128:hh4 * 128 + 128],
                                        maskb[:, h * 128:(h + 1) * 128],
                                        k2tm[:, h * 128:(h + 1) * 128],
                                        start=(hh4 == 0), stop=False)
                    w_ = [w(msk_evs[4 * g_ + hh4]), w(k2_tok)]
                    if hh4 == 0:
                        w_ += [w(G.get(f"qrd_{g_}")), w(lr8_tok), w(artok),
                               w(G.get("dst_ev"))]
                    op("t", fdg, waits=w_)

                    def flm(t, h=h, hh4=hh4, Qb=Qb):
                        return t.matmul(Qb[:, hh4 * 128:hh4 * 128 + 128],
                                        ones_row_f[:],
                                        lr8[0:1, h * 128:(h + 1) * 128],
                                        start=False, stop=False)
                    op("t", flm)

                    def fam(t, h=h, hh4=hh4, Qb=Qb):
                        return t.matmul(
                            Qb[:, hh4 * 128:hh4 * 128 + 128],
                            arow1[0:1, h * 128:(h + 1) * 128],
                            dstate[0:1, h * 128:(h + 1) * 128],
                            start=False, stop=(hh4 == 3))
                    dgl = op("t", fam)
                op("s", lambda s_, g_=g_, Qb=Qb: s_.activation(
                    sl(rnm, g_), Qb[:], AF.Ln), waits=[w(dgl)])
                drain("s")
                rc = op("s", lambda s_, g_=g_: s_.activation(
                    sl(rnm, g_), sl(rnm, g_), AF.Exp, scale=-1.0))
                G[f"qrd_{g_}"] = rc
                x0t = op("v", lambda v, g_=g_: v.tensor_tensor(
                    sl(xst, g_), sl(bqt, g_), sl(rnm, g_), ALU.mult),
                    waits=[w(G.get("pbt_ev")), w(rc)])
                x0toks.append(x0t)
                drain("v")
            state_tok = G.get("akkb_ev")
            for g_ in range(2):
                emit_mv_g(c, g_, xst, kfm, akkb, True, True, state_tok,
                          src_tok=x0toks[g_])
            r0l = []
            for g_ in range(2):
                r0t = op("v", lambda v, g_=g_: v.tensor_tensor(
                    sl(rst, g_), sl(bqt, g_), sl(qst, g_), ALU.subtract),
                    waits=[w(G[f"qev_{g_}"])])
                drain("v")
                p0t = op("v", lambda v, g_=g_: v.tensor_copy(sl(pst, g_),
                                                             sl(rst, g_)))
                G[f"qstrd_{g_}"] = r0t
                G[f"srcw_{g_}"] = p0t
                r0l.append(p0t)
                d0l = None
                for hh4 in range(4):
                    h = 4 * g_ + hh4

                    def fd0(s_, h=h):
                        return s_.activation(
                            stall[:, h * 128:(h + 1) * 128],
                            rst[:, h * 128:(h + 1) * 128], AF.Square,
                            accum_out=rowsg[:, R_D * MNH + h:
                                            R_D * MNH + h + 1])
                    d0l = op("s", fd0, waits=[w(p0t)] if hh4 == 0 else [])
                G[f"stallrd_{g_}"] = d0l
                G[f"d0_{g_}"] = d0l
            nd0 = op("v", lambda v: v.memset(row(R_ND), 1.0),
                     waits=[w(G["d0_0"]), w(G["d0_1"])])
            drain("v")

            for it in range(NITER):
                for g_ in range(2):
                    emit_mv_g(c, g_, pst, kfm, akkb, True, False, None)
                for g_ in range(2):
                    emit_post_g(c, g_)

            state_tok = G.get("akvb_ev")
            for g_ in range(2):
                emit_mv_g(c, g_, xst, vfm, akvb, False, True, state_tok)
            sst = None
            for h in range(MNH):
                g_ = h // 4

                def fss(s_, h=h):
                    return s_.activation(
                        stall[:, h * 128:(h + 1) * 128],
                        qst[:, h * 128:(h + 1) * 128], AF.Square,
                        accum_out=rowsg[:, R_S1 * MNH + h:R_S1 * MNH + h + 1])
                sst = op("s", fss, waits=[w(G[f"qev_{g_}"])]
                         if h % 4 == 0 else [])
            G["stallrd_0"] = sst
            G["stallrd_1"] = sst
            ln1 = op("s", lambda s: s.activation(row(R_S2), row(R_S1), AF.Ln,
                                                 bias=1e-6,
                                                 scale=1.0 / 128.0),
                     waits=[w(sst)])
            drain("s")
            ln2 = op("s", lambda s: s.activation(row(R_S2), row(R_S2), AF.Exp,
                                                 scale=-0.5))
            s2v = row(R_S2).unsqueeze(2).broadcast_to([128, MNH, 128])
            o1 = op("v", lambda v: v.tensor_tensor(v3(qst[:]), v3(qst[:]),
                                                   s2v, ALU.mult),
                    waits=[w(ln2)])
            drain("v")
            o2 = op("v", lambda v: v.tensor_tensor(obuf[:], qst[:], wnb[:],
                                                   ALU.mult),
                    waits=[w(G.get("obuf_rd"))])
            G["qstrd_0"] = o2
            G["qstrd_1"] = o2
            ottok = None
            for h in range(MNH):
                def fot(t, h=h):
                    return t.transpose(pbt[:, h * 128:(h + 1) * 128],
                                       obuf[:, h * 128:(h + 1) * 128],
                                       identtb[:])
                ottok = op("t", fot, waits=[w(o2), w(G.get("pbt_ev"))]
                           if h == 0 else [])
            G["obuf_rd"] = ottok
            oev = op("s", lambda s: s.activation(obuf2[:], pbt[:], AF.Copy),
                     waits=[w(ottok), w(G.get("oimg_dma"))])
            G["pbt_ev"] = oev
            op("y", lambda y, c=c: y.dma_start(
                out=oimg.rearrange("p (h t) -> p h t", t=T)
                [:, :, c * 128:(c + 1) * 128],
                in_=obuf2[:].rearrange("p (h j) -> p h j", j=128)
            ).then_inc(ose.h, 16), waits=[w(oev)], sem=ose, inc=16)
            G["oimg_dma"] = (ose, ose.n)

            kw = op("v", lambda v, c=c: v.tensor_tensor(
                v3(kwall[:]), v3(ktm[:]), bcv(wcall, c), ALU.mult),
                waits=[w(w3_tok)])
            drain("v")
            eb = op("v", lambda v: v.tensor_tensor(ebuf[:], kwall[:], ktm[:],
                                                   ALU.mult),
                    waits=[w(G.get("ebuf_rd"))])
            kkg = [None, None]
            for h in range(MNH):
                bk = ps[h // 4]

                def fkk(t, h=h, bk=bk):
                    return t.matmul(bk[:, (h % 4) * 128:(h % 4) * 128 + 128],
                                    kwall[:, h * 128:(h + 1) * 128],
                                    ktm[:, h * 128:(h + 1) * 128],
                                    start=(h % 4 == 0), stop=(h % 4 == 3))
                w_ = [w(kw)]
                if h == 0:
                    w_ += [w(G["E1_0"]), w(G["E1_1"])]
                kkg[h // 4] = op("t", fkk, waits=w_)
            kvg = [None, None]
            for h in range(MNH):
                bk = ps[2 + h // 4]

                def fkv(t, h=h, bk=bk):
                    return t.matmul(bk[:, (h % 4) * 128:(h % 4) * 128 + 128],
                                    vtm[:, h * 128:(h + 1) * 128],
                                    kwall[:, h * 128:(h + 1) * 128],
                                    start=(h % 4 == 0), stop=(h % 4 == 3))
                w_ = [w(G["qstrd_0"]), w(G["qstrd_1"])] if h == 0 else []
                kvg[h // 4] = op("t", fkv, waits=w_)
            ds0 = op("t", lambda t: t.matmul(ps[4][0:1, :], ones_col_b[:],
                                             ebuf[:, 0:512], start=True,
                                             stop=True),
                     waits=[w(eb), w(G.get("pfmev_0"))])
            ds1 = op("t", lambda t: t.matmul(ps[5][0:1, :], ones_col_b[:],
                                             ebuf[:, 512:1024], start=True,
                                             stop=True),
                     waits=[w(G.get("pfmev_1"))])
            ek = None
            for h in range(MNH):
                bk = ps[h // 4]

                def fek(v, h=h, bk=bk, c=c):
                    return v.scalar_tensor_tensor(
                        akk[:, h * 128:(h + 1) * 128],
                        akk[:, h * 128:(h + 1) * 128],
                        decbig[:, c * MNH + h:c * MNH + h + 1],
                        bk[:, (h % 4) * 128:(h % 4) * 128 + 128],
                        ALU.mult, ALU.add)
                ek = op("v", fek, waits=[w(kkg[h // 4]), w(d2_tok)]
                        if h % 4 == 0 else [])
            G["akk_ev"] = ek
            ev_ = None
            for h in range(MNH):
                bk = ps[2 + h // 4]

                def fev_(v, h=h, bk=bk, c=c):
                    return v.scalar_tensor_tensor(
                        akv[:, h * 128:(h + 1) * 128],
                        akv[:, h * 128:(h + 1) * 128],
                        decbig[:, c * MNH + h:c * MNH + h + 1],
                        bk[:, (h % 4) * 128:(h % 4) * 128 + 128],
                        ALU.mult, ALU.add)
                ev_ = op("v", fev_, waits=[w(kvg[h // 4])]
                         if h % 4 == 0 else [])
            G["akv_ev"] = ev_
            G["qrd_0"] = ev_
            G["qrd_1"] = ev_
            ed = None
            for h in range(MNH):
                bk = ps[4 + h // 4]

                def fed(v, h=h, bk=bk, c=c):
                    return v.scalar_tensor_tensor(
                        dstate[0:1, h * 128:(h + 1) * 128],
                        dstate[0:1, h * 128:(h + 1) * 128],
                        decbig[0:1, c * MNH + h:c * MNH + h + 1],
                        bk[0:1, (h % 4) * 128:(h % 4) * 128 + 128],
                        ALU.mult, ALU.add)
                ed = op("v", fed, waits=[w(ds0 if h < 4 else ds1)]
                        if h % 4 == 0 else [])
            G["dst_ev"] = ed
            drain("v")
            akkbev = op("s", lambda s: s.activation(akkb[:], akk[:], AF.Copy),
                        waits=[w(G["akk_ev"]), w(ed)])
            G["akkb_ev"] = akkbev
            akvbev = op("s", lambda s: s.activation(akvb[:], akv[:], AF.Copy),
                        waits=[w(G["akv_ev"])])
            G["akvb_ev"] = akvbev
            G["pfmev_0"] = akvbev
            G["pfmev_1"] = akvbev

        op("y", lambda y: y.wait_ge(ose.h, ose.n))

        def replay(key, eng):
            for fn, waits, sem_, incby in seq[key]:
                for tok in waits:
                    s_, v_ = tok
                    eng.wait_ge(s_.h, v_)
                inst = fn(eng)
                if sem_ is not None:
                    inst.then_inc(sem_.h, incby)

        with nc.Block() as b:
            @b.tensor
            def _(t):
                replay("t", t)

            @b.vector
            def _(v):
                replay("v", v)

            @b.scalar
            def _(s):
                replay("s", s)

            @b.sync
            def _(sy):
                replay("y", sy)

            @b.gpsimd
            def _(g):
                replay("g", g)

    return nc


def prep_l1attn_host(inp, hh):
    PIh = PI
    Wq = np.asarray(inp["L1_attn_Wq"])[PIh][:, hh * 1024:(hh + 1) * 1024]
    Wk = np.asarray(inp["L1_attn_Wk"])[PIh][:, hh * 1024:(hh + 1) * 1024]
    Wv = np.asarray(inp["L1_attn_Wv"])[PIh][:, hh * 1024:(hh + 1) * 1024]
    Wa = np.asarray(inp["L1_attn_Wa"])[PIh][:, hh * 8:hh * 8 + 8]
    Wb = np.asarray(inp["L1_attn_Wb"])[PIh][:, hh * 8:hh * 8 + 8]
    ba = np.asarray(inp["L1_attn_ba"])[hh * 8:hh * 8 + 8]
    bb = np.asarray(inp["L1_attn_bb"])[hh * 8:hh * 8 + 8]
    lam = np.asarray(inp["L1_attn_lambda_params"]).reshape(16, 128)
    onww = np.tile(np.asarray(inp["L1_attn_o_norm_weight"]).reshape(1, 128),
                   (1, MNH)).reshape(-1)
    sel = np.zeros((MNH, MNH * 128), np.float32)
    for h in range(MNH):
        sel[h, h * 128:(h + 1) * 128] = 1.0
    penl_np = np.triu(np.ones((128, 128), np.float32), 1).astype(BF)
    penr_np = (-1e9 * np.eye(128, dtype=np.float32)).astype(BF)
    return {
        "wq": img_kn(np.ascontiguousarray(Wq)),
        "wk": img_kn(np.ascontiguousarray(Wk)),
        "wv": img_kn(np.ascontiguousarray(Wv)),
        "waba": np.ascontiguousarray(
            Wa.reshape(16, 128, 8).transpose(1, 0, 2).reshape(128, 128)),
        "wabb": np.ascontiguousarray(
            Wb.reshape(16, 128, 8).transpose(1, 0, 2).reshape(128, 128)),
        "abain": np.ascontiguousarray(ba.reshape(MNH, 1)),
        "abbin": np.ascontiguousarray(bb.reshape(MNH, 1)),
        "lamin": np.ascontiguousarray(lam[hh * 8:hh * 8 + 8].T),
        "onw": np.ascontiguousarray(onww.reshape(1, MNH * 128)),
        "sel8": sel,
        "penl": penl_np,
        "penr": penr_np,
        "identb": np.eye(128, dtype=np.float32).astype(BF),
        "identf": np.eye(128, dtype=np.float32),
    }


def _np_silu(x):
    return x / (1.0 + np.exp(-x))


def l1attn_ref_np(xt, inp, n_iter=30):
    xt = np.asarray(xt, np.float64)
    T_ = xt.shape[0]
    Wq = np.asarray(inp["L1_attn_Wq"], np.float64)
    Wk = np.asarray(inp["L1_attn_Wk"], np.float64)
    Wv = np.asarray(inp["L1_attn_Wv"], np.float64)
    Wa = np.asarray(inp["L1_attn_Wa"], np.float64)
    Wb = np.asarray(inp["L1_attn_Wb"], np.float64)
    ba = np.asarray(inp["L1_attn_ba"], np.float64)
    bb = np.asarray(inp["L1_attn_bb"], np.float64)
    lamp = np.asarray(inp["L1_attn_lambda_params"], np.float64)
    onww = np.asarray(inp["L1_attn_o_norm_weight"], np.float64)

    q = _np_silu(xt @ Wq).reshape(T_, 16, 128)
    kk = _np_silu(xt @ Wk).reshape(T_, 16, 128)
    vv = _np_silu(xt @ Wv).reshape(T_, 16, 128)

    def l2n(u):
        n = np.sqrt((u * u).sum(-1, keepdims=True))
        return u / np.maximum(n, 1e-12)
    q = l2n(q)
    kk = l2n(kk)
    beta = 1.0 / (1.0 + np.exp(-(xt @ Wb + bb)))
    ga = xt @ Wa + ba
    g = -np.log1p(np.exp(-np.abs(ga))) + np.minimum(ga, 0.0)
    lamb = np.log1p(np.exp(-np.abs(lamp))) + np.maximum(lamp, 0.0) + 0.25
    lamb = lamb.reshape(16, 128)

    def cg(A, b):
        def mv(u):
            return np.einsum('hij,hj->hi', A, u) + lamb * u
        diag = np.einsum('hii->hi', A)
        x = b / (diag + lamb + 1e-8)
        r = b - mv(x)
        p = r.copy()
        d = (r * r).sum(-1)
        done = np.zeros(16, bool)
        for _ in range(n_iter):
            done = done | (d < 1e-10)
            qv = mv(p)
            pq = (p * qv).sum(-1)
            done = done | (np.abs(pq) < 1e-10)
            alpha = np.where(done, 0.0, d / np.where(done, 1.0, pq))[:, None]
            xn = x + alpha * p
            rn = r - alpha * qv
            dn = (rn * rn).sum(-1)
            bet = np.where(done, 0.0, dn / np.where(done, 1.0, d))[:, None]
            m = done[:, None]
            p = np.where(m, p, rn + bet * p)
            x = np.where(m, x, xn)
            r = np.where(m, r, rn)
            d = np.where(done[:], d, dn)
        return x

    Hkk = np.zeros((16, 128, 128))
    Hkv = np.zeros((16, 128, 128))
    o = np.zeros((T_, 16, 128))
    for t in range(T_):
        dec = np.exp(g[t])[:, None, None]
        kb = kk[t] * beta[t][:, None]
        Hkk = Hkk * dec + kb[:, :, None] * kk[t][:, None, :]
        Hkv = Hkv * dec + kb[:, :, None] * vv[t][:, None, :]
        qs = cg(Hkk, q[t])
        o[t] = np.einsum('hij,hj->hi', Hkv, qs)
    rms = np.sqrt((o * o).mean(-1, keepdims=True) + 1e-6)
    o = o / rms * onww[None, None, :]
    return o.reshape(T_, 2048)


def np_layernorm(x, eps=1e-5):
    x = np.asarray(x, np.float64)
    m = x.mean(axis=-1, keepdims=True)
    v = x.var(axis=-1, keepdims=True)
    return (x - m) / np.sqrt(v + eps)



def build_wo(T=1024):
    nc = bass.Bass("TRN2", target_bir_lowering=False)
    oin = nc.dram_tensor("oin", [128, NCH * T], BF16, kind="ExternalInput").ap()
    xres = nc.dram_tensor("xres", [128, NCH * T], F32, kind="ExternalInput").ap()
    wo = nc.dram_tensor("wo", [NCH, 128, NCH * 128], F32,
                        kind="ExternalInput").ap()
    xout = nc.dram_tensor("xout", [128, NCH * T], F32,
                          kind="ExternalOutput").ap()
    NTH = T // 512

    with ExitStack() as ctx:
        k = K(nc, ctx)
        ob = k.sbuf("ob", [128, NCH * T], BF16)
        wobuf = k.sbuf("wobuf", [128, NCH * NCH * 128], BF16)
        xrt = [k.sbuf(f"xrt{i}", [128, 512], F32) for i in range(2)]
        ot = [k.sbuf(f"ot{i}", [128, 512], F32) for i in range(2)]
        pb = [k.psum(f"pb{i}") for i in range(2)]
        dse = k.sem()
        wge = k.sem()
        xse4 = [k.sem(), k.sem()]
        ose4 = [k.sem(), k.sem()]
        mse = k.sem()
        vse = k.sem()

        with nc.Block() as b:
            @b.sync
            def _(sync):
                sync.dma_start(out=ob[:], in_=oin).then_inc(dse.h, 16)
                dse.n += 16

            @b.gpsimd
            def _(g):
                for i in range(NCH):
                    g.dma_start(out=wobuf[:, i * NCH * 128:(i + 1) * NCH * 128],
                                in_=wo[i]).then_inc(wge.h, 16)
                    wge.n += 16

        with nc.Block() as b:
            nseg = NCH * NTH

            @b.sync
            def _(sync):
                for i in range(nseg):
                    dc, th = i // NTH, i % NTH
                    bi = i % 2
                    if i >= 2:
                        sync.wait_ge(vse.h, i - 2 + 1)
                        sync.wait_ge(xse4[bi].h, 16 * (i // 2))
                    sync.dma_start(
                        out=xrt[bi][:],
                        in_=xres[:, dc * T + th * 512: dc * T + th * 512 + 512]
                    ).then_inc(xse4[bi].h, 16)
                    xse4[bi].n += 16
                    if i >= 1:
                        bj = (i - 1) % 2
                        sync.wait_ge(vse.h, i)
                        if i >= 3:
                            sync.wait_ge(ose4[bj].h, 16 * ((i - 1) // 2))
                        dcp, thp = (i - 1) // NTH, (i - 1) % NTH
                        sync.dma_start(
                            out=xout[:, dcp * T + thp * 512:
                                     dcp * T + thp * 512 + 512],
                            in_=ot[bj][:]).then_inc(ose4[bj].h, 16)
                        ose4[bj].n += 16
                bj = (nseg - 1) % 2
                sync.wait_ge(vse.h, nseg)
                sync.dma_start(
                    out=xout[:, (NCH - 1) * T + (NTH - 1) * 512:
                             (NCH - 1) * T + (NTH - 1) * 512 + 512],
                    in_=ot[bj][:]).then_inc(ose4[bj].h, 16)
                ose4[bj].n += 16
                for x in ose4:
                    sync.wait_ge(x.h, x.n)

            @b.tensor
            def _(t):
                t.wait_ge(dse.h, 16)
                t.wait_ge(wge.h, 16 * NCH)
                for i in range(nseg):
                    dc, th = i // NTH, i % NTH
                    if i >= 2:
                        t.wait_ge(vse.h, i - 2 + 1)
                    for kc in range(NCH):
                        inst = t.matmul(
                            pb[i % 2][:],
                            wobuf[:, dc * NCH * 128 + kc * 128:
                                  dc * NCH * 128 + kc * 128 + 128],
                            ob[:, kc * T + th * 512: kc * T + th * 512 + 512],
                            start=(kc == 0), stop=(kc == NCH - 1))
                    mse.inc(inst)

            @b.vector
            def _(v):
                for i in range(nseg):
                    bi = i % 2
                    v.wait_ge(mse.h, i + 1)
                    v.wait_ge(xse4[bi].h, 16 * (i // 2 + 1))
                    if i >= 2:
                        v.wait_ge(ose4[bi].h, 16 * (i // 2))
                    inst = v.tensor_tensor(ot[bi][:], xrt[bi][:], pb[i % 2][:],
                                           ALU.add)
                    vse.inc(inst)
    return nc
def prep_l0attn_host(inp):
    Win = np.asarray(inp["L0_attn_W_in"], np.float32)
    Wout = np.asarray(inp["L0_attn_W_out"], np.float32)
    WinT = np.ascontiguousarray(Win.T[PI])
    WinT = np.concatenate([WinT, np.zeros((D, 66 * 128 - 8384), np.float32)], axis=1)
    WoutT = np.ascontiguousarray(Wout.T[:, PI])
    cw = np.asarray(inp["L0_attn_conv_w"], np.float32)
    cb = np.asarray(inp["L0_attn_conv_b"], np.float32)
    cwt = np.zeros((128, 33 * 4), np.float32)
    cbt = np.zeros((128, 33), np.float32)
    for xc in range(33):
        for j in range(4):
            cwt[:, xc * 4 + j] = cw[128 * xc:128 * xc + 128, j]
        cbt[:, xc] = cb[128 * xc:128 * xc + 128]
    Dv = np.asarray(inp["L0_attn_D"], np.float32)
    drep = np.zeros((128, 32), np.float32)
    for j in range(32):
        drep[0:64, j] = Dv[2 * j]
        drep[64:128, j] = Dv[2 * j + 1]
    sel = np.zeros((64, 32 * 128), np.float32)
    for r in range(64):
        sel[r, (r % 32) * 128:(r % 32) * 128 + 128] = 1.0
    penl = np.zeros((128, 128), np.float32)
    for kk in range(128):
        penl[kk, kk + 1:] = 1.0
    penr = np.zeros((128, 4 * 128), np.float32)
    for q in range(4):
        penr[:, q * 128:(q + 1) * 128] = -1e9 * np.eye(128, dtype=np.float32)
    return {
        "win": img_kn(WinT),
        "wout": img_kn(WoutT),
        "cwt": cwt, "cbt": cbt,
        "dtb": np.asarray(inp["L0_attn_dt_bias"], np.float32).reshape(64, 1),
        "alog": np.asarray(inp["L0_attn_A_log"], np.float32).reshape(64, 1),
        "drep": drep,
        "nwt": col_tile(np.asarray(inp["L0_attn_norm_w"], np.float32)),
        "sel32": sel,
        "penl": penl.astype(BF), "penr": penr.astype(BF),
        "identb": np.eye(128, dtype=np.float32).astype(BF),
        "identf": np.eye(128, dtype=np.float32),
    }


def l0attn_ref_np(x, inp):
    x = np.asarray(x, np.float64)
    Tn = x.shape[0]
    m = x.mean(-1, keepdims=True)
    v = x.var(-1, keepdims=True)
    xx = (x - m) / np.sqrt(v + 1e-5)
    Win = np.asarray(inp["L0_attn_W_in"], np.float64)
    zxbcdt = xx @ Win.T
    z = zxbcdt[:, :4096]
    xBC = zxbcdt[:, 4096:8320]
    dtraw = zxbcdt[:, 8320:]
    cw = np.asarray(inp["L0_attn_conv_w"], np.float64)
    cb = np.asarray(inp["L0_attn_conv_b"], np.float64)
    xp = np.concatenate([np.zeros((3, 4224)), xBC], 0)
    conv = cb + sum(xp[kk:kk + Tn, :] * cw[:, kk] for kk in range(4))
    xBC2 = conv / (1 + np.exp(-conv))
    xh = xBC2[:, :4096].reshape(Tn, 64, 64)
    Bm = xBC2[:, 4096:4160]
    Cm = xBC2[:, 4160:4224]
    dtbias = np.asarray(inp["L0_attn_dt_bias"], np.float64)
    dt = np.logaddexp(0, dtraw + dtbias)
    A = -np.exp(np.asarray(inp["L0_attn_A_log"], np.float64))
    dA = np.exp(dt * A)
    h = np.zeros((64, 64, 64))
    ys = np.zeros((Tn, 64, 64))
    for t in range(Tn):
        h = h * dA[t][:, None, None] + dt[t][:, None, None] * \
            (xh[t][:, :, None] * Bm[t][None, None, :])
        ys[t] = h @ Cm[t]
    Dv = np.asarray(inp["L0_attn_D"], np.float64)
    y = ys + Dv[None, :, None] * xh
    y = y.reshape(Tn, 4096)
    y = y * (z / (1 + np.exp(-z)))
    y = y / np.sqrt((y ** 2).mean(-1, keepdims=True) + 1e-5) * \
        np.asarray(inp["L0_attn_norm_w"], np.float64)
    return x + y @ np.asarray(inp["L0_attn_W_out"], np.float64).T


def _img_half(img, th, T=2048):
    return np.ascontiguousarray(
        img.reshape(128, NCH, T)[:, :, th * 1024:(th + 1) * 1024]
        .reshape(128, NCH * 1024))


def _img_merge(h0, h1, T=2048):
    out = np.empty((128, NCH * T), h0.dtype)
    v = out.reshape(128, NCH, T)
    v[:, :, 0:1024] = h0.reshape(128, NCH, 1024)
    v[:, :, 1024:2048] = h1.reshape(128, NCH, 1024)
    return out


def kernel(**inputs):
    x0 = np.asarray(inputs["L0_attn_x"], np.float32)
    B = x0.shape[0]

    hp_dec = prep_dec1_host(inputs)
    res = run_bass_kernel_spmd(build_dec1(), dec1_in_maps(hp_dec),
                               core_ids=list(range(8)))
    dg, du, dd = assemble_dec1(res.results)

    hp0 = prep_l0attn_host(inputs)
    in_maps = []
    for b in range(B):
        m = {"xin": img_stream(np.ascontiguousarray(x0[b].T[PI]))}
        m.update(hp0)
        in_maps.append(m)
    res = run_bass_kernel_spmd(build_l0attn(2048, 512), in_maps,
                               core_ids=list(range(B)))
    x1img = [np.asarray(res.results[b]["xout"]).reshape(128, NCH * 2048)
             for b in range(B)]

    hpf = prep_l0ffn_weights(inputs)
    in_maps = []
    for c in range(2 * B):
        b, th = c // 2, c % 2
        m = {"xin": _img_half(x1img[b], th)}
        m.update(hpf)
        in_maps.append(m)
    res = run_bass_kernel_spmd(build_l0ffn(1024), in_maps,
                               core_ids=list(range(2 * B)))
    x2img = [_img_merge(np.asarray(res.results[2 * b]["xout"]).reshape(128, -1),
                        np.asarray(res.results[2 * b + 1]["xout"]).reshape(128, -1))
             for b in range(B)]

    hpm = [prep_l1attn_host(inputs, hh) for hh in range(2)]
    in_maps = []
    for c in range(2 * B):
        b, hh = c // 2, c % 2
        m = {"xin": x2img[b]}
        m.update(hpm[hh])
        in_maps.append(m)
    res = run_bass_kernel_spmd(build_l1attn(2048, NITER=16), in_maps,
                               core_ids=list(range(2 * B)))
    oimgs = [np.asarray(res.results[c]["oimg"]).reshape(128, 8, 2048)
             for c in range(2 * B)]

    Wo = np.asarray(inputs["L1_attn_Wo"], np.float32)
    wo_img = img_kn(np.ascontiguousarray(Wo[:, PI]))
    in_maps = []
    for c in range(2 * B):
        b, th = c // 2, c % 2
        ofull = np.concatenate([oimgs[2 * b], oimgs[2 * b + 1]], axis=1)
        oin = np.ascontiguousarray(
            ofull[:, :, th * 1024:(th + 1) * 1024].reshape(128, NCH * 1024))
        in_maps.append({"oin": oin, "xres": _img_half(x2img[b], th),
                        "wo": wo_img})
    res = run_bass_kernel_spmd(build_wo(1024), in_maps,
                               core_ids=list(range(2 * B)))
    x3h = [np.asarray(res.results[c]["xout"]).reshape(128, -1)
           for c in range(2 * B)]

    in_maps = [{"xin": x3h[c], "dg": dg, "du": du, "dd": dd}
               for c in range(2 * B)]
    res = run_bass_kernel_spmd(build_l1ffn(1024), in_maps,
                               core_ids=list(range(2 * B)))

    out = np.empty((B, 2048, 2048), np.float32)
    for c in range(2 * B):
        b, th = c // 2, c % 2
        ximg = np.asarray(res.results[c]["xout"]).reshape(128, -1)
        out[b, th * 1024:(th + 1) * 1024, :] = unimg_stream(ximg, D)[PI_INV].T
    return out

